# version 06
# baseline (speedup 1.0000x reference)
import contextlib
import math
import numpy as np
import concourse.bass as bass
import concourse.mybir as mybir
from concourse.bass_utils import run_bass_kernel_spmd

F32 = mybir.dt.float32
BF16 = mybir.dt.bfloat16
I32 = mybir.dt.int32
AF = mybir.ActivationFunctionType
ALU = mybir.AluOpType
AX = mybir.AxisListType

T = 4096
DM = 1024
NB = 4
NTILE = 32
NCH = 8
FMW = 4097
L0H, L0D = 8, 128
L1H, L1D = 16, 64
PD = 4
NEG = -1.0e30
LN_EPS = 1e-5
ROPE_BASE = 10000.0
TWO_PI = 2.0 * math.pi
PIH = 6.28125
PIL = TWO_PI - PIH

ENGINES = ("sync", "scalar", "vector", "tensor", "gpsimd")


def own_tiles(z):
    return list(range(0, 8)) + list(range(24, 32)) if z == 0 else list(range(8, 24))


def q_halves(z):
    res = []
    for hf in range(4):
        tiles = list(range(hf * 8, hf * 8 + 8))
        is_own = (hf in (0, 3)) if z == 0 else (hf in (1, 2))
        res.append((tiles, is_own))
    return res


class Sem:
    __slots__ = ("h", "val")

    def __init__(self, h):
        self.h = h
        self.val = 0


class Ctx:
    def __init__(self, nc, stack):
        self.nc = nc
        self.stack = stack
        self.scopes = [stack]
        self.q = {e: [] for e in ENGINES}
        self.sems = {}
        for name in ("LD", "GD", "SD", "PE", "DV", "AC", "PO", "WW",
                     "L1", "L2", "L3", "G1", "S1"):
            self.sems[name] = Sem(stack.enter_context(nc.semaphore("s" + name)))
        self.lanes = {"sync": ("LD", "L1", "L2", "L3"),
                      "gpsimd": ("GD", "G1"),
                      "scalar": ("SD", "S1")}
        self.ps_w = {}
        self.ps_r = {}
        self.warm = None
        self.sfx = ""
        self.stage = "init"

    def sbuf(self, name, shape, dt):
        return self.scopes[-1].enter_context(
            self.nc.sbuf_tensor(name + self.sfx, shape, dt))

    def psum(self, name, shape, dt):
        return self.scopes[-1].enter_context(
            self.nc.psum_tensor(name + self.sfx, shape, dt))

    @contextlib.contextmanager
    def scope(self):
        st = contextlib.ExitStack()
        self.scopes.append(st)
        try:
            yield
        finally:
            st.close()
            self.scopes.pop()

    def emit(self, eng, fname, *args, wait=None, inc=None, **kw):
        tok = None
        if inc is not None:
            inc[0].val += inc[1]
            tok = inc[0].val
        sem_h = inc[0].h if inc is not None else None
        inc_n = inc[1] if inc is not None else 0
        waits = tuple((s.h, v) for (s, v) in wait) if wait else ()

        drain = eng in ("vector", "scalar", "gpsimd") and fname != "dma_start" \
            and sem_h is not None

        def f(e, fname=fname, args=args, kw=kw, waits=waits, sem_h=sem_h,
              inc_n=inc_n, drain=drain):
            for h, v in waits:
                e.wait_ge(h, v)
            ins = getattr(e, fname)(*args, **kw)
            if sem_h is not None:
                if drain:
                    e.drain().then_inc(sem_h, inc_n)
                else:
                    ins.then_inc(sem_h, inc_n)
        self.q[eng].append((self.stage, f))
        return tok

    def raw(self, eng, f):
        self.q[eng].append((self.stage, f))

    def barrier(self):
        toks = [(s, s.val) for s in self.sems.values() if s.val > 0]
        for e in ENGINES:
            def f(eng, toks=tuple((s.h, v) for s, v in toks)):
                for h, v in toks:
                    eng.wait_ge(h, v)
            self.q[e].append((self.stage, f))

    def finalize(self):
        sems = [s.h for s in self.sems.values()]
        nums = [h.num for h in sems]
        rng = range(min(nums), max(nums) + 1)
        with self.nc.Block() as b0:
            def clearer(g, sems=sems, rng=rng):
                for h in sems:
                    g.sem_clear(h)
            b0.gpsimd(clearer)
        if self.warm is not None:
            dsts, src = self.warm
            ww = self.sems["WW"].h
            with self.nc.Block() as bw:
                def mk(dst):
                    def wq(e, dst=dst, src=src, ww=ww):
                        for i in range(8):
                            e.dma_start(dst[:, i * 128:(i + 1) * 128],
                                        src[:, i * 128:(i + 1) * 128]).then_inc(ww, 16)
                        e.wait_ge(ww, 256)
                    return wq
                bw.sync(mk(dsts[0]))
                bw.scalar(mk(dsts[1]))
        sems_nc = [s2.h for n2, s2 in self.sems.items() if n2 != "WW"]
        with self.nc.Block() as b2:
            def clearer2(g, sems=sems_nc):
                for h in sems:
                    g.sem_clear(h)
            b2.gpsimd(clearer2)
        with self.nc.Block() as block:
            for e in ENGINES:
                def runner(eng, e=e):
                    for lab, f in self.q[e]:
                        f(eng)
                getattr(block, e)(runner)

    def finalize_switch(self, qs, zfs, zfl):
        nc = self.nc
        sems = [s.h for s in self.sems.values()]
        with nc.Block() as b0:
            def clearer(g, sems=sems):
                for h in sems:
                    g.sem_clear(h)
            b0.gpsimd(clearer)
        dsts, src = self.warm
        ww = self.sems["WW"].h
        with nc.Block() as bw:
            def mk(dst):
                def wq(e, dst=dst, src=src, ww=ww):
                    for i in range(8):
                        e.dma_start(dst[:, i * 128:(i + 1) * 128],
                                    src[:, i * 128:(i + 1) * 128]).then_inc(ww, 16)
                    e.wait_ge(ww, 256)
                return wq
            bw.sync(mk(dsts[0]))
            bw.scalar(mk(dsts[1]))
        with nc.Block() as bz:
            def zload(s, zfs=zfs, zfl=zfl, ww=ww):
                s.dma_start(zfs[:], zfl).then_inc(ww, 16)
                s.wait_ge(ww, 272)
            bz.sync(zload)
        sems_nc = [s2.h for n2, s2 in self.sems.items() if n2 != "WW"]
        with nc.Block() as b2:
            def clearer2(g, sems=sems_nc):
                for h in sems:
                    g.sem_clear(h)
            b2.gpsimd(clearer2)
        with nc.Block() as block:
            for e in ENGINES:
                def runner(eng, e=e):
                    val = nc.values_load(zfs[0:1, 0:1], engines=[eng.engine],
                                         min_val=0, max_val=1,
                                         skip_runtime_bounds_check=True)
                    for zz in eng.Switch(val, 2):
                        cur, cm = None, None
                        for lab, f in qs[zz][e]:
                            if lab != cur:
                                if cm is not None:
                                    cm.__exit__(None, None, None)
                                cm = nc.named_scope(lab)
                                cm.__enter__()
                                cur = lab
                            f(eng)
                        if cm is not None:
                            cm.__exit__(None, None, None)
                getattr(block, e)(runner)

    def dma(self, queue, out, in_, wait=None, lane=0, **kw):
        so = self.sems[self.lanes[queue][lane]]
        wl = list(wait or [])
        if so.val > 0:
            wl.append((so, so.val))
        return self.emit(queue, "dma_start", out, in_, wait=wl or None,
                         inc=(so, 16), **kw)

    def mm(self, out, lhsT, rhs, start, stop, wait=None, inc=True):
        return self.emit("tensor", "matmul", out, lhsT, rhs, start=start, stop=stop,
                         wait=wait, inc=(self.sems["PE"], 1) if inc else None)

    def trans(self, out, in_, ident, wait=None):
        return self.emit("tensor", "transpose", out, in_, ident, wait=wait,
                         inc=(self.sems["PE"], 1))

    def act(self, out, in_, func, bias=0.0, scale=1.0, accum_out=None, wait=None):
        return self.emit("scalar", "activation", out, in_, func, bias=bias, scale=scale,
                         accum_out=accum_out, wait=wait, inc=(self.sems["AC"], 1))

    def dve(self, fname, *args, wait=None, **kw):
        return self.emit("vector", fname, *args, wait=wait, inc=(self.sems["DV"], 1), **kw)

    def fence_dv(self):
        return self.dve("tensor_copy", self.junk[:, 64:128], self.junk[:, 0:64])

    def fence_ac(self):
        return self.act(self.junk[:, 64:128], self.junk[:, 0:64], AF.Copy)

    def pool(self, fname, *args, wait=None, **kw):
        return self.emit("gpsimd", fname, *args, wait=wait, inc=(self.sems["PO"], 1), **kw)


class Common:
    pass


def build_common(ctx):
    c = Common()
    nc = ctx.nc
    c.identF = ctx.sbuf("identF", [128, 128], F32)
    c.iota_jp = ctx.sbuf("iota_jp", [128, 128], F32)
    c.mask_neg = ctx.sbuf("mask_neg", [128, 128], BF16)
    c.ones_row = ctx.sbuf("ones_row", [1, 128], BF16)
    c.ones_col = ctx.sbuf("ones_col", [128, 1], BF16)
    c.ones11f = ctx.sbuf("ones11f", [65, 1], F32)
    c.ones_rowF = ctx.sbuf("ones_rowF", [1, 128], F32)
    ctx.junk = ctx.sbuf("junkF", [128, 128], F32)
    ctx.dve("memset", ctx.junk[:], 0.0)

    t = ctx.pool("iota", c.iota_jp[:], [[1, 128]], channel_multiplier=-1,
                 allow_small_or_imprecise_dtypes=True)
    po = ctx.sems["PO"]
    ctx.dve("tensor_scalar", c.identF[:], c.iota_jp[:], 0.0, None, ALU.is_equal,
            wait=[(po, t)])
    ctx.dve("tensor_scalar", c.mask_neg[:], c.iota_jp[:], 0.0, NEG, ALU.is_lt, ALU.mult,
            wait=[(po, t)])
    c.permU = ctx.sbuf("permU", [128, 128], BF16)
    ctx.dve("tensor_scalar", c.permU[:], c.iota_jp[:], 64.0, None, ALU.is_equal,
            wait=[(po, t)])
    ctx.dve("memset", c.ones_row[:], 1.0)
    ctx.dve("memset", c.ones_col[:], 1.0)
    ctx.dve("memset", c.ones11f[:], 1.0)
    ctx.dve("memset", c.ones_rowF[:], 1.0)
    return c


def emit_ln_tile(ctx, c, src_f32, fm_dst, g, tmp, wait=None):
    DV, AC, PE = ctx.sems["DV"], ctx.sems["AC"], ctx.sems["PE"]
    sq, st = tmp["sq"], tmp["st"]
    lnf = tmp["lnf"]
    w = wait or []
    ctx.dve("reduce_sum", st[:, 0:1], src_f32, axis=AX.X, wait=w)
    ta = ctx.act(sq[:], src_f32, AF.Square, wait=w)
    ctx.dve("reduce_sum", st[:, 1:2], sq[:], axis=AX.X, wait=[(AC, ta)])
    ctx.dve("tensor_scalar", st[:, 2:3], st[:, 0:1], 1.0 / DM, None, ALU.mult)
    ctx.dve("tensor_scalar", st[:, 3:4], st[:, 1:2], 1.0 / DM, LN_EPS, ALU.mult, ALU.add)
    ctx.dve("tensor_tensor", st[:, 4:5], st[:, 2:3], st[:, 2:3], ALU.mult)
    tv = ctx.dve("tensor_tensor", st[:, 5:6], st[:, 3:4], st[:, 4:5], ALU.subtract)
    tl = ctx.act(st[:, 6:7], st[:, 5:6], AF.Ln, wait=[(DV, tv)])
    tr = ctx.act(st[:, 7:8], st[:, 6:7], AF.Exp, scale=-0.5)
    tn = ctx.dve("scalar_tensor_tensor", st[:, 4:5], st[:, 2:3], -1.0, st[:, 7:8],
                 op0=ALU.mult, op1=ALU.mult, wait=[(AC, tr)])
    tno = ctx.act(lnf[:], src_f32, AF.Copy, bias=st[:, 4:5], scale=st[:, 7:8],
                  wait=[(DV, tn)])
    last = None
    for cch in range(NCH):
        pb = tmp["psT"][cch % 2]
        wl = [(AC, tno)]
        if tmp["psT_tok"][cch % 2] is not None:
            wl.append((AC, tmp["psT_tok"][cch % 2]))
        tt = ctx.trans(pb[:, 0:128], lnf[:, cch * 128:(cch + 1) * 128], c.identF[:],
                       wait=wl)
        te = ctx.act(fm_dst[:, cch * FMW + 1 + g * 128: cch * FMW + 1 + (g + 1) * 128],
                     pb[:, 0:128], AF.Copy, wait=[(PE, tt)])
        tmp["psT_tok"][cch % 2] = te
        tmp["psT_mm"][cch % 2] = tt
        last = te
    return last, tno


def load_cast_weight(ctx, dst_bf, dram_slices, stage_f32, wait_extra=None):
    LD, DV = ctx.sems["LD"], ctx.sems["DV"]
    toks = []
    for (c0, w, ap) in dram_slices:
        wl = list(wait_extra or [])
        if stage_f32 in (None,):
            pass
        toks.append((c0, w, ctx.dma("sync", stage_f32[:, c0:c0 + w], ap, wait=wl or None)))
    last = None
    for (c0, w, tok) in toks:
        last = ctx.dve("tensor_copy", dst_bf[:, c0:c0 + w], stage_f32[:, c0:c0 + w],
                       wait=[(LD, tok)])
    return last


class PsumPool:

    def __init__(self, tiles, n_rot=2):
        self.t = [{"ap": t, "r": None} for t in tiles]
        self.n_rot = n_rot
        self.i = 0

    def acquire(self):
        d = self.t[self.i % self.n_rot]
        self.i += 1
        waits = [d["r"]] if d["r"] is not None else []
        return d, waits


def emit_ln_tile2(ctx, c, src_f32, fm_writes, tmp, wait=None):
    DV, AC, PE = ctx.sems["DV"], ctx.sems["AC"], ctx.sems["PE"]
    sq, st, lnf = tmp["sq"], tmp["st"], tmp["lnf"]
    w = list(wait or [])
    if tmp.get("act_prev") is not None:
        w = w + [tmp["act_prev"]]
    ctx.dve("reduce_sum", st[:, 0:1], src_f32, axis=AX.X, wait=w)
    ta = ctx.act(sq[:], src_f32, AF.Square, wait=w)
    ctx.dve("reduce_sum", st[:, 1:2], sq[:], axis=AX.X, wait=[(AC, ta)])
    ctx.dve("tensor_scalar", st[:, 2:3], st[:, 0:1], 1.0 / DM, None, ALU.mult)
    ctx.dve("tensor_scalar", st[:, 3:4], st[:, 1:2], 1.0 / DM, LN_EPS, ALU.mult, ALU.add)
    ctx.dve("tensor_tensor", st[:, 4:5], st[:, 2:3], st[:, 2:3], ALU.mult)
    tv = ctx.dve("tensor_tensor", st[:, 5:6], st[:, 3:4], st[:, 4:5], ALU.subtract)
    tvf = ctx.fence_dv()
    tl = ctx.act(st[:, 6:7], st[:, 5:6], AF.Ln, wait=[(DV, tvf)])
    tr = ctx.act(st[:, 7:8], st[:, 6:7], AF.Exp, scale=-0.5)
    trf = ctx.fence_ac()
    tmp["act_prev"] = (AC, trf)
    wno = [(AC, trf)]
    if tmp.get("lnf_rd") is not None:
        wno.append(tmp["lnf_rd"])
    tno = ctx.dve("tensor_scalar", lnf[:], src_f32, st[:, 2:3], st[:, 7:8],
                  ALU.subtract, ALU.mult, wait=wno)
    ntok = (DV, tno)
    last = None
    for cch in range(NCH):
        slot = cch % 2
        pb = tmp["psT"][slot]
        wl = [ntok]
        if tmp["psT_tok"][slot] is not None:
            wl.append(tmp["psT_tok"][slot])
        tt = ctx.trans(pb["ap"][:, 0:128], lnf[:, cch * 128:(cch + 1) * 128],
                       c.identF[:], wait=wl)
        te = ctx.act(fm_writes[cch], pb["ap"][:, 0:128], AF.Copy, wait=[(PE, tt)])
        tmp["psT_tok"][slot] = (AC, te)
        last = te
    tmp["lnf_rd"] = (PE, tt)
    return last, ntok, tv


def fm_col(cch, t0):
    return cch * FMW + 1 + t0


def emit_part1(ctx, c, P, fused, stages=("s1", "s2")):
    nc = ctx.nc
    LD, GD, SD, PE, DV, AC = (ctx.sems[k] for k in ("LD", "GD", "SD", "PE", "DV", "AC"))
    z = P["z"]
    towns = own_tiles(z)
    r = P["r"]
    lnfm = P["lnfm"]
    pool = P["pool"]

    with ctx.scope():
        xt = [ctx.sbuf("xt0", [128, DM], F32), ctx.sbuf("xt1", [128, DM], F32)]
        xt_tok = [None, None]
        tmp = {"sq": ctx.sbuf("ln_sq", [128, DM], F32),
               "st": ctx.sbuf("ln_st", [128, 16], F32),
               "lnf": ctx.sbuf("ln_f", [128, DM], F32),
               "psT": [pool.t[2], pool.t[3]],
               "psT_tok": [None, None], "act_prev": None, "lnf_rd": None}
        lnfm_r = lnfm[:].rearrange("p (c w) -> p c w", w=FMW)
        ctx.dve("memset", lnfm_r[:, :, 0:1], 0.0)
        tdum = ctx.dma("sync", xt[0][:], P["x"][0:128, :])
        xt_tok[0] = [(ctx.sems["LD"], tdum)]
        for g in range(NTILE):
            slot = g % 2
            wl = xt_tok[slot] if xt_tok[slot] is not None else []
            tl = ctx.dma("sync", xt[slot][:], P["x"][g * 128:(g + 1) * 128, :],
                         wait=wl or None)
            if g in towns:
                oi = towns.index(g)
                ctx.dma("sync", r[:, oi * DM:(oi + 1) * DM],
                        P["x"][g * 128:(g + 1) * 128, :])
            fmw = [lnfm[:, fm_col(cc, g * 128): fm_col(cc, g * 128) + 128]
                   for cc in range(NCH)]
            te, ntok, tv = emit_ln_tile2(ctx, c, xt[slot][:], fmw, tmp, wait=[(LD, tl)])
            xt_tok[slot] = [ntok]
    ctx.barrier()
    if P.get("dbg_lnfm_early") is not None:
        ctx.dma("gpsimd", P["dbg_lnfm_early"], lnfm[:])
        ctx.barrier()
    if "s2" not in stages:
        return

    cosT = ctx.sbuf("cosT", [128, T], BF16)
    sinT = ctx.sbuf("sinT", [128, T], BF16)
    ctx.stage = "rope"
    rope_scope = ctx.scope()
    rope_scope.__enter__()
    frow = ctx.sbuf("frow", [1, 128], F32)
    posc = ctx.sbuf("posc", [1, 512], F32)
    rt1 = ctx.sbuf("rt1", [128, 512], F32)
    rt2 = ctx.sbuf("rt2", [128, 512], F32)
    rti = ctx.sbuf("rti", [128, 512], I32)
    tf = ctx.pool("iota", frow[0:1, 0:64], [[1, 64]], channel_multiplier=0,
                  allow_small_or_imprecise_dtypes=True)
    tfc = ctx.dve("tensor_copy", frow[0:1, 64:128], frow[0:1, 0:64],
                  wait=[(ctx.sems["PO"], tf)])
    tf2 = ctx.act(frow[:], frow[:], AF.Exp, scale=-math.log(ROPE_BASE) / 64.0,
                  wait=[(DV, tfc)])
    tc2 = None
    prev_tm1 = None
    for ch in range(8):
        pb, wl = pool.acquire()
        wpos = [(ctx.sems["PE"], prev_tm1)] if prev_tm1 is not None else []
        tpos = ctx.pool("iota", posc[:], [[1, 512]], base=ch * 512,
                        channel_multiplier=0, allow_small_or_imprecise_dtypes=True,
                        wait=wpos or None)
        tm1 = ctx.mm(pb["ap"][:], frow[:], posc[:],
                     True, True, wait=wl + [(AC, tf2), (ctx.sems["PO"], tpos)])
        prev_tm1 = tm1
        wfirst = [(PE, tm1)] + ([(AC, tc2)] if tc2 is not None else [])
        ctx.dve("tensor_scalar", rt1[:], pb["ap"][:], 1.0 / TWO_PI, None, ALU.mult,
                wait=wfirst)
        ctx.dve("tensor_copy", rti[:], rt1[:])
        ctx.dve("tensor_copy", rt1[:], rti[:])
        ctx.dve("scalar_tensor_tensor", rt2[:], rt1[:], -PIH, pb["ap"][:],
                op0=ALU.mult, op1=ALU.add)
        ctx.dve("scalar_tensor_tensor", rt2[:], rt1[:], -PIL, rt2[:],
                op0=ALU.mult, op1=ALU.add)
        tcl = ctx.dve("tensor_scalar", rt2[:], rt2[:], math.pi, -math.pi, ALU.min, ALU.max)
        ts1 = ctx.act(sinT[:, ch * 512:(ch + 1) * 512], rt2[:], AF.Sin, wait=[(DV, tcl)])
        ctx.dve("tensor_scalar", rt1[:], pb["ap"][:], 1.0 / TWO_PI, 0.25,
                ALU.mult, ALU.add, wait=[(AC, ts1)])
        ctx.dve("tensor_copy", rti[:], rt1[:])
        ctx.dve("tensor_copy", rt1[:], rti[:])
        ctx.dve("scalar_tensor_tensor", rt2[:], rt1[:], -PIH, pb["ap"][:],
                op0=ALU.mult, op1=ALU.add)
        ctx.dve("scalar_tensor_tensor", rt2[:], rt1[:], -PIL, rt2[:],
                op0=ALU.mult, op1=ALU.add)
        ctx.dve("tensor_scalar", rt2[:], rt2[:], math.pi / 2.0, None, ALU.add)
        tcl2 = ctx.dve("tensor_scalar", rt2[:], rt2[:], math.pi, -math.pi, ALU.min, ALU.max)
        tc2 = ctx.act(cosT[:, ch * 512:(ch + 1) * 512], rt2[:], AF.Sin, wait=[(DV, tcl2)])
        pb["r"] = (AC, tc2)
    ctx.barrier()
    rope_scope.__exit__(None, None, None)

    kA = ctx.sbuf("kA", [65, T], BF16)
    kB = ctx.sbuf("kB", [64, T], BF16)
    qA = ctx.sbuf("qA", [65, T], BF16)
    qB = ctx.sbuf("qB", [64, T], BF16)
    vtm = ctx.sbuf("vtm", [128, NTILE * 129], BF16)
    wbuf = ctx.sbuf("wbuf", [128, DM], BF16)
    wq_bf = wk_bf = wv_bf = wo_bf = wbuf
    wstage = [ctx.sbuf("wst0", [128, 128], F32),
              ctx.sbuf("wst1", [128, 128], F32)]
    scr512 = ctx.sbuf("scr512", [128, 512], F32)
    vstage = scr512
    ptb = [ctx.sbuf("ptb0", [128, 1024], BF16),
           ctx.sbuf("ptb1", [128, 1024], BF16)]
    obf = ctx.sbuf("obf", [128, 1024], BF16)
    km = ctx.sbuf("km", [1, 4], F32)
    rs_s = ctx.sbuf("rs_s", [65, 1024], F32)
    tmp512 = rs_s
    rncols = ctx.sbuf("rncols", [128, 8], F32)
    rntmp = ctx.sbuf("rntmp", [128, 2], F32)
    xacc = [ctx.sbuf("xacc0", [128, DM], F32)]
    xacc_tok = [None]
    ropeP = ctx.sbuf("ropeP", [128, 512], BF16)
    brow_bf = ropeP[0:1, 0:512]
    sqq = ptb[0][0:64, 0:512]

    vtm_r = vtm[:].rearrange("p (t w) -> p t w", w=129)
    ctx.dve("memset", vtm_r[:, :, 64:65], 1.0)
    ctx.dve("memset", kA[64:65, :], 1.0)
    pv_done = [None, None]

    psO1, psO2 = P["psO1"], P["psO2"]
    psO_rd = [[]]
    wbuf_rd = {"w": None}
    kq_last_dv = [None]
    kq_last_gd = [None]
    v_last_ac = [None]
    wst_tok = [None, None]

    LL = (LD, ctx.sems["L1"])

    def load_w(dst, slices, name="w"):
        lastdv = None
        for i, ap in enumerate(slices):
            slot = i % 2
            wl = []
            if wst_tok[slot] is not None:
                wl.append(wst_tok[slot])
            tl = ctx.dma("sync", wstage[slot][:], ap, wait=wl or None, lane=slot)
            wl2 = [(LL[slot], tl)]
            if wbuf_rd[name] is not None:
                wl2.append(wbuf_rd[name])
            lastdv = ctx.dve("tensor_copy", dst[:, i * 128:(i + 1) * 128],
                             wstage[slot][:], wait=wl2)
            wst_tok[slot] = (DV, lastdv)
        return lastdv

    halves = q_halves(z)

    for h in range(L0H):
        ctx.stage = "Akq%d" % h
        def proj_rope(wbf, wtok, dstA, dstB, kind):
            lastdv = None
            ry_tok = [None]
            for tc in range(8):
                pb, wl = pool.acquire()
                cols = slice(tc * 512, (tc + 1) * 512)
                for cc in range(NCH):
                    w0 = (wl + [(DV, wtok)]) if cc == 0 else None
                    tmm = ctx.mm(pb["ap"][:], wbf[:, cc * 128:(cc + 1) * 128],
                                 lnfm[:, fm_col(cc, tc * 512): fm_col(cc, tc * 512) + 512],
                                 cc == 0, cc == NCH - 1, wait=w0)
                wev = [(PE, tmm)]
                if ry_tok[0] is not None:
                    wev.append(ry_tok[0])
                if kq_last_gd[0] is not None:
                    wev.append((GD, kq_last_gd[0]))
                tev = ctx.act(ropeP[:], pb["ap"][:], AF.Copy, wait=wev)
                pb["r"] = (AC, tev)
                pb2, wl2 = pool.acquire()
                tpm = ctx.mm(pb2["ap"][0:64, 0:512], c.permU[:, 0:64], ropeP[:],
                             True, True, wait=wl2 + [(AC, tev)])
                wev2 = [(PE, tpm)]
                if ry_tok[0] is not None:
                    wev2.append(ry_tok[0])
                rsw = rs_s[0:64, 0:512]
                rq = rs_s[0:64, 512:1024]
                tsw = ctx.act(rsw, pb2["ap"][0:64, 0:512], AF.Copy, wait=wev2)
                pb2["r"] = (AC, tsw)
                ctx.dve("tensor_tensor", rq, rsw,
                        cosT[0:64, cols], ALU.mult, wait=[(AC, tsw)])
                ctx.dve("tensor_tensor", scr512[0:64, :], ropeP[0:64, :],
                        sinT[0:64, cols], ALU.mult)
                ctx.dve("tensor_tensor", dstB[:, cols], rq,
                        scr512[0:64, :], ALU.add)
                ctx.dve("tensor_tensor", rq, ropeP[0:64, :],
                        cosT[0:64, cols], ALU.mult)
                ctx.dve("tensor_tensor", rsw, rsw,
                        sinT[0:64, cols], ALU.mult)
                lastdv = ctx.dve("tensor_tensor", dstA[0:64, cols], rq,
                                 rsw, ALU.subtract)
                ry_tok[0] = (DV, lastdv)
            return lastdv

        t_wk = load_w(wbuf, [P["w_ak"][cc * 128:(cc + 1) * 128, h, :] for cc in range(NCH)])
        tka = proj_rope(wbuf, t_wk, kA, kB, "k")
        wbuf_rd["w"] = (PE, PE.val)
        t_wq = load_w(wbuf, [P["w_aq"][cc * 128:(cc + 1) * 128, h, :] for cc in range(NCH)])
        tqa = proj_rope(wbuf, t_wq, qA, qB, "q")
        wbuf_rd["w"] = (PE, PE.val)
        ctx.stage = "Av%d" % h
        t_wv = load_w(wbuf, [P["w_av"][cc * 128:(cc + 1) * 128, h, :] for cc in range(NCH)])

        lastv = None
        for tc in range(8):
            pb, wl = pool.acquire()
            for cc in range(NCH):
                w0 = (wl + [(DV, t_wv)]) if cc == 0 else None
                tmm = ctx.mm(pb["ap"][:], wv_bf[:, cc * 128:(cc + 1) * 128],
                             lnfm[:, fm_col(cc, tc * 512): fm_col(cc, tc * 512) + 512],
                             cc == 0, cc == NCH - 1, wait=w0)
            tvs = ctx.act(vstage[:], pb["ap"][:], AF.Copy,
                          wait=[(PE, tmm), (DV, tqa)])
            pb["r"] = (AC, tvs)
            for j in range(4):
                g = tc * 4 + j
                pt, wlt = pool.acquire()
                tt = ctx.trans(pt["ap"][:, 0:128], vstage[:, j * 128:(j + 1) * 128],
                               c.identF[:], wait=wlt + [(AC, tvs)])
                ctx.act(vtm[:, g * 129: g * 129 + 64], pt["ap"][:, 0:64], AF.Copy,
                        wait=[(PE, tt)])
                lastv = ctx.act(vtm[:, g * 129 + 65: g * 129 + 129], pt["ap"][:, 64:128],
                                AF.Copy)
                pt["r"] = (AC, lastv)
        wbuf_rd["w"] = (PE, PE.val)
        t_wo = load_w(wbuf, [P["w_ao"][h, :, cc2 * 128:(cc2 + 1) * 128] for cc2 in range(NCH)])
        v_last_ac[0] = lastv

        ctx.stage = "Ab%d" % h
        for tc in range(8):
            cols = slice(tc * 512, (tc + 1) * 512)
            wsq = [(DV, tka)]
            if tc == 0 and pv_done[0] is not None:
                wsq.append(pv_done[0])
            tsq = ctx.act(sqq[:], kA[0:64, cols], AF.Square, wait=wsq)
            pb, wl = pool.acquire()
            tm1 = ctx.mm(pb["ap"][0:1, :], c.ones_col[0:64, :], sqq[:], True, False,
                         wait=wl + [(AC, tsq)])
            tsq2 = ctx.act(sqq[:], kB[:, cols], AF.Square, wait=[(PE, tm1)])
            tmm = ctx.mm(pb["ap"][0:1, :], c.ones_col[0:64, :], sqq[:], False, True,
                         wait=[(AC, tsq2)])
            tev = ctx.dve("reduce_max", km[0:1, 1:2] if tc > 0 else km[0:1, 0:1],
                          pb["ap"][0:1, :], axis=AX.X, wait=[(PE, tmm)])
            if tc > 0:
                tev = ctx.dve("tensor_tensor", km[0:1, 0:1], km[0:1, 0:1],
                              km[0:1, 1:2], ALU.max)
            pb["r"] = (DV, tev)
        ctx.dve("tensor_scalar", km[0:1, 1:2], km[0:1, 0:1], 1e-38, None, ALU.add)
        tkf = ctx.fence_dv()
        tl1 = ctx.act(km[0:1, 2:3], km[0:1, 1:2], AF.Ln, wait=[(DV, tkf)])
        tk2 = ctx.act(km[0:1, 3:4], km[0:1, 2:3], AF.Exp, scale=0.5)
        tB = None
        tBg = None
        for tc in range(8):
            cols = slice(tc * 512, (tc + 1) * 512)
            tsq = ctx.act(sqq[:], qA[0:64, cols], AF.Square, wait=[(DV, tqa)])
            pb, wl = pool.acquire()
            tm1 = ctx.mm(pb["ap"][0:1, :], c.ones_col[0:64, :], sqq[:], True, False,
                         wait=wl + [(AC, tsq)])
            tsq2 = ctx.act(sqq[:], qB[:, cols], AF.Square, wait=[(PE, tm1)])
            tmm = ctx.mm(pb["ap"][0:1, :], c.ones_col[0:64, :], sqq[:], False, True,
                         wait=[(AC, tsq2)])
            td = ctx.dve("tensor_scalar", rs_s[0:1, 0:512], pb["ap"][0:1, :],
                         1e-38, None, ALU.add, wait=[(PE, tmm)])
            tdf = ctx.fence_dv()
            ta = ctx.act(rs_s[0:1, 512:1024], rs_s[0:1, 0:512], AF.Ln,
                         wait=[(DV, tdf)])
            ta = ctx.act(rs_s[0:1, 0:512], rs_s[0:1, 512:1024], AF.Exp, scale=0.5)
            ta = ctx.fence_ac()
            wB = [(AC, ta)]
            if tBg is not None:
                wB.append((GD, tBg))
            tB = ctx.dve("tensor_scalar", brow_bf[:], rs_s[0:1, 0:512],
                         km[0:1, 3:4], -1.0, ALU.mult, ALU.mult, wait=wB)
            tB = ctx.fence_dv()
            tBg = ctx.dma("gpsimd", qA[64:65, cols], brow_bf[:], wait=[(DV, tB)])
            pb["r"] = (DV, td)
        kq_last_dv[0] = tB
        kq_last_gd[0] = tBg

        if P.get("dbg_h0") is not None and h == 0:
            d = P["dbg_h0"]
            ctx.dma("gpsimd", d["kA"], kA[:], wait=[(DV, kq_last_dv[0])])
            ctx.dma("gpsimd", d["kB"], kB[:])
            ctx.dma("gpsimd", d["qA"], qA[:])
            ctx.dma("gpsimd", d["qB"], qB[:])
            ctx.dma("gpsimd", d["vtm"], vtm[:], wait=[(AC, v_last_ac[0])])
            ctx.barrier()

        ctx.stage = "As%d" % h
        for (tiles, is_own) in halves:
            if not fused and not is_own:
                continue
            t0g = tiles[0]
            lastkt = tiles[-1]
            chunk_lastkt = [min(lastkt, t0g + (qc * 512 + 511) // 128) for qc in range(2)]
            for kt in range(0, lastkt + 1):
                base = max(0, (kt - t0g) * 128)
                slot = kt % 2
                pt = ptb[slot]
                wz = []
                if pv_done[slot] is not None:
                    wz.append(pv_done[slot])
                tz = None
                if base > 0:
                    z0 = max(0, base - 256)
                    tz = ctx.dve("memset", pt[:, z0:base], 0.0, wait=wz or None)
                exp_toks = []
                for qc in range(2):
                    if kt > chunk_lastkt[qc]:
                        continue
                    lo = max(base, qc * 512)
                    hi = (qc + 1) * 512
                    if lo >= hi:
                        continue
                    pb, wl = pool.acquire()
                    w0 = wl + [(DV, kq_last_dv[0]), (GD, kq_last_gd[0])]
                    ctx.mm(pb["ap"][:, 0:hi - lo], kA[:, kt * 128:(kt + 1) * 128],
                           qA[:, t0g * 128 + lo: t0g * 128 + hi], True, False, wait=w0)
                    tmm = ctx.mm(pb["ap"][:, 0:hi - lo], kB[:, kt * 128:(kt + 1) * 128],
                                 qB[:, t0g * 128 + lo: t0g * 128 + hi], False, True)
                    wexp = [(PE, tmm)]
                    if kt in tiles and lo == base:
                        tmsk = ctx.dve("tensor_tensor", pb["ap"][:, 0:128],
                                       pb["ap"][:, 0:128], c.mask_neg[:], ALU.add,
                                       wait=[(PE, tmm)])
                        wexp.append((DV, tmsk))
                    if tz is not None:
                        wexp.append((DV, tz))
                    elif wz:
                        wexp += wz
                    texp = ctx.act(pt[:, lo:hi], pb["ap"][:, 0:hi - lo], AF.Exp,
                                   scale=1.0 / L0D, wait=wexp)
                    pb["r"] = (AC, texp)
                    exp_toks.append(texp)
                wpv = [(AC, tk) for tk in exp_toks]
                if kt == 0:
                    wpv = wpv + psO_rd[0] + [(AC, v_last_ac[0])]
                lastpv = None
                for qc in range(2):
                    if kt > chunk_lastkt[qc]:
                        continue
                    cols = slice(qc * 512, (qc + 1) * 512)
                    st_, sp_ = (kt == 0), (kt == chunk_lastkt[qc])
                    ctx.mm(psO1[0:65, cols], vtm[:, kt * 129: kt * 129 + 65],
                           pt[:, cols], st_, sp_, wait=wpv if qc == 0 else None)
                    lastpv = ctx.mm(psO2[64:128, cols], vtm[:, kt * 129 + 65: kt * 129 + 129],
                                    pt[:, cols], st_, sp_)
                pv_done[slot] = (PE, lastpv)
            t_rs = ctx.dve("tensor_copy", rs_s[64:65, 0:1024], psO1[64:65, :],
                           wait=[(PE, PE.val)])
            to1 = ctx.act(obf[0:64, :], psO1[0:64, :], AF.Copy, wait=[(PE, PE.val)])
            to2 = ctx.act(obf[64:128, :], psO2[64:128, :], AF.Copy)
            psO_rd[0] = [(AC, to2), (DV, t_rs)]
            if P.get("dbg_h0") is not None and h == 0:
                d = P["dbg_h0"]
                hf = tiles[0] // 8
                ctx.dma("gpsimd", d["obf"][hf * 128:(hf + 1) * 128, :], obf[:],
                        wait=[(AC, to2)])
                ctx.dma("gpsimd", d["rsum"][hf:hf + 1, :], rs_s[64:65, :],
                        wait=[(DV, t_rs)])
            prev_ta2 = None
            for ti in range(8):
                pb, wl = pool.acquire()
                tmmr = ctx.mm(pb["ap"][:, 0:1], rs_s[64:65, ti * 128:(ti + 1) * 128],
                              c.ones11f[64:65, :], True, True, wait=wl + [(DV, t_rs)])
                wtd = [(PE, tmmr)]
                if prev_ta2 is not None:
                    wtd.append((AC, prev_ta2))
                td1 = ctx.dve("tensor_scalar", rntmp[:, 1:2], pb["ap"][:, 0:1],
                              1e-38, None, ALU.add, wait=wtd)
                td1f = ctx.fence_dv()
                ta1 = ctx.act(rntmp[:, 0:1], rntmp[:, 1:2], AF.Ln, wait=[(DV, td1f)])
                ta2 = ctx.act(rncols[:, ti:ti + 1], rntmp[:, 0:1], AF.Exp, scale=-1.0)
                ta2 = ctx.fence_ac()
                prev_ta2 = ta2
                pb["r"] = (AC, ta2)
            for ti in range(8):
                g = tiles[ti]
                is_own_tile = g in towns
                if not fused and not is_own_tile:
                    continue
                evac_toks = []
                if (not is_own_tile):
                    slot = 0
                    if h == 0:
                        wlx = [xacc_tok[slot]] if xacc_tok[slot] is not None else []
                        tx = ctx.dma("sync", xacc[slot][:],
                                     P["x"][g * 128:(g + 1) * 128, :], wait=wlx or None)
                for qc in range(2):
                    pb, wl = pool.acquire()
                    tmm = ctx.mm(pb["ap"][:], obf[:, ti * 128:(ti + 1) * 128],
                                 wo_bf[:, qc * 512:(qc + 1) * 512], True, True,
                                 wait=wl + [(AC, AC.val), (DV, t_wo)])
                    if is_own_tile:
                        oi = towns.index(g)
                        tev = ctx.dve("scalar_tensor_tensor",
                                      r[:, oi * DM + qc * 512: oi * DM + (qc + 1) * 512],
                                      pb["ap"][:], rncols[:, ti:ti + 1],
                                      r[:, oi * DM + qc * 512: oi * DM + (qc + 1) * 512],
                                      op0=ALU.mult, op1=ALU.add, wait=[(PE, tmm)])
                    else:
                        slot = 0
                        cols = slice(qc * 512, (qc + 1) * 512)
                        if h == 0:
                            tev = ctx.dve("scalar_tensor_tensor", xacc[slot][:, cols],
                                          pb["ap"][:], rncols[:, ti:ti + 1],
                                          xacc[slot][:, cols],
                                          op0=ALU.mult, op1=ALU.add,
                                          wait=[(PE, tmm), (LD, tx)])
                        else:
                            wlx2 = [(PE, tmm)]
                            if qc == 0 and xacc_tok[slot] is not None:
                                wlx2.append(xacc_tok[slot])
                            tev = ctx.dve("tensor_scalar", xacc[slot][:, cols],
                                          pb["ap"][:], rncols[:, ti:ti + 1], None,
                                          ALU.mult, wait=wlx2)
                    pb["r"] = (DV, tev)
                    evac_toks.append(tev)
                if not is_own_tile:
                    slot = 0
                    tfx = ctx.fence_dv()
                    td = ctx.dma("gpsimd", P["xmid"][g * 128:(g + 1) * 128, :],
                                 xacc[slot][:],
                                 wait=[(DV, tfx)],
                                 accum_op=(ALU.add if h > 0 else ALU.bypass))
                    xacc_tok[slot] = (GD, td)
        wbuf_rd["w"] = (PE, PE.val)
    ctx.barrier()


def emit_l0_ffn(ctx, c, P, fused):
    LD, GD, PE, DV, AC = (ctx.sems[k] for k in ("LD", "GD", "PE", "DV", "AC"))
    z = P["z"]
    towns = own_tiles(z)
    r, lnfm, pool = P["r"], P["lnfm"], P["pool"]

    kstall = ctx.sbuf("kstall", [128, NCH * DM], F32)
    qw_bf = ctx.sbuf("qw_bf", [128, NCH * DM], BF16)
    a_bf = ctx.sbuf("a_bf", [128, NCH * 16], BF16)
    b_bf = ctx.sbuf("b_bf", [16, DM], BF16)
    bias_bf = ctx.sbuf("bias_bf", [1, DM], BF16)
    qs = ctx.sbuf("qs", [128, 8], F32)
    qt = ctx.sbuf("qt", [128, DM], F32)
    qi = ctx.sbuf("qi", [128, DM], I32)
    fmtile = ctx.sbuf("fmtile", [128, DM], BF16)
    uT = ctx.sbuf("uT", [16, 128], BF16)
    xm = ctx.sbuf("xm", [128, DM], F32)
    tmp = {"sq": ctx.sbuf("l3_sq", [128, DM], F32),
           "st": ctx.sbuf("l3_st", [128, 16], F32),
           "lnf": ctx.sbuf("l3_f", [128, DM], F32),
           "psT": [pool.t[2], pool.t[3]],
           "psT_tok": [None, None], "act_prev": None, "lnf_rd": None}

    tstg = []
    for cc in range(NCH):
        tstg.append(ctx.dma("sync", kstall[:, cc * DM:(cc + 1) * DM],
                            P["ffn_kernel"][cc * 128:(cc + 1) * 128, :]))
        tA = ctx.dma("sync", tmp["sq"][:, cc * 16:(cc + 1) * 16],
                     P["ffn_a"][cc * 128:(cc + 1) * 128, :])
        ctx.dve("tensor_copy", a_bf[:, cc * 16:(cc + 1) * 16],
                tmp["sq"][:, cc * 16:(cc + 1) * 16], wait=[(LD, tA)])
    tb = ctx.dma("sync", tmp["sq"][0:16, 0:DM], P["ffn_b"])
    ctx.dve("tensor_copy", b_bf[:], tmp["sq"][0:16, 0:DM], wait=[(LD, tb)])
    tbias = ctx.dma("sync", tmp["sq"][0:1, 0:DM], P["ffn_bias"][None, :])
    tbias2 = ctx.dve("tensor_copy", bias_bf[:], tmp["sq"][0:1, 0:DM], wait=[(LD, tbias)])

    for cc in range(NCH):
        t1 = ctx.dve("tensor_reduce", qs[:, 1:2] if cc else qs[:, 0:1],
                     kstall[:, cc * DM:(cc + 1) * DM], axis=AX.X, op=ALU.max,
                     apply_absolute_value=True, wait=[(LD, tstg[cc])])
        if cc:
            ctx.dve("tensor_tensor", qs[:, 0:1], qs[:, 0:1], qs[:, 1:2], ALU.max)
    ctx.fence_dv()
    pb, wl = pool.acquire()
    tt = ctx.trans(pb["ap"][0:1, 0:128], qs[:, 0:1], c.identF[:],
                   wait=wl + [(DV, DV.val)])
    tgm = ctx.dve("reduce_max", qs[0:1, 2:3], pb["ap"][0:1, 0:128], axis=AX.X,
                  wait=[(PE, tt)])
    pb["r"] = (DV, tgm)
    tsc = ctx.dve("tensor_scalar", qs[0:1, 3:4], qs[0:1, 2:3], 1.0 / 7.0, None, ALU.mult)
    tla = ctx.act(qs[0:1, 6:7], qs[0:1, 3:4], AF.Ln, wait=[(DV, ctx.fence_dv())])
    tra = ctx.act(qs[0:1, 5:6], qs[0:1, 6:7], AF.Exp, scale=-1.0)
    tsc = ctx.fence_ac()
    pb, wl = pool.acquire()
    tbc = ctx.mm(pb["ap"][:, 0:1], c.ones_rowF[:, 0:128], qs[0:1, 3:4], True, False,
                 wait=wl + [(AC, tsc)])
    tbc = ctx.mm(pb["ap"][:, 1:2], c.ones_rowF[:, 0:128], qs[0:1, 5:6], False, True)
    tscol = ctx.act(qs[:, 4:5], pb["ap"][:, 0:1], AF.Copy, wait=[(PE, tbc)])
    tscol = ctx.act(qs[:, 5:6], pb["ap"][:, 1:2], AF.Copy)
    tscol = ctx.fence_ac()
    pb["r"] = (AC, tscol)
    for cc in range(NCH):
        ctx.dve("tensor_scalar", qt[:], kstall[:, cc * DM:(cc + 1) * DM],
                qs[:, 5:6], None, ALU.mult, wait=[(AC, tscol)])
        ctx.dve("tensor_copy", qi[:], qt[:])
        ctx.dve("tensor_copy", qt[:], qi[:])
        ctx.dve("tensor_scalar", qt[:], qt[:], 7.0, -8.0, ALU.min, ALU.max)
        ctx.dve("tensor_scalar", qw_bf[:, cc * DM:(cc + 1) * DM], qt[:],
                qs[:, 4:5], None, ALU.mult)

    xm_tok = None
    for g in range(NTILE):
        is_own = g in towns
        if not fused and not is_own:
            continue
        if is_own:
            oi = towns.index(g)
            src = r[:, oi * DM:(oi + 1) * DM]
            w0 = []
        else:
            wlx = xm_tok if xm_tok is not None else []
            txm = ctx.dma("sync", xm[:], P["xmid"][g * 128:(g + 1) * 128, :],
                          wait=(wlx or None))
            src = xm[:]
            w0 = [(LD, txm)]
        fmw = [fmtile[:, cc * 128:(cc + 1) * 128] for cc in range(NCH)]
        te, ntok, tv = emit_ln_tile2(ctx, c, src, fmw, tmp, wait=w0)
        pb, wl = pool.acquire()
        for cc in range(NCH):
            wu = (wl + [(AC, te)]) if cc == 0 else None
            tu = ctx.mm(pb["ap"][0:16, 0:128], a_bf[:, cc * 16:(cc + 1) * 16],
                        fmtile[:, cc * 128:(cc + 1) * 128], cc == 0, cc == NCH - 1,
                        wait=wu)
        tue = ctx.act(uT[:], pb["ap"][0:16, 0:128], AF.Copy, wait=[(PE, tu)])
        pb["r"] = (AC, tue)
        add_toks = []
        for qc in range(2):
            pb, wl = pool.acquire()
            for cc in range(NCH):
                wq_ = (wl + [(DV, DV.val), (AC, tue)]) if cc == 0 else None
                ctx.mm(pb["ap"][:], fmtile[:, cc * 128:(cc + 1) * 128],
                       qw_bf[:, cc * DM + qc * 512: cc * DM + (qc + 1) * 512],
                       cc == 0, False, wait=wq_)
            ctx.mm(pb["ap"][:], uT[:], b_bf[:, qc * 512:(qc + 1) * 512], False, False)
            tmm = ctx.mm(pb["ap"][:], c.ones_row[0:1, 0:128],
                         bias_bf[:, qc * 512:(qc + 1) * 512], False, True,
                         wait=[(DV, tbias2)] if qc == 0 and g == (towns[0] if not fused else 0) else None)
            if is_own:
                oi = towns.index(g)
                ta = ctx.dve("tensor_tensor", r[:, oi * DM + qc * 512: oi * DM + (qc + 1) * 512],
                             r[:, oi * DM + qc * 512: oi * DM + (qc + 1) * 512],
                             pb["ap"][:], ALU.add, wait=[(PE, tmm)])
            else:
                ta = ctx.dve("tensor_tensor", xm[:, qc * 512:(qc + 1) * 512],
                             xm[:, qc * 512:(qc + 1) * 512],
                             pb["ap"][:], ALU.add, wait=[(PE, tmm)])
            pb["r"] = (DV, ta)
            add_toks.append(ta)
        fmw3 = [lnfm[:, fm_col(cc, g * 128): fm_col(cc, g * 128) + 128]
                for cc in range(NCH)]
        src3 = r[:, towns.index(g) * DM: (towns.index(g) + 1) * DM] if is_own else xm[:]
        te3, ntok3, tv3 = emit_ln_tile2(ctx, c, src3, fmw3, tmp,
                                        wait=[(DV, tk) for tk in add_toks])
        if not is_own:
            xm_tok = [ntok3]
    ctx.barrier()


def declare_l0_inputs(nc):
    d = {}
    d["x"] = nc.dram_tensor("x", [T, DM], F32, kind="ExternalInput").ap()
    d["w_aq"] = nc.dram_tensor("w_aq", [DM, L0H, L0D], F32, kind="ExternalInput").ap()
    d["w_ak"] = nc.dram_tensor("w_ak", [DM, L0H, L0D], F32, kind="ExternalInput").ap()
    d["w_av"] = nc.dram_tensor("w_av", [DM, L0H, L0D], F32, kind="ExternalInput").ap()
    d["w_ao"] = nc.dram_tensor("w_ao", [L0H, L0D, DM], F32, kind="ExternalInput").ap()
    d["ffn_kernel"] = nc.dram_tensor("ffn_kernel", [DM, DM], F32, kind="ExternalInput").ap()
    d["ffn_bias"] = nc.dram_tensor("ffn_bias", [DM], F32, kind="ExternalInput").ap()
    d["ffn_a"] = nc.dram_tensor("ffn_a", [DM, 16], F32, kind="ExternalInput").ap()
    d["ffn_b"] = nc.dram_tensor("ffn_b", [16, DM], F32, kind="ExternalInput").ap()
    return d


def build_part1(z, fused=False, stages=("s1", "s2", "s3"), debug=()):
    nc = bass.Bass("TRN2", target_bir_lowering=False)
    P = declare_l0_inputs(nc)
    P["z"] = z
    out_r = nc.dram_tensor("out_r", [16 * 128, DM], F32, kind="ExternalOutput").ap()
    xmid_kind = "ExternalOutput" if "xmid" in debug else "Internal"
    P["xmid"] = nc.dram_tensor("xmid", [T, DM], F32, kind=xmid_kind).ap()
    dbg_lnfm = None
    if "lnfm" in debug:
        dbg_lnfm = nc.dram_tensor("lnfm_out", [128, NCH * FMW], BF16,
                                  kind="ExternalOutput").ap()
    if "h0" in debug:
        P["dbg_lnfm_early"] = nc.dram_tensor("d_lnfm_early", [128, NCH * FMW], BF16,
                                             kind="ExternalOutput").ap()
        P["dbg_rope"] = {
            "ropeP": nc.dram_tensor("d_ropeP", [128, 512], BF16, kind="ExternalOutput").ap(),
            "ropeQ": nc.dram_tensor("d_ropeQ", [128, 512], BF16, kind="ExternalOutput").ap(),
            "ropeY": nc.dram_tensor("d_ropeY", [64, 512], BF16, kind="ExternalOutput").ap(),
            "cosT": nc.dram_tensor("d_cosT", [128, T], BF16, kind="ExternalOutput").ap(),
            "sinT": nc.dram_tensor("d_sinT", [128, T], BF16, kind="ExternalOutput").ap(),
        }
        P["dbg_h0"] = {
            "kA": nc.dram_tensor("d_kA", [65, T], BF16, kind="ExternalOutput").ap(),
            "kB": nc.dram_tensor("d_kB", [64, T], BF16, kind="ExternalOutput").ap(),
            "qA": nc.dram_tensor("d_qA", [65, T], BF16, kind="ExternalOutput").ap(),
            "qB": nc.dram_tensor("d_qB", [64, T], BF16, kind="ExternalOutput").ap(),
            "vtm": nc.dram_tensor("d_vtm", [128, NTILE * 129], BF16, kind="ExternalOutput").ap(),
            "obf": nc.dram_tensor("d_obf", [512, 1024], BF16, kind="ExternalOutput").ap(),
            "rsum": nc.dram_tensor("d_rsum", [4, 1024], F32, kind="ExternalOutput").ap(),
        }
    with contextlib.ExitStack() as stack:
        ctx = Ctx(nc, stack)
        pool = PsumPool([ctx.psum("ps%d" % i, [128, 512], F32) for i in range(4)])
        P["pool"] = pool
        P["psO1"] = ctx.psum("psO1", [65, 1024], F32)
        P["psO2"] = ctx.psum("psO2", [128, 1024], F32)
        c = build_common(ctx)
        P["r"] = ctx.sbuf("r", [128, 16 * DM], F32)
        P["lnfm"] = ctx.sbuf("lnfm", [128, NCH * FMW], BF16)
        ctx.warm = ([P["r"][:, i * DM:(i + 1) * DM] for i in range(3)],
                    P["x"][0:128, :])
        with ctx.scope():
            emit_part1(ctx, c, P, fused, stages)
        if "s3" in stages:
            with ctx.scope():
                emit_l0_ffn(ctx, c, P, fused)
        towns = own_tiles(z)
        for i in range(16):
            ctx.dma("gpsimd", out_r[i * 128:(i + 1) * 128, :],
                    P["r"][:, i * DM:(i + 1) * DM])
        if dbg_lnfm is not None:
            ctx.dma("gpsimd", dbg_lnfm, P["lnfm"][:])
        GD = ctx.sems["GD"]
        ctx.emit("gpsimd", "wait_ge", GD.h, GD.val)
        ctx.finalize()
    return nc


def declare_l1_inputs(nc):
    d = {}
    d["qkv_w"] = nc.dram_tensor("qkv_w", [DM, 3 * DM], F32, kind="ExternalInput").ap()
    d["qkv_b"] = nc.dram_tensor("qkv_b", [3 * DM], F32, kind="ExternalInput").ap()
    d["w1_write"] = nc.dram_tensor("w1_write", [DM, 64], F32, kind="ExternalInput").ap()
    d["w2_write"] = nc.dram_tensor("w2_write", [DM, 64], F32, kind="ExternalInput").ap()
    d["w1_read"] = nc.dram_tensor("w1_read", [DM, 64], F32, kind="ExternalInput").ap()
    d["w2_read"] = nc.dram_tensor("w2_read", [DM, 64], F32, kind="ExternalInput").ap()
    d["kv_geo_w"] = nc.dram_tensor("kv_geo_w", [DM, 2 * DM], F32, kind="ExternalInput").ap()
    d["out_w"] = nc.dram_tensor("out_w", [DM, DM], F32, kind="ExternalInput").ap()
    d["out_b"] = nc.dram_tensor("out_b", [DM], F32, kind="ExternalInput").ap()
    d["kv_gate"] = nc.dram_tensor("kv_gate", [1, 1], F32, kind="ExternalInput").ap()
    d["mem_scale"] = nc.dram_tensor("mem_scale", [1, L1H], F32, kind="ExternalInput").ap()
    d["decay_logits"] = nc.dram_tensor("decay_logits", [1, L1H], F32, kind="ExternalInput").ap()
    d["f_w1"] = nc.dram_tensor("f_w1", [4096, DM], F32, kind="ExternalInput").ap()
    d["f_w1s"] = nc.dram_tensor("f_w1s", [1, 4096], F32, kind="ExternalInput").ap()
    d["f_w2"] = nc.dram_tensor("f_w2", [4096, DM], F32, kind="ExternalInput").ap()
    d["f_w2s"] = nc.dram_tensor("f_w2s", [1, 4096], F32, kind="ExternalInput").ap()
    d["f_h1"] = nc.dram_tensor("f_h1", [4096, 4096], F32, kind="ExternalInput").ap()
    d["f_h1s"] = nc.dram_tensor("f_h1s", [1, 4096], F32, kind="ExternalInput").ap()
    d["f_h2"] = nc.dram_tensor("f_h2", [4096, 4096], F32, kind="ExternalInput").ap()
    d["f_h2s"] = nc.dram_tensor("f_h2s", [1, 4096], F32, kind="ExternalInput").ap()
    d["f_w3"] = nc.dram_tensor("f_w3", [DM, 4096], F32, kind="ExternalInput").ap()
    d["f_w3s"] = nc.dram_tensor("f_w3s", [1, DM], F32, kind="ExternalInput").ap()
    return d


PAIRS = [(di, ei) for di in range(6) for ei in range(6) if di <= ei]
SIGNS = [1.0, -1.0, 1.0, 1.0, -1.0, 1.0]


def emit_l1_prep(ctx, c, P):
    LD, GD, SD, PE, DV, AC, PO = (ctx.sems[k] for k in
                                  ("LD", "GD", "SD", "PE", "DV", "AC", "PO"))
    lnfm, pool = P["lnfm"], P["pool"]

    wbf = ctx.sbuf("l1p_w", [128, NCH * 64], BF16)
    wst = ctx.sbuf("l1p_wst", [128, 64], F32)
    pstage = ctx.sbuf("l1p_ps", [64, 512], F32)
    ptm = {}
    for nm in ("w1", "w2", "r1", "r2"):
        ptm[nm] = ctx.sbuf("l1p_" + nm, [128, NTILE * 64], BF16)
    ww = ctx.sbuf("l1p_ww", [128, NTILE * 336], BF16)
    rl2 = ctx.sbuf("l1p_rl2", [128, NTILE * 96], BF16)
    u = ctx.sbuf("l1p_u", [128, 96], F32)
    u2 = ctx.sbuf("l1p_u2", [128, 96], F32)
    n2 = ctx.sbuf("l1p_n2", [128, 16], F32)
    t16 = ctx.sbuf("l1p_t16", [128, 16], F32)
    rr = ctx.sbuf("l1p_rr", [128, 336], F32)
    mss = ctx.sbuf("l1p_mss", [128, 336], F32)
    mem = P["gg"]
    Ah = ctx.sbuf("l1p_Ah", [128, L1H * 128], BF16)
    drows = ctx.sbuf("l1p_drows", [128, L1H], BF16)
    dcols = ctx.sbuf("l1p_dcols", [1, L1H * 128], BF16)
    d128r = ctx.sbuf("l1p_d128r", [1, 336], F32)
    lrow = ctx.sbuf("l1p_lrow", [1, 64], F32)
    lndb = ctx.sbuf("l1p_lndb", [128, 16], F32)
    mup = ctx.sbuf("l1p_mup", [128, 128], F32)
    irow = ctx.sbuf("l1p_irow", [1, 128], F32)
    pcol = ctx.sbuf("l1p_pcol", [128, 1], F32)
    scr = ctx.sbuf("l1p_scr", [128, 257], F32)
    Prow = ctx.sbuf("l1p_P", [1, 336], F32)
    Prowb = ctx.sbuf("l1p_Pb", [1, 336], BF16)
    qrow = ctx.sbuf("l1p_q", [1, 336], F32)
    msb = ctx.sbuf("l1p_msb", [128, 16], F32)
    kgb = ctx.sbuf("l1p_kgb", [128, 1], F32)

    t0 = ctx.dma("sync", lrow[0:1, 0:16], P["decay_logits"])
    ts0 = ctx.act(lrow[0:1, 16:32], lrow[0:1, 0:16], AF.Sigmoid, wait=[(LD, t0)])
    tl0 = ctx.act(lrow[0:1, 32:48], lrow[0:1, 16:32], AF.Ln)
    td0 = ctx.act(lrow[0:1, 48:64], lrow[0:1, 32:48], AF.Exp, scale=128.0)
    td0 = ctx.fence_ac()
    pb, wl = pool.acquire()
    tb0 = ctx.mm(pb["ap"][:, 0:16], c.ones_rowF[:, 0:128], lrow[0:1, 32:48],
                 True, True, wait=wl + [(AC, td0)])
    tlb = ctx.act(lndb[:], pb["ap"][:, 0:16], AF.Copy, wait=[(PE, tb0)])
    tlb = ctx.fence_ac()
    pb["r"] = (AC, tlb)
    ctx.dve("tensor_scalar", mup[:], c.iota_jp[:], 0.0, None, ALU.is_gt)
    tpo = ctx.pool("iota", irow[:], [[1, 128]], channel_multiplier=0,
                   allow_small_or_imprecise_dtypes=True)
    tpc = ctx.pool("iota", pcol[:], [[1, 1]], channel_multiplier=1,
                   allow_small_or_imprecise_dtypes=True)
    for cc in range(21):
        ctx.dve("tensor_copy", d128r[0:1, cc * 16:(cc + 1) * 16], lrow[0:1, 48:64],
                wait=[(AC, tlb)] if cc == 0 else None)
    tps = ctx.dve("tensor_scalar", pcol[:], pcol[:], 128.0, None, ALU.subtract,
                  wait=[(PO, tpc)])
    tprev = tlb
    for h in range(L1H):
        tA1 = ctx.dve("tensor_scalar", scr[:, 0:128], c.iota_jp[:],
                      lndb[:, h:h + 1], None, ALU.mult, wait=[(AC, tprev)])
        tA1f = ctx.fence_dv()
        tA2 = ctx.act(scr[:, 128:256], scr[:, 0:128], AF.Exp, wait=[(DV, tA1f)])
        tA2f = ctx.fence_ac()
        tA3 = ctx.dve("tensor_tensor", Ah[:, h * 128:(h + 1) * 128], scr[:, 128:256],
                      mup[:], ALU.mult, wait=[(AC, tA2f)])
        tD1 = ctx.dve("tensor_scalar", scr[0:1, 0:128], irow[:],
                      lrow[0:1, 32 + h:33 + h], None, ALU.mult, wait=[(PO, tpo)])
        tD1f = ctx.fence_dv()
        tD2 = ctx.act(dcols[0:1, h * 128:(h + 1) * 128], scr[0:1, 0:128], AF.Exp,
                      wait=[(DV, tD1f)])
        tD3 = ctx.dve("tensor_scalar", scr[:, 256:257], pcol[:],
                      lndb[:, h:h + 1], None, ALU.mult, wait=[(DV, tps)])
        tD3f = ctx.fence_dv()
        tD4 = ctx.act(drows[:, h:h + 1], scr[:, 256:257], AF.Exp, scale=-1.0,
                      wait=[(DV, tD3f)])
        tprev = ctx.fence_ac()

    t1 = ctx.dma("sync", lrow[0:1, 0:16], P["mem_scale"])
    t2 = ctx.dma("sync", kgb[0:1, 0:1], P["kv_gate"])
    pb, wl = pool.acquire()
    tb1 = ctx.mm(pb["ap"][:, 0:16], c.ones_rowF[:, 0:128], lrow[0:1, 0:16],
                 True, False, wait=wl + [(LD, t2), (AC, tprev)])
    tb1 = ctx.mm(pb["ap"][:, 16:17], c.ones_rowF[:, 0:128], kgb[0:1, 0:1],
                 False, True)
    tmsb = ctx.act(msb[:], pb["ap"][:, 0:16], AF.Copy, wait=[(PE, tb1)])
    tkgb = ctx.act(kgb[:], pb["ap"][:, 16:17], AF.Copy)
    tkgf = ctx.fence_ac()
    pb["r"] = (AC, tkgf)

    wsrc = {"w1": "w1_write", "w2": "w2_write", "r1": "w1_read", "r2": "w2_read"}
    wbf_rd = None
    for nm in ("w1", "w2", "r1", "r2"):
        lastc = None
        for cc in range(NCH):
            tl = ctx.dma("sync", wst[:], P[wsrc[nm]][cc * 128:(cc + 1) * 128, :],
                         wait=([wbf_rd] if (wbf_rd and cc == 0) else None))
            wv = wbf[:, cc * 64:(cc + 1) * 64].rearrange("p (pd h) -> p h pd", h=16)
            sv = wst[:].rearrange("p (h pd) -> p h pd", pd=PD)
            lastc = ctx.dve("tensor_copy", wv, sv, wait=[(LD, tl)])
        shift = -1 if nm == "w1" else 0
        for tc in range(8):
            pb, wl = pool.acquire()
            for cc in range(NCH):
                col0 = fm_col(cc, tc * 512) + shift
                w0 = (wl + [(DV, lastc)]) if cc == 0 else None
                tmm = ctx.mm(pb["ap"][0:64, :], wbf[:, cc * 64:(cc + 1) * 64],
                             lnfm[:, col0: col0 + 512], cc == 0, cc == NCH - 1,
                             wait=w0)
            tev = ctx.act(pstage[:], pb["ap"][0:64, :], AF.Copy, wait=[(PE, tmm)])
            tevf = ctx.fence_ac()
            pb["r"] = (AC, tevf)
            for j in range(4):
                g = tc * 4 + j
                pt, wlt = pool.acquire()
                tt = ctx.trans(pt["ap"][0:128, 0:64],
                               pstage[:, j * 128:(j + 1) * 128],
                               c.identF[0:64, 0:64], wait=wlt + [(AC, tevf)])
                te2 = ctx.act(ptm[nm][:, g * 64:(g + 1) * 64], pt["ap"][0:128, 0:64],
                              AF.Copy, wait=[(PE, tt)])
                pt["r"] = (AC, te2)
        wbf_rd = (PE, PE.val)

    tfa = ctx.fence_ac()
    for g in range(NTILE):
        for which in range(2):
            p1 = ptm["w1"] if which == 0 else ptm["r1"]
            p2 = ptm["w2"] if which == 0 else ptm["r2"]
            for cc in range(6):
                pi, pj = (0, 0, 0, 1, 1, 2)[cc], (1, 2, 3, 2, 3, 3)[cc]
                w0 = [(AC, tfa)] if cc == 0 else None
                ctx.dve("tensor_tensor", u[:, cc * 16:(cc + 1) * 16],
                        p1[:, g * 64 + pi * 16: g * 64 + (pi + 1) * 16],
                        p2[:, g * 64 + pj * 16: g * 64 + (pj + 1) * 16], ALU.mult,
                        wait=w0)
                ctx.dve("tensor_tensor", u2[:, cc * 16:(cc + 1) * 16],
                        p1[:, g * 64 + pj * 16: g * 64 + (pj + 1) * 16],
                        p2[:, g * 64 + pi * 16: g * 64 + (pi + 1) * 16], ALU.mult)
                ctx.dve("tensor_tensor", u[:, cc * 16:(cc + 1) * 16],
                        u[:, cc * 16:(cc + 1) * 16],
                        u2[:, cc * 16:(cc + 1) * 16], ALU.subtract)
                ctx.dve("tensor_tensor", u2[:, cc * 16:(cc + 1) * 16],
                        u[:, cc * 16:(cc + 1) * 16],
                        u[:, cc * 16:(cc + 1) * 16], ALU.mult)
                if cc == 0:
                    ctx.dve("tensor_copy", n2[:], u2[:, 0:16])
                else:
                    ctx.dve("tensor_tensor", n2[:], n2[:],
                            u2[:, cc * 16:(cc + 1) * 16], ALU.add)
            ctx.dve("tensor_scalar", n2[:], n2[:], 1e-24, None, ALU.max)
            tnf = ctx.fence_dv()
            tln = ctx.act(t16[:], n2[:], AF.Ln, wait=[(DV, tnf)])
            tex = ctx.act(n2[:], t16[:], AF.Exp, scale=-0.5)
            tfa = ctx.fence_ac()
            for cc in range(6):
                if which == 0:
                    ctx.dve("tensor_tensor", u[:, cc * 16:(cc + 1) * 16],
                            u[:, cc * 16:(cc + 1) * 16], n2[:], ALU.mult,
                            wait=[(AC, tfa)] if cc == 0 else None)
                else:
                    ctx.dve("scalar_tensor_tensor",
                            rl2[:, g * 96 + cc * 16: g * 96 + (cc + 1) * 16],
                            u[:, cc * 16:(cc + 1) * 16], SIGNS[cc], n2[:],
                            op0=ALU.mult, op1=ALU.mult,
                            wait=[(AC, tfa)] if cc == 0 else None)
            if which == 0:
                ctx.dve("tensor_scalar", u2[:, 0:96], u[:, 0:96], 2.0, None, ALU.mult)
                for ci, (di, ei) in enumerate(PAIRS):
                    src1 = u2[:, di * 16:(di + 1) * 16] if di < ei else \
                        u[:, di * 16:(di + 1) * 16]
                    ctx.dve("tensor_tensor",
                            ww[:, g * 336 + ci * 16: g * 336 + (ci + 1) * 16],
                            src1, u[:, ei * 16:(ei + 1) * 16], ALU.mult)
    tprep = ctx.fence_dv()

    ctx.dve("memset", Prow[:], 0.0)
    ctx.dve("memset", Prowb[:], 0.0)
    tPb = ctx.fence_dv()
    for g in range(NTILE):
        wwg = ww[:, g * 336:(g + 1) * 336]
        pb, wl = pool.acquire()
        for h in range(L1H):
            rhs = wwg.rearrange("p (cc h) -> p cc h", h=16)[:, :, h]
            ctx.mm(pb["ap"][0:1, h * 21:(h + 1) * 21], drows[:, h:h + 1], rhs,
                   h == 0, h == L1H - 1,
                   wait=(wl + [(DV, tprep)]) if h == 0 else None)
        qv = qrow[:].rearrange("p (cc h) -> p h cc", h=16)
        tq = ctx.dve("tensor_copy", qv, pb["ap"][0:1, 0:336].rearrange(
            "p (h cc) -> p h cc", cc=21), wait=[(PE, PE.val)])
        pb["r"] = (DV, tq)
        pm, wlm = pool.acquire()
        for h in range(L1H):
            wA = (wlm + [(DV, tPb)]) if h == 0 else None
            ctx.mm(pm["ap"][:, h * 21:(h + 1) * 21],
                   Ah[:, h * 128:(h + 1) * 128],
                   wwg.rearrange("p (cc h) -> p cc h", h=16)[:, :, h],
                   True, False, wait=wA)
            ctx.mm(pm["ap"][:, h * 21:(h + 1) * 21],
                   dcols[0:1, h * 128:(h + 1) * 128],
                   Prowb[:].rearrange("p (cc h) -> p h cc", h=16)[:, h, :],
                   False, True)
        tpm = ctx.sems["PE"].val
        rlg = rl2[:, g * 96:(g + 1) * 96]
        ctx.dve("tensor_scalar", u2[:, 0:96], rlg, 2.0, None, ALU.mult,
                wait=[(DV, tq)])
        for ci, (di, ei) in enumerate(PAIRS):
            src1 = u2[:, di * 16:(di + 1) * 16] if di < ei else \
                rlg[:, di * 16:(di + 1) * 16]
            ctx.dve("tensor_tensor", rr[:, ci * 16:(ci + 1) * 16],
                    src1, rlg[:, ei * 16:(ei + 1) * 16], ALU.mult)
        for h in range(L1H):
            rrh = rr[:].rearrange("p (cc h) -> p cc h", h=16)[:, :, h]
            ctx.dve("tensor_tensor", mss[:, h * 21:(h + 1) * 21],
                    rrh, pm["ap"][:, h * 21:(h + 1) * 21], ALU.mult,
                    wait=[(PE, tpm)] if h == 0 else None)
        for h in range(L1H):
            ctx.dve("reduce_sum", mem[:, g * 16 + h: g * 16 + h + 1],
                    mss[:, h * 21:(h + 1) * 21], axis=AX.X)
        tms = ctx.sems["DV"].val
        pm["r"] = (DV, tms)
        ctx.dve("tensor_tensor", Prow[:], Prow[:], d128r[:], ALU.mult,
                wait=[(PE, tpm)])
        ctx.dve("tensor_tensor", Prow[:], Prow[:], qrow[:], ALU.add)
        ctx.dve("tensor_copy", Prowb[:], Prow[:])
        tPb = ctx.fence_dv()
    for g in range(NTILE):
        cols = slice(g * 16, (g + 1) * 16)
        ctx.dve("tensor_tensor", mem[:, cols], mem[:, cols], msb[:], ALU.mult,
                wait=[(DV, tPb)] if g == 0 else None)
        tg1f = ctx.fence_dv()
        ctx.act(mem[:, cols], mem[:, cols], AF.Sigmoid, wait=[(DV, tg1f)])
        tg2f = ctx.fence_ac()
        ctx.dve("tensor_scalar", mem[:, cols], mem[:, cols], kgb[:, 0:1],
                None, ALU.mult, wait=[(AC, tg2f)])
        tg3f = ctx.fence_dv()
        pt, wlt = pool.acquire()
        tt = ctx.trans(pt["ap"][0:16, 0:128], mem[:, cols], c.identF[:],
                       wait=wlt + [(DV, tg3f)])
        lastgg = ctx.act(P["ggfm"][:, g * 128:(g + 1) * 128], pt["ap"][0:16, 0:128],
                         AF.Copy, wait=[(PE, tt)])
        pt["r"] = (AC, lastgg)
    ctx.barrier()


def emit_l1_attn(ctx, c, P):
    LD, GD, SD, PE, DV, AC = (ctx.sems[k] for k in
                              ("LD", "GD", "SD", "PE", "DV", "AC"))
    z = P["z"]
    towns = own_tiles(z)
    r, lnfm, pool = P["r"], P["lnfm"], P["pool"]
    psO1 = P["psO1"]
    SCALE = 0.125

    kT = ctx.sbuf("l1a_k", [65, T], BF16)
    qT = ctx.sbuf("l1a_q", [65, 2048], BF16)
    vtm = ctx.sbuf("l1a_v", [128, NTILE * 65], BF16)
    wbf = ctx.sbuf("l1a_w", [128, NCH * 64 * 5], BF16)
    wstW = ctx.sbuf("l1a_wst", [128, 1024], F32)
    wout = ctx.sbuf("l1a_wout", [64, DM], BF16)
    bq = ctx.sbuf("l1a_bq", [64, 4], F32)
    brow = ctx.sbuf("l1a_brow", [1, 64], F32)
    obrow = ctx.sbuf("l1a_obrow", [1, DM], F32)
    kf = ctx.sbuf("l1a_kf", [64, 512], F32)
    kg = ctx.sbuf("l1a_kg", [64, 512], F32)
    ptb = [ctx.sbuf("l1a_ptb", [128, 1024], BF16),
           ctx.sbuf("l1a_ptb1", [128, 1024], BF16)]
    obf = ctx.sbuf("l1a_obf", [64, 1024], BF16)
    sqq = ptb[0][0:64, 0:512]
    km = ctx.sbuf("l1a_km", [1, 8], F32)
    rs_s = ctx.sbuf("l1a_rs", [65, 1024], F32)
    brow_bf = ctx.sbuf("l1a_brbf", [1, 512], BF16)
    rncols = ctx.sbuf("l1a_rn", [128, 8], F32)
    rntmp = ctx.sbuf("l1a_rnt", [128, 2], F32)
    ggrow = ctx.sbuf("l1a_ggrow", [1, T], BF16)

    vtm_r = vtm[:].rearrange("p (t w) -> p t w", w=65)
    ctx.dve("memset", vtm_r[:, :, 64:65], 1.0)
    ctx.dve("memset", kT[64:65, :], 1.0)
    tob = ctx.dma("sync", obrow[:], P["out_b"][None, :])

    halves = [hf for hf in q_halves(z) if hf[1]]
    ggrow_rd = [None]
    wbuf_rd = None
    kq_dv = [None]
    v_ac = [None]
    psO_rd = [[]]
    pv_done = [None, None]

    LL2 = (LD, ctx.sems["L1"])
    wst_rd = [None, None]
    for h in range(L1H):
        ctx.stage = "Bw%d" % h
        col_of = {0: h * 64, 1: DM + h * 64, 2: 2 * DM + h * 64}
        lastc = None
        for p5 in range(5):
            sl = p5 % 2
            if p5 < 3:
                src = P["qkv_w"][:, col_of[p5]: col_of[p5] + 64]
            else:
                base = (p5 - 3) * DM + h * 64
                src = P["kv_geo_w"][:, base: base + 64]
            src3 = src.rearrange("(c p) f -> p c f", p=128)
            dst3 = wstW[:, sl * 512:(sl + 1) * 512].rearrange("p (c f) -> p c f",
                                                              f=64)
            wl0 = [wst_rd[sl]] if wst_rd[sl] is not None else None
            tl = ctx.dma("sync", dst3, src3, wait=wl0, lane=sl)
            wl2 = [(LL2[sl], tl)]
            if wbuf_rd is not None and p5 == 0:
                wl2.append(wbuf_rd)
            lastc = ctx.dve("tensor_copy", wbf[:, p5 * 512:(p5 + 1) * 512],
                            wstW[:, sl * 512:(sl + 1) * 512], wait=wl2)
            wst_rd[sl] = (DV, lastc)
        for qc in range(2):
            sl = qc
            wl0 = [wst_rd[sl]] if wst_rd[sl] is not None else None
            tlo = ctx.dma("sync", wstW[0:64, sl * 512:(sl + 1) * 512],
                          P["out_w"][h * 64:(h + 1) * 64, qc * 512:(qc + 1) * 512],
                          wait=wl0, lane=sl)
            lastc2 = ctx.dve("tensor_copy", wout[:, qc * 512:(qc + 1) * 512],
                             wstW[0:64, sl * 512:(sl + 1) * 512],
                             wait=[(LL2[sl], tlo)])
            wst_rd[sl] = (DV, lastc2)
        wgg = [ggrow_rd[0]] if ggrow_rd[0] is not None else []
        tggr = ctx.dma("gpsimd", ggrow[:], P["ggfm"][h:h + 1, :], wait=wgg or None)
        pb, wl = pool.acquire()
        for j in range(3):
            tbl = ctx.dma("sync", brow[:],
                          P["qkv_b"][None, j * DM + h * 64: j * DM + (h + 1) * 64])
            tt = ctx.trans(pb["ap"][0:64, j:j + 1], brow[:],
                           c.ones11f[0:1, 0:1],
                           wait=(wl if j == 0 else []) + [(LD, tbl)])
        tbq = ctx.act(bq[:, 0:3], pb["ap"][0:64, 0:3], AF.Copy, wait=[(PE, tt)])
        tbqf = ctx.fence_ac()
        pb["r"] = (AC, tbqf)

        ctx.stage = "Bkv%d" % h
        lastk = None
        lastv = None
        for tc in range(8):
            cols = slice(tc * 512, (tc + 1) * 512)
            pk, wlk = pool.acquire()
            for cc in range(NCH):
                w0 = (wlk + [(DV, lastc), (AC, tbqf)]) if cc == 0 else None
                tmm = ctx.mm(pk["ap"][0:64, :],
                             wbf[:, (1 * NCH + cc) * 64:(1 * NCH + cc + 1) * 64],
                             lnfm[:, fm_col(cc, tc * 512): fm_col(cc, tc * 512) + 512],
                             cc == 0, cc == NCH - 1, wait=w0)
            tkf = ctx.dve("tensor_scalar", kf[:], pk["ap"][0:64, :], bq[0:64, 1:2],
                          None, ALU.add, wait=[(PE, tmm)])
            tkff = ctx.fence_dv()
            pk["r"] = (DV, tkff)
            pg, wlg = pool.acquire()
            for cc in range(NCH):
                w0 = wlg if cc == 0 else None
                tmm = ctx.mm(pg["ap"][0:64, :],
                             wbf[:, (3 * NCH + cc) * 64:(3 * NCH + cc + 1) * 64],
                             lnfm[:, fm_col(cc, tc * 512): fm_col(cc, tc * 512) + 512],
                             cc == 0, cc == NCH - 1, wait=w0)
            tkg = ctx.act(kg[:], pg["ap"][0:64, :], AF.Copy, wait=[(PE, tmm)])
            tkgf2 = ctx.fence_ac()
            pg["r"] = (AC, tkgf2)
            pgg, wlgg = pool.acquire()
            tggb = ctx.mm(pgg["ap"][0:64, :], c.ones_row[0:1, 0:64],
                          ggrow[0:1, cols], True, True,
                          wait=wlgg + [(AC, tkgf2), (GD, tggr)])
            tke1 = ctx.dve("tensor_tensor", kg[:], kg[:], pgg["ap"][0:64, :],
                           ALU.mult, wait=[(PE, tggb), (AC, tkgf2), (DV, tkff)])
            lastk = ctx.dve("tensor_tensor", kT[0:64, cols], kf[:], kg[:], ALU.add)
            pgg["r"] = (DV, lastk)
            pv, wlv = pool.acquire()
            for cc in range(NCH):
                w0 = (wlv + [(DV, lastk)]) if cc == 0 else None
                tmm = ctx.mm(pv["ap"][0:64, :],
                             wbf[:, (2 * NCH + cc) * 64:(2 * NCH + cc + 1) * 64],
                             lnfm[:, fm_col(cc, tc * 512): fm_col(cc, tc * 512) + 512],
                             cc == 0, cc == NCH - 1, wait=w0)
            tvf = ctx.dve("tensor_scalar", kf[:], pv["ap"][0:64, :], bq[0:64, 2:3],
                          None, ALU.add, wait=[(PE, tmm)])
            pv["r"] = (DV, tvf)
            pvg, wlvg = pool.acquire()
            for cc in range(NCH):
                w0 = (wlvg + [(DV, tvf)]) if cc == 0 else None
                tmm = ctx.mm(pvg["ap"][0:64, :],
                             wbf[:, (4 * NCH + cc) * 64:(4 * NCH + cc + 1) * 64],
                             lnfm[:, fm_col(cc, tc * 512): fm_col(cc, tc * 512) + 512],
                             cc == 0, cc == NCH - 1, wait=w0)
            tvg = ctx.act(kg[:], pvg["ap"][0:64, :], AF.Copy, wait=[(PE, tmm)])
            tvgf = ctx.fence_ac()
            pvg["r"] = (AC, tvgf)
            pgg2, wlgg2 = pool.acquire()
            tggb2 = ctx.mm(pgg2["ap"][0:64, :], c.ones_row[0:1, 0:64],
                           ggrow[0:1, cols], True, True,
                           wait=wlgg2 + [(AC, tvgf), (GD, tggr)])
            ctx.dve("tensor_tensor", kg[:], kg[:], pgg2["ap"][0:64, :],
                    ALU.mult, wait=[(PE, tggb2), (AC, tvgf)])
            tven = ctx.dve("tensor_tensor", kf[:], kf[:], kg[:], ALU.add)
            tvenf = ctx.fence_dv()
            pgg2["r"] = (DV, tven)
            for j in range(4):
                g = tc * 4 + j
                pt, wlt = pool.acquire()
                tt = ctx.trans(pt["ap"][0:128, 0:64], kf[:, j * 128:(j + 1) * 128],
                               c.identF[0:64, 0:64], wait=wlt + [(DV, tvenf)])
                lastv = ctx.act(vtm[:, g * 65: g * 65 + 64], pt["ap"][0:128, 0:64],
                                AF.Copy, wait=[(PE, tt)])
                pt["r"] = (AC, lastv)
        v_ac[0] = ctx.fence_ac()
        ctx.stage = "Bq%d" % h
        ranges = [(0, 1024), (3072, 4096)] if z == 0 else [(1024, 3072)]
        lastq = None
        qofs = 0
        for (lo, hi) in ranges:
            for t0 in range(lo, hi, 512):
                pq, wlq = pool.acquire()
                for cc in range(NCH):
                    w0 = wlq if cc == 0 else None
                    tmm = ctx.mm(pq["ap"][0:64, :],
                                 wbf[:, (0 * NCH + cc) * 64:(0 * NCH + cc + 1) * 64],
                                 lnfm[:, fm_col(cc, t0): fm_col(cc, t0) + 512],
                                 cc == 0, cc == NCH - 1, wait=w0)
                lastq = ctx.dve("tensor_scalar", qT[0:64, qofs:qofs + 512],
                                pq["ap"][0:64, :], bq[0:64, 0:1],
                                None, ALU.add, wait=[(PE, tmm)])
                pq["r"] = (DV, lastq)
                qofs += 512
        tqf = ctx.fence_dv()
        ctx.stage = "Bb%d" % h
        for tc in range(8):
            cols = slice(tc * 512, (tc + 1) * 512)
            tsq = ctx.act(sqq[:], kT[0:64, cols], AF.Square, wait=[(DV, lastk)])
            pb, wl = pool.acquire()
            tmm = ctx.mm(pb["ap"][0:1, :], c.ones_col[0:64, :], sqq[:], True, True,
                         wait=wl + [(AC, tsq)])
            tev = ctx.dve("reduce_max", km[0:1, 1:2] if tc > 0 else km[0:1, 0:1],
                          pb["ap"][0:1, :], axis=AX.X, wait=[(PE, tmm)])
            if tc > 0:
                tev = ctx.dve("tensor_tensor", km[0:1, 0:1], km[0:1, 0:1],
                              km[0:1, 1:2], ALU.max)
            pb["r"] = (DV, tev)
        ctx.dve("tensor_scalar", km[0:1, 1:2], km[0:1, 0:1], 1e-38, None, ALU.add)
        tkmf = ctx.fence_dv()
        tl1 = ctx.act(km[0:1, 2:3], km[0:1, 1:2], AF.Ln, wait=[(DV, tkmf)])
        tk2 = ctx.act(km[0:1, 3:4], km[0:1, 2:3], AF.Exp, scale=0.5)
        tBg = None
        for tc in range(4):
            cols = slice(tc * 512, (tc + 1) * 512)
            tsq = ctx.act(sqq[:], qT[0:64, cols], AF.Square, wait=[(DV, tqf)])
            pb, wl = pool.acquire()
            tmm = ctx.mm(pb["ap"][0:1, :], c.ones_col[0:64, :], sqq[:], True, True,
                         wait=wl + [(AC, tsq)])
            td = ctx.dve("tensor_scalar", rs_s[0:1, 0:512], pb["ap"][0:1, :],
                         1e-38, None, ALU.add, wait=[(PE, tmm)])
            tdf = ctx.fence_dv()
            ta = ctx.act(rs_s[0:1, 512:1024], rs_s[0:1, 0:512], AF.Ln,
                         wait=[(DV, tdf)])
            ta = ctx.act(rs_s[0:1, 0:512], rs_s[0:1, 512:1024], AF.Exp, scale=0.5)
            ta = ctx.fence_ac()
            wB = [(AC, ta)]
            if tBg is not None:
                wB.append((GD, tBg))
            tB = ctx.dve("tensor_scalar", brow_bf[:], rs_s[0:1, 0:512],
                         km[0:1, 3:4], -1.0, ALU.mult, ALU.mult, wait=wB)
            tB = ctx.fence_dv()
            tBg = ctx.dma("gpsimd", qT[64:65, cols], brow_bf[:], wait=[(DV, tB)])
            pb["r"] = (DV, tB)
        kq_dv[0] = ctx.fence_dv()

        ctx.stage = "Bs%d" % h
        qbase = {}
        qofs = 0
        for (lo, hi) in ranges:
            qbase[lo // 1024] = qofs
            if hi - lo == 2048:
                qbase[lo // 1024 + 1] = qofs + 1024
            qofs += hi - lo
        for (tiles, _own) in halves:
            t0g = tiles[0]
            lastkt = tiles[-1]
            qb = qbase[t0g // 8]
            chunk_lastkt = [min(lastkt, t0g + (qc * 512 + 511) // 128)
                            for qc in range(2)]
            for kt in range(0, lastkt + 1):
                base = max(0, (kt - t0g) * 128)
                slot = kt % 2
                pt = ptb[slot]
                wz = [pv_done[slot]] if pv_done[slot] is not None else []
                tz = None
                if base > 0:
                    z0c = max(0, base - 256)
                    tz = ctx.dve("memset", pt[:, z0c:base], 0.0, wait=wz or None)
                exp_toks = []
                for qc in range(2):
                    if kt > chunk_lastkt[qc]:
                        continue
                    lo2 = max(base, qc * 512)
                    hi2 = (qc + 1) * 512
                    if lo2 >= hi2:
                        continue
                    pb, wl = pool.acquire()
                    w0 = wl + [(DV, kq_dv[0]), (GD, tBg)]
                    tmm = ctx.mm(pb["ap"][:, 0:hi2 - lo2], kT[:, kt * 128:(kt + 1) * 128],
                                 qT[:, qb + lo2: qb + hi2], True, True, wait=w0)
                    wexp = [(PE, tmm)]
                    if kt in tiles and lo2 == base:
                        tmsk = ctx.dve("tensor_tensor", pb["ap"][:, 0:128],
                                       pb["ap"][:, 0:128], c.mask_neg[:], ALU.add,
                                       wait=[(PE, tmm)])
                        wexp.append((DV, tmsk))
                    if tz is not None:
                        wexp.append((DV, tz))
                    elif wz:
                        wexp += wz
                    texp = ctx.act(pt[:, lo2:hi2], pb["ap"][:, 0:hi2 - lo2], AF.Exp,
                                   scale=SCALE, wait=wexp)
                    pb["r"] = (AC, texp)
                    exp_toks.append(texp)
                wpv = [(AC, tk) for tk in exp_toks]
                if kt == 0:
                    wpv = wpv + psO_rd[0] + [(AC, v_ac[0])]
                lastpv = None
                for qc in range(2):
                    if kt > chunk_lastkt[qc]:
                        continue
                    cols = slice(qc * 512, (qc + 1) * 512)
                    st_, sp_ = (kt == 0), (kt == chunk_lastkt[qc])
                    lastpv = ctx.mm(psO1[0:65, cols], vtm[:, kt * 65: kt * 65 + 65],
                                    pt[:, cols], st_, sp_,
                                    wait=wpv if qc == 0 else None)
                pv_done[slot] = (PE, lastpv)
            t_rs = ctx.dve("tensor_copy", rs_s[64:65, 0:1024], psO1[64:65, :],
                           wait=[(PE, PE.val)])
            to1 = ctx.act(obf[:], psO1[0:64, :], AF.Copy, wait=[(PE, PE.val)])
            to1 = ctx.fence_ac()
            psO_rd[0] = [(AC, to1), (DV, t_rs)]
            prev_ta2 = None
            for ti in range(8):
                pb, wl = pool.acquire()
                tmmr = ctx.mm(pb["ap"][:, 0:1], rs_s[64:65, ti * 128:(ti + 1) * 128],
                              c.ones11f[64:65, :], True, True, wait=wl + [(DV, t_rs)])
                wtd = [(PE, tmmr)]
                if prev_ta2 is not None:
                    wtd.append((AC, prev_ta2))
                td1 = ctx.dve("tensor_scalar", rntmp[:, 1:2], pb["ap"][:, 0:1],
                              1e-38, None, ALU.add, wait=wtd)
                td1f = ctx.fence_dv()
                ta1 = ctx.act(rntmp[:, 0:1], rntmp[:, 1:2], AF.Ln, wait=[(DV, td1f)])
                ta2 = ctx.act(rncols[:, ti:ti + 1], rntmp[:, 0:1], AF.Exp, scale=-1.0)
                ta2 = ctx.fence_ac()
                prev_ta2 = ta2
                pb["r"] = (AC, ta2)
            for ti in range(8):
                g = tiles[ti]
                oi = towns.index(g)
                for qc in range(2):
                    pb, wl = pool.acquire()
                    tmm = ctx.mm(pb["ap"][:], obf[:, ti * 128:(ti + 1) * 128],
                                 wout[:, qc * 512:(qc + 1) * 512], True, True,
                                 wait=wl + [(AC, AC.val), (DV, lastc2)])
                    tev = ctx.dve("scalar_tensor_tensor",
                                  r[:, oi * DM + qc * 512: oi * DM + (qc + 1) * 512],
                                  pb["ap"][:], rncols[:, ti:ti + 1],
                                  r[:, oi * DM + qc * 512: oi * DM + (qc + 1) * 512],
                                  op0=ALU.mult, op1=ALU.add, wait=[(PE, tmm)])
                    pb["r"] = (DV, tev)
        wbuf_rd = (PE, PE.val)
        ggrow_rd[0] = (PE, PE.val)
    tdvall = ctx.fence_dv()
    for oi in range(16):
        for qc in range(2):
            pb, wl = pool.acquire()
            tmm = ctx.mm(pb["ap"][:], c.ones_rowF[:, 0:128],
                         obrow[0:1, qc * 512:(qc + 1) * 512], True, True,
                         wait=wl + [(LD, tob), (DV, tdvall)])
            tadd = ctx.dve("tensor_tensor",
                           r[:, oi * DM + qc * 512: oi * DM + (qc + 1) * 512],
                           r[:, oi * DM + qc * 512: oi * DM + (qc + 1) * 512],
                           pb["ap"][:], ALU.add, wait=[(PE, tmm)])
            pb["r"] = (DV, tadd)
    ctx.barrier()


def emit_wprep(ctx, c, P):
    LD, PE, AC = (ctx.sems[k] for k in ("LD", "PE", "AC"))
    LN = [ctx.sems[k] for k in ("LD", "L1", "L2", "L3")]
    GN = [ctx.sems[k] for k in ("GD", "G1")]
    pool = P["pool"]
    st_in = [ctx.sbuf("wp_in%d" % i, [128, 2048], F32) for i in range(2)]
    st_out = [ctx.sbuf("wp_out%d" % i, [128, 2048], BF16) for i in range(2)]
    rd_in = [[None] * 4, [None] * 4]
    wr_out = [None, None]
    rd_out = [[None, None], [None, None]]
    sts = []
    for (wname, dname, K, NO) in (("f_w1", "w1t", DM, 4096), ("f_w2", "w2t", DM, 4096),
                                  ("f_h1", "h1t", 4096, 4096), ("f_h2", "h2t", 4096, 4096),
                                  ("f_w3", "w3t", 4096, DM)):
        for so in range(NO // 512):
            for si in range(K // 512):
                sts.append((P[wname], P[dname], K, NO, so, si))
    tins = {}
    for sti, (w, scr, K, NO, so, si) in enumerate(sts):
        bi = sti % 2
        for a in range(4):
            wl = [rd_in[bi][a]] if rd_in[bi][a] is not None else None
            tins[(sti, a)] = ctx.dma("sync", st_in[bi][:, a * 512:(a + 1) * 512],
                                     w[so * 512 + a * 128: so * 512 + (a + 1) * 128,
                                       si * 512:(si + 1) * 512],
                                     wait=wl, lane=a)
        first_ev = True
        for a in range(4):
            for bb in range(4):
                pt, wlt = pool.acquire()
                wmt = wlt + [(LN[a], tins[(sti, a)])]
                tt = ctx.trans(pt["ap"][0:128, 0:128],
                               st_in[bi][:, a * 512 + bb * 128: a * 512 + (bb + 1) * 128],
                               c.identF[:], wait=wmt)
                if bb == 3:
                    rd_in[bi][a] = (PE, tt)
                wev = [(PE, tt)]
                if first_ev:
                    for g2 in range(2):
                        if rd_out[bi][g2] is not None:
                            wev.append(rd_out[bi][g2])
                    first_ev = False
                lastev = ctx.act(st_out[bi][:, bb * 512 + a * 128: bb * 512 + (a + 1) * 128],
                                 pt["ap"][0:128, 0:128], AF.Copy,
                                 wait=wev)
                pt["r"] = (AC, lastev)
        wr_out[bi] = lastev
        for oq in range(4):
            ov = st_out[bi][:].rearrange("p (b w) -> p b w", w=512)[:, :, oq * 128:(oq + 1) * 128]
            dst = scr[so * 4 + oq, si * 512:(si + 1) * 512, :].rearrange(
                "(b p) o -> p b o", p=128)
            g2 = oq % 2
            tw = ctx.dma("gpsimd", dst, ov,
                         wait=[(AC, lastev)] if oq < 2 else None, lane=g2)
            rd_out[bi][g2] = (GN[g2], tw)
    ctx.barrier()


def emit_l1_ffn(ctx, c, P):
    LD, SD, GD, PE, DV, AC = (ctx.sems[k] for k in
                              ("LD", "SD", "GD", "PE", "DV", "AC"))
    z = P["z"]
    towns = own_tiles(z)
    pool = P["pool"]
    rdram = P["rdram"]
    out = P["out"]

    pool.n_rot = 4
    ln4blk = ctx.sbuf("f_ln4", [128, NCH * 512], BF16)
    g_fm = ctx.sbuf("f_g", [128, 32 * 512], BF16)
    h_fm = ctx.sbuf("f_h", [128, 32 * 512], BF16)
    slab = [ctx.sbuf("f_sl%d" % i, [128, 32 * 128], BF16) for i in range(4)]
    scol = ctx.sbuf("f_scol", [128, 32 * 5], F32)
    srow = ctx.sbuf("f_srow", [1, 4096], F32)
    ev1 = ctx.sbuf("f_ev1", [128, 512], F32)
    outst = ctx.sbuf("f_outst", [128, 4 * 1024], F32)
    rst = ctx.sbuf("f_rst", [128, 1024], F32)

    tprev = None
    for si, nm in enumerate(("f_w1s", "f_w2s", "f_h1s", "f_h2s", "f_w3s")):
        n = 4096 if si < 4 else DM
        tl = ctx.dma("sync", srow[0:1, 0:n], P[nm])
        for ccb in range(n // 128):
            pt, wlt = pool.acquire()
            tt = ctx.trans(pt["ap"][0:128, 0:1],
                           srow[0:1, ccb * 128:(ccb + 1) * 128],
                           c.ones11f[0:1, 0:1], wait=wlt + [(LD, tl)])
            tprev = ctx.act(scol[:, si * 32 + ccb: si * 32 + ccb + 1],
                            pt["ap"][0:128, 0:1], AF.Copy, wait=[(PE, tt)])
            pt["r"] = (AC, tprev)
    tscf = ctx.fence_ac()

    slab_rd = {}
    LSEM = {0: LD, 1: ctx.sems["L1"], 2: SD, 3: ctx.sems["S1"]}

    def slab_guard(*keys):
        vs = [slab_rd[k][1] for k in keys if k in slab_rd]
        return [(PE, max(vs))] if vs else []
    nblk = 4
    ln4_tok = None
    for blk in range(nblk):
        ctx.stage = "Fb%d" % blk
        tb0 = blk * 512
        tl4 = None
        for cc in range(NCH):
            wl4 = [ln4_tok] if (ln4_tok is not None and cc == 0) else None
            tl4 = ctx.dma("gpsimd", ln4blk[:, cc * 512:(cc + 1) * 512],
                          P["ln4_dram"][cc, :, tb0:tb0 + 512], wait=wl4)
        for og in range(16):
            pp = og % 2
            co = pp * 1024
            toks = []
            for j, wnm in ((0, "w1t"), (1, "w1t"), (2, "w2t"), (3, "w2t")):
                oc = og * 2 + (j % 2)
                wls = slab_guard((j, pp), ("f", j))
                toks.append(ctx.dma(
                    "sync" if j < 2 else "scalar",
                    slab[j][:, co:co + 1024].rearrange("p (cc o) -> p cc o", o=128),
                    P[wnm][oc].rearrange("(cc p) o -> p cc o", p=128),
                    wait=wls or None, lane=j % 2))
            pss = []
            for j in range(4):
                ps, wlp = pool.acquire()
                for cc in range(NCH):
                    w0 = (wlp + [(LSEM[j], toks[j]), (AC, tscf), (GD, tl4)]) \
                        if cc == 0 else None
                    tmm = ctx.mm(ps["ap"][:], slab[j][:, co + cc * 128: co + (cc + 1) * 128],
                                 ln4blk[:, cc * 512:(cc + 1) * 512],
                                 cc == 0, cc == NCH - 1, wait=w0)
                slab_rd[(j, pp)] = (PE, tmm)
                pss.append((ps, tmm))
            prev_t3 = None
            for jj in range(2):
                oc = og * 2 + jj
                psa, ta = pss[jj]
                psb, tb = pss[2 + jj]
                w1w = [(PE, tb)]
                if prev_t3 is not None:
                    w1w.append((DV, prev_t3))
                t1 = ctx.act(ev1[:], psa["ap"][:], AF.Silu,
                             scale=scol[:, 0 * 32 + oc:0 * 32 + oc + 1],
                             wait=w1w)
                t1f = ctx.fence_ac()
                psa["r"] = (AC, t1f)
                t2 = ctx.dve("tensor_scalar", g_fm[:, oc * 512:(oc + 1) * 512],
                             psb["ap"][:], scol[:, 1 * 32 + oc:1 * 32 + oc + 1],
                             None, ALU.mult, wait=[(AC, t1f)])
                t3 = ctx.dve("tensor_tensor", g_fm[:, oc * 512:(oc + 1) * 512],
                             g_fm[:, oc * 512:(oc + 1) * 512], ev1[:], ALU.mult)
                prev_t3 = t3
                psb["r"] = (DV, t3)
        tgf = ctx.fence_dv()
        if P.get("dbg_ffn") is not None and blk == 0:
            ctx.dma("gpsimd", P["dbg_ffn"]["g"], g_fm[:], wait=[(DV, tgf)])
            ctx.barrier()
        for (wnm, sidx, src, dst) in (("h1t", 2, g_fm, h_fm), ("h2t", 3, h_fm, g_fm)):
            for og in range(16):
                pp = og % 2
                toks = []
                for j in range(2):
                    b = j + 2 * pp
                    oc = og * 2 + j
                    wls = slab_guard(("f", b), (b, 0), (b, 1))
                    toks.append(ctx.dma(
                        "sync" if j == 0 else "scalar",
                        slab[b][:].rearrange("p (cc o) -> p cc o", o=128),
                        P[wnm][oc].rearrange("(cc p) o -> p cc o", p=128),
                        wait=wls or None, lane=pp))
                for j in range(2):
                    b = j + 2 * pp
                    oc = og * 2 + j
                    ps, wlp = pool.acquire()
                    sem = LSEM[pp] if j == 0 else LSEM[2 + pp]
                    for cc in range(32):
                        w0 = (wlp + [(sem, toks[j]), (DV, tgf)]) if cc == 0 else None
                        tmm = ctx.mm(ps["ap"][:], slab[b][:, cc * 128:(cc + 1) * 128],
                                     src[:, cc * 512:(cc + 1) * 512],
                                     cc == 0, cc == 31, wait=w0)
                    slab_rd[("f", b)] = (PE, tmm)
                    tg = ctx.act(dst[:, oc * 512:(oc + 1) * 512], ps["ap"][:],
                                 AF.Gelu, scale=scol[:, sidx * 32 + oc:sidx * 32 + oc + 1],
                                 wait=[(PE, tmm)])
                    tgf2 = ctx.fence_ac()
                    ps["r"] = (AC, tgf2)
            tgf = ctx.fence_ac()
            if P.get("dbg_ffn") is not None and blk == 0:
                ctx.dma("gpsimd", P["dbg_ffn"]["h" if wnm == "h1t" else "h2"],
                        dst[:], wait=[(AC, tgf)])
                ctx.barrier()
        for og in range(4):
            pp = og % 2
            toks = []
            for j in range(2):
                b = j + 2 * pp
                oc = og * 2 + j
                wls = slab_guard(("f", b), (b, 0), (b, 1))
                toks.append(ctx.dma(
                    "sync" if j == 0 else "scalar",
                    slab[b][:].rearrange("p (cc o) -> p cc o", o=128),
                    P["w3t"][oc].rearrange("(cc p) o -> p cc o", p=128),
                    wait=wls or None, lane=pp))
            for j in range(2):
                b = j + 2 * pp
                oc = og * 2 + j
                ps, wlp = pool.acquire()
                sem = LSEM[pp] if j == 0 else LSEM[2 + pp]
                for cc in range(32):
                    w0 = (wlp + [(sem, toks[j]), (AC, tgf)]) if cc == 0 else None
                    tmm = ctx.mm(ps["ap"][:], slab[b][:, cc * 128:(cc + 1) * 128],
                                 g_fm[:, cc * 512:(cc + 1) * 512],
                                 cc == 0, cc == 31, wait=w0)
                slab_rd[("f", b)] = (PE, tmm)
                tw3 = ctx.act(ev1[:], ps["ap"][:], AF.Copy,
                              scale=scol[:, 4 * 32 + oc:4 * 32 + oc + 1],
                              wait=[(PE, tmm)])
                tw3f = ctx.fence_ac()
                ps["r"] = (AC, tw3f)
                for j2 in range(4):
                    pt, wlt = pool.acquire()
                    tt = ctx.trans(pt["ap"][0:128, 0:128],
                                   ev1[:, j2 * 128:(j2 + 1) * 128],
                                   c.identF[:], wait=wlt + [(AC, tw3f)])
                    tev = ctx.act(outst[:, j2 * 1024 + oc * 128: j2 * 1024 + (oc + 1) * 128],
                                  pt["ap"][0:128, 0:128], AF.Copy, wait=[(PE, tt)])
                    pt["r"] = (AC, tev)
        tof = ctx.fence_ac()
        if P.get("dbg_ffn") is not None and blk == 0:
            ctx.dma("gpsimd", P["dbg_ffn"]["o"], outst[:], wait=[(AC, tof)])
            ctx.barrier()
        tlast_out = None
        tadf = None
        for j2 in range(4):
            trl = ctx.dma("sync", rst[:], rdram[(tb0 + j2 * 128):(tb0 + (j2 + 1) * 128), :],
                          wait=[(DV, tadf)] if tadf is not None else None)
            tadd = ctx.dve("tensor_tensor", outst[:, j2 * 1024:(j2 + 1) * 1024],
                           outst[:, j2 * 1024:(j2 + 1) * 1024], rst[:], ALU.add,
                           wait=[(LD, trl), (AC, tof)])
            tadf = ctx.fence_dv()
            tlast_out = ctx.dma("gpsimd", out[(tb0 + j2 * 128):(tb0 + (j2 + 1) * 128), :],
                                outst[:, j2 * 1024:(j2 + 1) * 1024], wait=[(DV, tadf)])
        ctx.emit("tensor", "wait_ge", GD.h, tlast_out)
        ln4_tok = (PE, ctx.sems["PE"].val)
        tgf = ctx.fence_dv()
    ctx.barrier()


def emit_ln4(ctx, c, P):
    LD, GD, PE, DV, AC = (ctx.sems[k] for k in ("LD", "GD", "PE", "DV", "AC"))
    r, pool = P["r"], P["pool"]
    with ctx.scope():
        fmt = ctx.sbuf("l4_fmt", [128, DM], BF16)
        tmp = {"sq": ctx.sbuf("l4_sq", [128, DM], F32),
               "st": ctx.sbuf("l4_st", [128, 16], F32),
               "lnf": ctx.sbuf("l4_f", [128, DM], F32),
               "psT": [pool.t[2], pool.t[3]],
               "psT_tok": [None, None], "act_prev": None, "lnf_rd": None}
        fm_tok = None
        for oi in range(16):
            fmw = [fmt[:, cc * 128:(cc + 1) * 128] for cc in range(NCH)]
            if fm_tok is not None:
                tmp["lnf_rd"] = tmp["lnf_rd"]
            te, ntok, tv = emit_ln_tile2(ctx, c, r[:, oi * DM:(oi + 1) * DM],
                                         fmw, tmp,
                                         wait=[fm_tok] if fm_tok else None)
            tfa = ctx.fence_ac()
            for cc in range(NCH):
                tW = ctx.dma("gpsimd",
                             P["ln4_dram"][cc, :, oi * 128:(oi + 1) * 128],
                             fmt[:, cc * 128:(cc + 1) * 128],
                             wait=[(AC, tfa)] if cc == 0 else None)
            fm_tok = (GD, tW)
            ctx.dma("sync", P["rdram"][oi * 128:(oi + 1) * 128, :],
                    r[:, oi * DM:(oi + 1) * DM])
    ctx.barrier()


def emit_part2_body(ctx, c, P):
    emit_l1_prep(ctx, c, P)
    emit_l1_attn(ctx, c, P)


def build_part2(z, debug=()):
    nc = bass.Bass("TRN2", target_bir_lowering=False)
    P = declare_l1_inputs(nc)
    P["z"] = z
    P["x"] = nc.dram_tensor("x2", [T, DM], F32, kind="ExternalInput").ap()
    P["out"] = nc.dram_tensor("outp", [16 * 128, DM], F32, kind="ExternalOutput").ap()
    P["rdram"] = nc.dram_tensor("rdram", [2048, DM], F32, kind="Internal").ap()
    P["w1t"] = nc.dram_tensor("w1t", [32, DM, 128], BF16, kind="Internal").ap()
    P["w2t"] = nc.dram_tensor("w2t", [32, DM, 128], BF16, kind="Internal").ap()
    P["h1t"] = nc.dram_tensor("h1t", [32, 4096, 128], BF16, kind="Internal").ap()
    P["h2t"] = nc.dram_tensor("h2t", [32, 4096, 128], BF16, kind="Internal").ap()
    P["w3t"] = nc.dram_tensor("w3t", [8, 4096, 128], BF16, kind="Internal").ap()
    with contextlib.ExitStack() as stack:
        ctx = Ctx(nc, stack)
        pool = PsumPool([ctx.psum("ps%d" % i, [128, 512], F32) for i in range(4)])
        P["pool"] = pool
        P["psO1"] = ctx.psum("psO1", [65, 1024], F32)
        P["psO2"] = ctx.psum("psO2", [128, 1024], F32)
        c = build_common(ctx)
        P["ln4_dram"] = nc.dram_tensor("ln4d", [NCH, 128, 2048], BF16,
                                       kind="Internal").ap()
        if "ffn" in debug:
            P["dbg_ffn"] = {
                "g": nc.dram_tensor("d_g", [128, 32 * 512], BF16, kind="ExternalOutput").ap(),
                "h": nc.dram_tensor("d_h", [128, 32 * 512], BF16, kind="ExternalOutput").ap(),
                "h2": nc.dram_tensor("d_h2", [128, 32 * 512], BF16, kind="ExternalOutput").ap(),
                "o": nc.dram_tensor("d_o", [128, 4096], F32, kind="ExternalOutput").ap(),
            }
        with ctx.scope():
            P["gg"] = ctx.sbuf("gg", [128, NTILE * 16], F32)
            P["ggfm"] = ctx.sbuf("ggfm", [16, T], BF16)
            P["r"] = ctx.sbuf("r", [128, 16 * DM], F32)
            P["lnfm"] = ctx.sbuf("lnfm", [128, NCH * FMW], BF16)
            ctx.warm = ([P["r"][:, i * DM:(i + 1) * DM] for i in range(3)],
                        P["x"][0:128, :])
            towns = own_tiles(z)
            with ctx.scope():
                xt = [ctx.sbuf("xt0", [128, DM], F32), ctx.sbuf("xt1", [128, DM], F32)]
                xt_tok = [None, None]
                tmp = {"sq": ctx.sbuf("ln_sq", [128, DM], F32),
                       "st": ctx.sbuf("ln_st", [128, 16], F32),
                       "lnf": ctx.sbuf("ln_f", [128, DM], F32),
                       "psT": [pool.t[2], pool.t[3]],
                       "psT_tok": [None, None], "act_prev": None, "lnf_rd": None}
                lnfm_r = P["lnfm"][:].rearrange("p (c w) -> p c w", w=FMW)
                ctx.dve("memset", lnfm_r[:, :, 0:1], 0.0)
                LD = ctx.sems["LD"]
                tdum = ctx.dma("sync", xt[0][:], P["x"][0:128, :])
                xt_tok[0] = [(LD, tdum)]
                for g in range(NTILE):
                    slot = g % 2
                    wl = xt_tok[slot] if xt_tok[slot] is not None else []
                    tl = ctx.dma("sync", xt[slot][:], P["x"][g * 128:(g + 1) * 128, :],
                                 wait=wl or None)
                    if g in towns:
                        oi = towns.index(g)
                        ctx.dma("sync", P["r"][:, oi * DM:(oi + 1) * DM],
                                P["x"][g * 128:(g + 1) * 128, :])
                    fmw = [P["lnfm"][:, fm_col(cc, g * 128): fm_col(cc, g * 128) + 128]
                           for cc in range(NCH)]
                    te, ntok, tv = emit_ln_tile2(ctx, c, xt[slot][:], fmw, tmp,
                                                 wait=[(LD, tl)])
                    xt_tok[slot] = [ntok]
            ctx.barrier()
            with ctx.scope():
                emit_l1_prep(ctx, c, P)
            with ctx.scope():
                emit_l1_attn(ctx, c, P)
            emit_ln4(ctx, c, P)
        with ctx.scope():
            emit_wprep(ctx, c, P)
        with ctx.scope():
            emit_l1_ffn(ctx, c, P)
        GD = ctx.sems["GD"]
        ctx.emit("gpsimd", "wait_ge", GD.h, GD.val)
        ctx.finalize()
    return nc


def _shard_inputs_p1(inp):
    base = {
        "w_aq": inp["L0_attn_w_aq"], "w_ak": inp["L0_attn_w_ak"],
        "w_av": inp["L0_attn_w_av"], "w_ao": inp["L0_attn_w_ao"],
        "ffn_kernel": inp["L0_ffn_kernel"], "ffn_bias": inp["L0_ffn_bias"],
        "ffn_a": inp["L0_ffn_a"], "ffn_b": inp["L0_ffn_b"],
    }
    maps = []
    for core in range(8):
        m = dict(base)
        m["x"] = np.ascontiguousarray(inp["L0_attn_x"][core // 2])
        maps.append(m)
    return maps


def _shard_inputs_p2(inp, x2):
    base = {
        "qkv_w": inp["L1_attn_qkv_w"], "qkv_b": inp["L1_attn_qkv_b"],
        "w1_write": inp["L1_attn_w1_write"], "w2_write": inp["L1_attn_w2_write"],
        "w1_read": inp["L1_attn_w1_read"], "w2_read": inp["L1_attn_w2_read"],
        "kv_geo_w": inp["L1_attn_kv_geo_w"], "out_w": inp["L1_attn_out_w"],
        "out_b": inp["L1_attn_out_b"],
        "kv_gate": np.ascontiguousarray(inp["L1_attn_kv_gate"]).reshape(1, 1),
        "mem_scale": np.ascontiguousarray(inp["L1_attn_mem_scale"]).reshape(1, -1),
        "decay_logits": np.ascontiguousarray(inp["L1_attn_decay_logits"]).reshape(1, -1),
        "f_w1": inp["L1_ffn_w1_w"], "f_w1s": inp["L1_ffn_w1_s"].reshape(1, -1),
        "f_w2": inp["L1_ffn_w2_w"], "f_w2s": inp["L1_ffn_w2_s"].reshape(1, -1),
        "f_h1": inp["L1_ffn_h1_w"], "f_h1s": inp["L1_ffn_h1_s"].reshape(1, -1),
        "f_h2": inp["L1_ffn_h2_w"], "f_h2s": inp["L1_ffn_h2_s"].reshape(1, -1),
        "f_w3": inp["L1_ffn_w3_w"], "f_w3s": inp["L1_ffn_w3_s"].reshape(1, -1),
    }
    maps = []
    for core in range(8):
        m = dict(base)
        m["x2"] = np.ascontiguousarray(x2[core // 2])
        maps.append(m)
    return maps


_NC_CACHE = {}


def kernel(**inputs):
    inp = {k: np.ascontiguousarray(np.asarray(v)) for k, v in inputs.items()}
    x2 = np.zeros((NB, T, DM), np.float32)
    outs = np.zeros((NB, T, DM), np.float32)
    p1maps = _shard_inputs_p1(inp)
    for z in (0, 1):
        if ("p1", z) not in _NC_CACHE:
            _NC_CACHE[("p1", z)] = build_part1(z, fused=True,
                                               stages=("s1", "s2", "s3"))
        res = run_bass_kernel_spmd(_NC_CACHE[("p1", z)], p1maps,
                                   core_ids=list(range(8)))
        towns = own_tiles(z)
        for b in range(NB):
            rr = res.results[2 * b + z]["out_r"]
            for i, g in enumerate(towns):
                x2[b, g * 128:(g + 1) * 128] = rr[i * 128:(i + 1) * 128]
    p2maps = _shard_inputs_p2(inp, x2)
    for z in (0, 1):
        if ("p2", z) not in _NC_CACHE:
            _NC_CACHE[("p2", z)] = build_part2(z)
        res = run_bass_kernel_spmd(_NC_CACHE[("p2", z)], p2maps,
                                   core_ids=list(range(8)))
        towns = own_tiles(z)
        for b in range(NB):
            rr = res.results[2 * b + z]["outp"]
            for i, g in enumerate(towns):
                outs[b, g * 128:(g + 1) * 128] = rr[i * 128:(i + 1) * 128]
    return outs


def build_fused(z):
    nc = bass.Bass("TRN2", target_bir_lowering=False)
    P = declare_l0_inputs(nc)
    P.update(declare_l1_inputs(nc))
    P["z"] = z
    P["out"] = nc.dram_tensor("outp", [16 * 128, DM], F32, kind="ExternalOutput").ap()
    P["xmid"] = nc.dram_tensor("xmid", [T, DM], F32, kind="Internal").ap()
    P["rdram"] = nc.dram_tensor("rdram", [2048, DM], F32, kind="Internal").ap()
    P["w1t"] = nc.dram_tensor("w1t", [32, DM, 128], BF16, kind="Internal").ap()
    P["w2t"] = nc.dram_tensor("w2t", [32, DM, 128], BF16, kind="Internal").ap()
    P["h1t"] = nc.dram_tensor("h1t", [32, 4096, 128], BF16, kind="Internal").ap()
    P["h2t"] = nc.dram_tensor("h2t", [32, 4096, 128], BF16, kind="Internal").ap()
    P["w3t"] = nc.dram_tensor("w3t", [8, 4096, 128], BF16, kind="Internal").ap()
    P["ln4_dram"] = nc.dram_tensor("ln4d", [NCH, 128, 2048], BF16,
                                   kind="Internal").ap()
    with contextlib.ExitStack() as stack:
        ctx = Ctx(nc, stack)
        pool = PsumPool([ctx.psum("ps%d" % i, [128, 512], F32) for i in range(4)])
        P["pool"] = pool
        P["psO1"] = ctx.psum("psO1", [65, 1024], F32)
        P["psO2"] = ctx.psum("psO2", [128, 1024], F32)
        c = build_common(ctx)
        with ctx.scope():
            P["r"] = ctx.sbuf("r", [128, 16 * DM], F32)
            P["lnfm"] = ctx.sbuf("lnfm", [128, NCH * FMW], BF16)
            ctx.warm = ([P["r"][:, i * DM:(i + 1) * DM] for i in range(3)],
                        P["x"][0:128, :])
            with ctx.scope():
                emit_part1(ctx, c, P, fused=True)
            with ctx.scope():
                emit_l0_ffn(ctx, c, P, fused=True)
            P["gg"] = ctx.sbuf("gg", [128, NTILE * 16], F32)
            P["ggfm"] = ctx.sbuf("ggfm", [16, T], BF16)
            with ctx.scope():
                emit_l1_prep(ctx, c, P)
            with ctx.scope():
                emit_l1_attn(ctx, c, P)
            emit_ln4(ctx, c, P)
        with ctx.scope():
            emit_wprep(ctx, c, P)
        with ctx.scope():
            emit_l1_ffn(ctx, c, P)
        GD = ctx.sems["GD"]
        ctx.emit("gpsimd", "wait_ge", GD.h, GD.val)
        ctx.finalize()
    return nc


def _shard_inputs_fused(inp):
    m1 = _shard_inputs_p1(inp)
    m2 = _shard_inputs_p2(inp, np.zeros((NB, 1, 1), np.float32))
    maps = []
    for core in range(8):
        m = dict(m1[core])
        m.update({k: v for k, v in m2[core].items() if k != "x2"})
        maps.append(m)
    return maps


def kernel(**inputs):
    inp = {k: np.ascontiguousarray(np.asarray(v)) for k, v in inputs.items()}
    outs = np.zeros((NB, T, DM), np.float32)
    maps = _shard_inputs_fused(inp)
    for z in (0, 1):
        if ("f", z) not in _NC_CACHE:
            _NC_CACHE[("f", z)] = build_fused(z)
        res = run_bass_kernel_spmd(_NC_CACHE[("f", z)], maps,
                                   core_ids=list(range(8)))
        towns = own_tiles(z)
        for b in range(NB):
            rr = res.results[2 * b + z]["outp"]
            for i, g in enumerate(towns):
                outs[b, g * 128:(g + 1) * 128] = rr[i * 128:(i + 1) * 128]
    return outs

def build_onelaunch():
    nc = bass.Bass("TRN2", target_bir_lowering=False)
    PB = declare_l0_inputs(nc)
    PB.update(declare_l1_inputs(nc))
    zfl = nc.dram_tensor("zflag", [1, 1], I32, kind="ExternalInput").ap()
    PB["out"] = nc.dram_tensor("outp", [16 * 128, DM], F32, kind="ExternalOutput").ap()
    PB["xmid"] = nc.dram_tensor("xmid", [T, DM], F32, kind="Internal").ap()
    PB["rdram"] = nc.dram_tensor("rdram", [2048, DM], F32, kind="Internal").ap()
    PB["w1t"] = nc.dram_tensor("w1t", [32, DM, 128], BF16, kind="Internal").ap()
    PB["w2t"] = nc.dram_tensor("w2t", [32, DM, 128], BF16, kind="Internal").ap()
    PB["h1t"] = nc.dram_tensor("h1t", [32, 4096, 128], BF16, kind="Internal").ap()
    PB["h2t"] = nc.dram_tensor("h2t", [32, 4096, 128], BF16, kind="Internal").ap()
    PB["w3t"] = nc.dram_tensor("w3t", [8, 4096, 128], BF16, kind="Internal").ap()
    PB["ln4_dram"] = nc.dram_tensor("ln4d", [NCH, 128, 2048], BF16,
                                    kind="Internal").ap()
    with contextlib.ExitStack() as stack:
        ctx = Ctx(nc, stack)
        zfs = ctx.sbuf("zfs", [1, 1], I32)
        qs = []
        warm = None
        for z in (0, 1):
            ctx.sfx = "_z%d" % z
            ctx.q = {e: [] for e in ENGINES}
            for s in ctx.sems.values():
                s.val = 0
            ctx.ps_w = {}
            ctx.ps_r = {}
            P = dict(PB)
            P["z"] = z
            with ctx.scope():
                pool = PsumPool([ctx.psum("ps%d" % i, [128, 512], F32)
                                 for i in range(4)])
                P["pool"] = pool
                P["psO1"] = ctx.psum("psO1", [65, 1024], F32)
                P["psO2"] = ctx.psum("psO2", [128, 1024], F32)
                c = build_common(ctx)
                with ctx.scope():
                    P["r"] = ctx.sbuf("r", [128, 16 * DM], F32)
                    P["lnfm"] = ctx.sbuf("lnfm", [128, NCH * FMW], BF16)
                    if z == 0:
                        warm = ([P["r"][:, i * DM:(i + 1) * DM] for i in range(3)],
                                P["x"][0:128, :])
                    ctx.stage = "l0attn"
                    with ctx.scope():
                        emit_part1(ctx, c, P, fused=True)
                    ctx.stage = "l0ffn"
                    with ctx.scope():
                        emit_l0_ffn(ctx, c, P, fused=True)
                    P["gg"] = ctx.sbuf("gg", [128, NTILE * 16], F32)
                    P["ggfm"] = ctx.sbuf("ggfm", [16, T], BF16)
                    ctx.stage = "l1prep"
                    with ctx.scope():
                        emit_l1_prep(ctx, c, P)
                    ctx.stage = "l1attn"
                    with ctx.scope():
                        emit_l1_attn(ctx, c, P)
                    ctx.stage = "ln4"
                    emit_ln4(ctx, c, P)
                ctx.stage = "wprep"
                with ctx.scope():
                    emit_wprep(ctx, c, P)
                ctx.stage = "l1ffn"
                with ctx.scope():
                    emit_l1_ffn(ctx, c, P)
                GD = ctx.sems["GD"]
                ctx.emit("gpsimd", "wait_ge", GD.h, GD.val)
            qs.append(ctx.q)
        ctx.warm = warm
        ctx.finalize_switch(qs, zfs, zfl)
    return nc


def kernel(**inputs):
    inp = {k: np.ascontiguousarray(np.asarray(v)) for k, v in inputs.items()}
    maps = _shard_inputs_fused(inp)
    for ci in range(8):
        maps[ci]["zflag"] = np.array([[ci % 2]], np.int32)
    if "one" not in _NC_CACHE:
        _NC_CACHE["one"] = build_onelaunch()
    res = run_bass_kernel_spmd(_NC_CACHE["one"], maps, core_ids=list(range(8)))
    outs = np.zeros((NB, T, DM), np.float32)
    for ci in range(8):
        b, z = ci // 2, ci % 2
        rr = res.results[ci]["outp"]
        for i, g in enumerate(own_tiles(z)):
            outs[b, g * 128:(g + 1) * 128] = rr[i * 128:(i + 1) * 128]
    return outs


# version 08
# speedup vs baseline: 1.2192x; 1.2192x over previous
import contextlib
import math
import numpy as np
import concourse.bass as bass
import concourse.mybir as mybir
from concourse.bass_utils import run_bass_kernel_spmd

F32 = mybir.dt.float32
BF16 = mybir.dt.bfloat16
I32 = mybir.dt.int32
AF = mybir.ActivationFunctionType
ALU = mybir.AluOpType
AX = mybir.AxisListType

T = 4096
DM = 1024
NB = 4
NTILE = 32
NCH = 8
FMW = 4097
L0H, L0D = 8, 128
L1H, L1D = 16, 64
PD = 4
NEG = -1.0e30
LN_EPS = 1e-5
ROPE_BASE = 10000.0
TWO_PI = 2.0 * math.pi
PIH = 6.28125
PIL = TWO_PI - PIH

ENGINES = ("sync", "scalar", "vector", "tensor", "gpsimd")


def own_tiles(z):
    return list(range(0, 8)) + list(range(24, 32)) if z == 0 else list(range(8, 24))


def q_halves(z):
    res = []
    for hf in range(4):
        tiles = list(range(hf * 8, hf * 8 + 8))
        is_own = (hf in (0, 3)) if z == 0 else (hf in (1, 2))
        res.append((tiles, is_own))
    return res


class Sem:
    __slots__ = ("h", "val")

    def __init__(self, h):
        self.h = h
        self.val = 0


class Ctx:
    def __init__(self, nc, stack):
        self.nc = nc
        self.stack = stack
        self.scopes = [stack]
        self.q = {e: [] for e in ENGINES}
        self.sems = {}
        for name in ("LD", "GD", "SD", "PE", "DV", "AC", "PO", "WW",
                     "L1", "L2", "L3", "G1", "S1"):
            self.sems[name] = Sem(stack.enter_context(nc.semaphore("s" + name)))
        self.lanes = {"sync": ("LD", "L1", "L2", "L3"),
                      "gpsimd": ("GD", "G1"),
                      "scalar": ("SD", "S1")}
        self.ps_w = {}
        self.ps_r = {}
        self.warm = None
        self.sfx = ""
        self.stage = "init"

    def sbuf(self, name, shape, dt):
        return self.scopes[-1].enter_context(
            self.nc.sbuf_tensor(name + self.sfx, shape, dt))

    def psum(self, name, shape, dt):
        return self.scopes[-1].enter_context(
            self.nc.psum_tensor(name + self.sfx, shape, dt))

    @contextlib.contextmanager
    def scope(self):
        st = contextlib.ExitStack()
        self.scopes.append(st)
        try:
            yield
        finally:
            st.close()
            self.scopes.pop()

    def emit(self, eng, fname, *args, wait=None, inc=None, **kw):
        tok = None
        if inc is not None:
            inc[0].val += inc[1]
            tok = inc[0].val
        sem_h = inc[0].h if inc is not None else None
        inc_n = inc[1] if inc is not None else 0
        waits = tuple((s.h, v) for (s, v) in wait) if wait else ()

        drain = eng in ("vector", "scalar", "gpsimd") and fname != "dma_start" \
            and sem_h is not None

        def f(e, fname=fname, args=args, kw=kw, waits=waits, sem_h=sem_h,
              inc_n=inc_n, drain=drain):
            for h, v in waits:
                e.wait_ge(h, v)
            ins = getattr(e, fname)(*args, **kw)
            if sem_h is not None:
                if drain:
                    e.drain().then_inc(sem_h, inc_n)
                else:
                    ins.then_inc(sem_h, inc_n)
        self.q[eng].append((self.stage, f))
        return tok

    def raw(self, eng, f):
        self.q[eng].append((self.stage, f))

    def barrier(self):
        toks = [(s, s.val) for s in self.sems.values() if s.val > 0]
        for e in ENGINES:
            def f(eng, toks=tuple((s.h, v) for s, v in toks)):
                for h, v in toks:
                    eng.wait_ge(h, v)
            self.q[e].append((self.stage, f))

    def finalize(self):
        sems = [s.h for s in self.sems.values()]
        nums = [h.num for h in sems]
        rng = range(min(nums), max(nums) + 1)
        with self.nc.Block() as b0:
            def clearer(g, sems=sems, rng=rng):
                for h in sems:
                    g.sem_clear(h)
            b0.gpsimd(clearer)
        if self.warm is not None:
            dsts, src = self.warm
            ww = self.sems["WW"].h
            with self.nc.Block() as bw:
                def mk(dst):
                    def wq(e, dst=dst, src=src, ww=ww):
                        for i in range(8):
                            e.dma_start(dst[:, i * 128:(i + 1) * 128],
                                        src[:, i * 128:(i + 1) * 128]).then_inc(ww, 16)
                        e.wait_ge(ww, 256)
                    return wq
                bw.sync(mk(dsts[0]))
                bw.scalar(mk(dsts[1]))
        sems_nc = [s2.h for n2, s2 in self.sems.items() if n2 != "WW"]
        with self.nc.Block() as b2:
            def clearer2(g, sems=sems_nc):
                for h in sems:
                    g.sem_clear(h)
            b2.gpsimd(clearer2)
        with self.nc.Block() as block:
            for e in ENGINES:
                def runner(eng, e=e):
                    for lab, f in self.q[e]:
                        f(eng)
                getattr(block, e)(runner)

    def finalize_switch(self, qs, zfs, zfl):
        nc = self.nc
        sems = [s.h for s in self.sems.values()]
        with nc.Block() as b0:
            def clearer(g, sems=sems):
                for h in sems:
                    g.sem_clear(h)
            b0.gpsimd(clearer)
        dsts, src = self.warm
        ww = self.sems["WW"].h
        with nc.Block() as bw:
            def mk(dst):
                def wq(e, dst=dst, src=src, ww=ww):
                    for i in range(8):
                        e.dma_start(dst[:, i * 128:(i + 1) * 128],
                                    src[:, i * 128:(i + 1) * 128]).then_inc(ww, 16)
                    e.wait_ge(ww, 256)
                return wq
            bw.sync(mk(dsts[0]))
            bw.scalar(mk(dsts[1]))
        with nc.Block() as bz:
            def zload(s, zfs=zfs, zfl=zfl, ww=ww):
                s.dma_start(zfs[:], zfl).then_inc(ww, 16)
                s.wait_ge(ww, 272)
            bz.sync(zload)
        sems_nc = [s2.h for n2, s2 in self.sems.items() if n2 != "WW"]
        with nc.Block() as b2:
            def clearer2(g, sems=sems_nc):
                for h in sems:
                    g.sem_clear(h)
            b2.gpsimd(clearer2)
        with nc.Block() as block:
            for e in ENGINES:
                def runner(eng, e=e):
                    val = nc.values_load(zfs[0:1, 0:1], engines=[eng.engine],
                                         min_val=0, max_val=1,
                                         skip_runtime_bounds_check=True)
                    for zz in eng.Switch(val, 2):
                        cur, cm = None, None
                        for lab, f in qs[zz][e]:
                            if lab != cur:
                                if cm is not None:
                                    cm.__exit__(None, None, None)
                                cm = nc.named_scope(lab)
                                cm.__enter__()
                                cur = lab
                            f(eng)
                        if cm is not None:
                            cm.__exit__(None, None, None)
                getattr(block, e)(runner)

    def dma(self, queue, out, in_, wait=None, lane=0, **kw):
        so = self.sems[self.lanes[queue][lane]]
        wl = list(wait or [])
        if so.val > 0:
            wl.append((so, so.val))
        return self.emit(queue, "dma_start", out, in_, wait=wl or None,
                         inc=(so, 16), **kw)

    def mm(self, out, lhsT, rhs, start, stop, wait=None, inc=True):
        return self.emit("tensor", "matmul", out, lhsT, rhs, start=start, stop=stop,
                         wait=wait, inc=(self.sems["PE"], 1) if inc else None)

    def trans(self, out, in_, ident, wait=None):
        return self.emit("tensor", "transpose", out, in_, ident, wait=wait,
                         inc=(self.sems["PE"], 1))

    def act(self, out, in_, func, bias=0.0, scale=1.0, accum_out=None, wait=None,
            inc=True):
        return self.emit("scalar", "activation", out, in_, func, bias=bias, scale=scale,
                         accum_out=accum_out, wait=wait,
                         inc=(self.sems["AC"], 1) if inc else None)

    def dve(self, fname, *args, wait=None, inc=True, **kw):
        return self.emit("vector", fname, *args, wait=wait,
                         inc=(self.sems["DV"], 1) if inc else None, **kw)

    def fence_dv(self):
        return self.sems["DV"].val

    def fence_ac(self):
        return self.sems["AC"].val

    def pool(self, fname, *args, wait=None, **kw):
        return self.emit("gpsimd", fname, *args, wait=wait, inc=(self.sems["PO"], 1), **kw)


class Common:
    pass


def build_common(ctx):
    c = Common()
    nc = ctx.nc
    c.identF = ctx.sbuf("identF", [128, 128], F32)
    c.iota_jp = ctx.sbuf("iota_jp", [128, 128], F32)
    c.mask_neg = ctx.sbuf("mask_neg", [128, 128], BF16)
    c.ones_row = ctx.sbuf("ones_row", [1, 128], BF16)
    c.ones_col = ctx.sbuf("ones_col", [128, 1], BF16)
    c.ones11f = ctx.sbuf("ones11f", [65, 1], F32)
    c.ones_rowF = ctx.sbuf("ones_rowF", [1, 128], F32)
    ctx.junk = ctx.sbuf("junkF", [128, 128], F32)
    ctx.dve("memset", ctx.junk[:], 0.0)

    t = ctx.pool("iota", c.iota_jp[:], [[1, 128]], channel_multiplier=-1,
                 allow_small_or_imprecise_dtypes=True)
    po = ctx.sems["PO"]
    ctx.dve("tensor_scalar", c.identF[:], c.iota_jp[:], 0.0, None, ALU.is_equal,
            wait=[(po, t)])
    ctx.dve("tensor_scalar", c.mask_neg[:], c.iota_jp[:], 0.0, NEG, ALU.is_lt, ALU.mult,
            wait=[(po, t)])
    c.permU = ctx.sbuf("permU", [128, 128], BF16)
    ctx.dve("tensor_scalar", c.permU[:], c.iota_jp[:], 64.0, None, ALU.is_equal,
            wait=[(po, t)])
    ctx.dve("memset", c.ones_row[:], 1.0)
    ctx.dve("memset", c.ones_col[:], 1.0)
    ctx.dve("memset", c.ones11f[:], 1.0)
    ctx.dve("memset", c.ones_rowF[:], 1.0)
    return c


def emit_ln_tile(ctx, c, src_f32, fm_dst, g, tmp, wait=None):
    DV, AC, PE = ctx.sems["DV"], ctx.sems["AC"], ctx.sems["PE"]
    sq, st = tmp["sq"], tmp["st"]
    lnf = tmp["lnf"]
    w = wait or []
    ctx.dve("reduce_sum", st[:, 0:1], src_f32, axis=AX.X, wait=w)
    ta = ctx.act(sq[:], src_f32, AF.Square, wait=w)
    ctx.dve("reduce_sum", st[:, 1:2], sq[:], axis=AX.X, wait=[(AC, ta)])
    ctx.dve("tensor_scalar", st[:, 2:3], st[:, 0:1], 1.0 / DM, None, ALU.mult)
    ctx.dve("tensor_scalar", st[:, 3:4], st[:, 1:2], 1.0 / DM, LN_EPS, ALU.mult, ALU.add)
    ctx.dve("tensor_tensor", st[:, 4:5], st[:, 2:3], st[:, 2:3], ALU.mult)
    tv = ctx.dve("tensor_tensor", st[:, 5:6], st[:, 3:4], st[:, 4:5], ALU.subtract)
    tl = ctx.act(st[:, 6:7], st[:, 5:6], AF.Ln, wait=[(DV, tv)])
    tr = ctx.act(st[:, 7:8], st[:, 6:7], AF.Exp, scale=-0.5)
    tn = ctx.dve("scalar_tensor_tensor", st[:, 4:5], st[:, 2:3], -1.0, st[:, 7:8],
                 op0=ALU.mult, op1=ALU.mult, wait=[(AC, tr)])
    tno = ctx.act(lnf[:], src_f32, AF.Copy, bias=st[:, 4:5], scale=st[:, 7:8],
                  wait=[(DV, tn)])
    last = None
    for cch in range(NCH):
        pb = tmp["psT"][cch % 2]
        wl = [(AC, tno)]
        if tmp["psT_tok"][cch % 2] is not None:
            wl.append((AC, tmp["psT_tok"][cch % 2]))
        tt = ctx.trans(pb[:, 0:128], lnf[:, cch * 128:(cch + 1) * 128], c.identF[:],
                       wait=wl)
        te = ctx.act(fm_dst[:, cch * FMW + 1 + g * 128: cch * FMW + 1 + (g + 1) * 128],
                     pb[:, 0:128], AF.Copy, wait=[(PE, tt)])
        tmp["psT_tok"][cch % 2] = te
        tmp["psT_mm"][cch % 2] = tt
        last = te
    return last, tno


def load_cast_weight(ctx, dst_bf, dram_slices, stage_f32, wait_extra=None):
    LD, DV = ctx.sems["LD"], ctx.sems["DV"]
    toks = []
    for (c0, w, ap) in dram_slices:
        wl = list(wait_extra or [])
        if stage_f32 in (None,):
            pass
        toks.append((c0, w, ctx.dma("sync", stage_f32[:, c0:c0 + w], ap, wait=wl or None)))
    last = None
    for (c0, w, tok) in toks:
        last = ctx.dve("tensor_copy", dst_bf[:, c0:c0 + w], stage_f32[:, c0:c0 + w],
                       wait=[(LD, tok)])
    return last


class PsumPool:

    def __init__(self, tiles, n_rot=2):
        self.t = [{"ap": t, "r": None} for t in tiles]
        self.n_rot = n_rot
        self.i = 0

    def acquire(self):
        d = self.t[self.i % self.n_rot]
        self.i += 1
        waits = [d["r"]] if d["r"] is not None else []
        return d, waits


def emit_ln_tile2(ctx, c, src_f32, fm_writes, tmp, wait=None):
    DV, AC, PE = ctx.sems["DV"], ctx.sems["AC"], ctx.sems["PE"]
    sq, st, lnf = tmp["sq"], tmp["st"], tmp["lnf"]
    w = list(wait or [])
    if tmp.get("act_prev") is not None:
        w = w + [tmp["act_prev"]]
    ctx.dve("reduce_sum", st[:, 0:1], src_f32, axis=AX.X, wait=w)
    ta = ctx.act(sq[:], src_f32, AF.Square, wait=w)
    ctx.dve("reduce_sum", st[:, 1:2], sq[:], axis=AX.X, wait=[(AC, ta)])
    ctx.dve("tensor_scalar", st[:, 2:3], st[:, 0:1], 1.0 / DM, None, ALU.mult)
    ctx.dve("tensor_scalar", st[:, 3:4], st[:, 1:2], 1.0 / DM, LN_EPS, ALU.mult, ALU.add)
    ctx.dve("tensor_tensor", st[:, 4:5], st[:, 2:3], st[:, 2:3], ALU.mult)
    tv = ctx.dve("tensor_tensor", st[:, 5:6], st[:, 3:4], st[:, 4:5], ALU.subtract)
    tvf = ctx.fence_dv()
    tl = ctx.act(st[:, 6:7], st[:, 5:6], AF.Ln, wait=[(DV, tvf)])
    tr = ctx.act(st[:, 7:8], st[:, 6:7], AF.Exp, scale=-0.5)
    trf = ctx.fence_ac()
    tmp["act_prev"] = (AC, trf)
    wno = [(AC, trf)]
    if tmp.get("lnf_rd") is not None:
        wno.append(tmp["lnf_rd"])
    tno = ctx.dve("tensor_scalar", lnf[:], src_f32, st[:, 2:3], st[:, 7:8],
                  ALU.subtract, ALU.mult, wait=wno)
    ntok = (DV, tno)
    last = None
    for cch in range(NCH):
        slot = cch % 2
        pb = tmp["psT"][slot]
        wl = [ntok]
        if tmp["psT_tok"][slot] is not None:
            wl.append(tmp["psT_tok"][slot])
        tt = ctx.trans(pb["ap"][:, 0:128], lnf[:, cch * 128:(cch + 1) * 128],
                       c.identF[:], wait=wl)
        te = ctx.act(fm_writes[cch], pb["ap"][:, 0:128], AF.Copy, wait=[(PE, tt)])
        tmp["psT_tok"][slot] = (AC, te)
        last = te
    tmp["lnf_rd"] = (PE, tt)
    return last, ntok, tv


def fm_col(cch, t0):
    return cch * FMW + 1 + t0


def emit_part1(ctx, c, P, fused, stages=("s1", "s2")):
    nc = ctx.nc
    LD, GD, SD, PE, DV, AC = (ctx.sems[k] for k in ("LD", "GD", "SD", "PE", "DV", "AC"))
    z = P["z"]
    towns = own_tiles(z)
    r = P["r"]
    lnfm = P["lnfm"]
    pool = P["pool"]

    with ctx.scope():
        xt = [ctx.sbuf("xt0", [128, DM], F32), ctx.sbuf("xt1", [128, DM], F32)]
        xt_tok = [None, None]
        tmp = {"sq": ctx.sbuf("ln_sq", [128, DM], F32),
               "st": ctx.sbuf("ln_st", [128, 16], F32),
               "lnf": ctx.sbuf("ln_f", [128, DM], F32),
               "psT": [pool.t[2], pool.t[3]],
               "psT_tok": [None, None], "act_prev": None, "lnf_rd": None}
        lnfm_r = lnfm[:].rearrange("p (c w) -> p c w", w=FMW)
        ctx.dve("memset", lnfm_r[:, :, 0:1], 0.0)
        tdum = ctx.dma("sync", xt[0][:], P["x"][0:128, :])
        xt_tok[0] = [(ctx.sems["LD"], tdum)]
        for g in range(NTILE):
            slot = g % 2
            wl = xt_tok[slot] if xt_tok[slot] is not None else []
            tl = ctx.dma("sync", xt[slot][:], P["x"][g * 128:(g + 1) * 128, :],
                         wait=wl or None)
            if g in towns:
                oi = towns.index(g)
                ctx.dma("sync", r[:, oi * DM:(oi + 1) * DM],
                        P["x"][g * 128:(g + 1) * 128, :])
            fmw = [lnfm[:, fm_col(cc, g * 128): fm_col(cc, g * 128) + 128]
                   for cc in range(NCH)]
            te, ntok, tv = emit_ln_tile2(ctx, c, xt[slot][:], fmw, tmp, wait=[(LD, tl)])
            xt_tok[slot] = [ntok]
    ctx.barrier()
    if P.get("dbg_lnfm_early") is not None:
        ctx.dma("gpsimd", P["dbg_lnfm_early"], lnfm[:])
        ctx.barrier()
    if "s2" not in stages:
        return

    cosT = ctx.sbuf("cosT", [128, T], BF16)
    sinT = ctx.sbuf("sinT", [128, T], BF16)
    ctx.stage = "rope"
    rope_scope = ctx.scope()
    rope_scope.__enter__()
    frow = ctx.sbuf("frow", [1, 128], F32)
    posc = ctx.sbuf("posc", [1, 512], F32)
    rt1 = ctx.sbuf("rt1", [128, 512], F32)
    rt2 = ctx.sbuf("rt2", [128, 512], F32)
    rti = ctx.sbuf("rti", [128, 512], I32)
    tf = ctx.pool("iota", frow[0:1, 0:64], [[1, 64]], channel_multiplier=0,
                  allow_small_or_imprecise_dtypes=True)
    tfc = ctx.dve("tensor_copy", frow[0:1, 64:128], frow[0:1, 0:64],
                  wait=[(ctx.sems["PO"], tf)])
    tf2 = ctx.act(frow[:], frow[:], AF.Exp, scale=-math.log(ROPE_BASE) / 64.0,
                  wait=[(DV, tfc)])
    tc2 = None
    prev_tm1 = None
    for ch in range(8):
        pb, wl = pool.acquire()
        wpos = [(ctx.sems["PE"], prev_tm1)] if prev_tm1 is not None else []
        tpos = ctx.pool("iota", posc[:], [[1, 512]], base=ch * 512,
                        channel_multiplier=0, allow_small_or_imprecise_dtypes=True,
                        wait=wpos or None)
        tm1 = ctx.mm(pb["ap"][:], frow[:], posc[:],
                     True, True, wait=wl + [(AC, tf2), (ctx.sems["PO"], tpos)])
        prev_tm1 = tm1
        wfirst = [(PE, tm1)] + ([(AC, tc2)] if tc2 is not None else [])
        ctx.dve("tensor_scalar", rt1[:], pb["ap"][:], 1.0 / TWO_PI, None, ALU.mult,
                wait=wfirst)
        ctx.dve("tensor_copy", rti[:], rt1[:])
        ctx.dve("tensor_copy", rt1[:], rti[:])
        ctx.dve("scalar_tensor_tensor", rt2[:], rt1[:], -PIH, pb["ap"][:],
                op0=ALU.mult, op1=ALU.add)
        ctx.dve("scalar_tensor_tensor", rt2[:], rt1[:], -PIL, rt2[:],
                op0=ALU.mult, op1=ALU.add)
        tcl = ctx.dve("tensor_scalar", rt2[:], rt2[:], math.pi, -math.pi, ALU.min, ALU.max)
        ts1 = ctx.act(sinT[:, ch * 512:(ch + 1) * 512], rt2[:], AF.Sin, wait=[(DV, tcl)])
        ctx.dve("tensor_scalar", rt1[:], pb["ap"][:], 1.0 / TWO_PI, 0.25,
                ALU.mult, ALU.add, wait=[(AC, ts1)])
        ctx.dve("tensor_copy", rti[:], rt1[:])
        ctx.dve("tensor_copy", rt1[:], rti[:])
        ctx.dve("scalar_tensor_tensor", rt2[:], rt1[:], -PIH, pb["ap"][:],
                op0=ALU.mult, op1=ALU.add)
        ctx.dve("scalar_tensor_tensor", rt2[:], rt1[:], -PIL, rt2[:],
                op0=ALU.mult, op1=ALU.add)
        ctx.dve("tensor_scalar", rt2[:], rt2[:], math.pi / 2.0, None, ALU.add)
        tcl2 = ctx.dve("tensor_scalar", rt2[:], rt2[:], math.pi, -math.pi, ALU.min, ALU.max)
        tc2 = ctx.act(cosT[:, ch * 512:(ch + 1) * 512], rt2[:], AF.Sin, wait=[(DV, tcl2)])
        pb["r"] = (AC, tc2)
    ctx.barrier()
    rope_scope.__exit__(None, None, None)

    kA = ctx.sbuf("kA", [65, T], BF16)
    kB = ctx.sbuf("kB", [64, T], BF16)
    qA = ctx.sbuf("qA", [65, T], BF16)
    qB = ctx.sbuf("qB", [64, T], BF16)
    vtm = ctx.sbuf("vtm", [128, NTILE * 129], BF16)
    wbuf = ctx.sbuf("wbuf", [128, DM], BF16)
    wq_bf = wk_bf = wv_bf = wo_bf = wbuf
    wstage = [ctx.sbuf("wst0", [128, 128], F32),
              ctx.sbuf("wst1", [128, 128], F32)]
    scr512 = ctx.sbuf("scr512", [128, 512], F32)
    vstage = scr512
    ptb = [ctx.sbuf("ptb0", [128, 1024], BF16),
           ctx.sbuf("ptb1", [128, 1024], BF16)]
    obf = ctx.sbuf("obf", [128, 1024], BF16)
    km = ctx.sbuf("km", [1, 4], F32)
    rs_s = ctx.sbuf("rs_s", [65, 1024], F32)
    tmp512 = rs_s
    rncols = ctx.sbuf("rncols", [128, 8], F32)
    rntmp = ctx.sbuf("rntmp", [128, 2], F32)
    xacc = [ctx.sbuf("xacc0", [128, DM], F32)]
    xacc_tok = [None]
    ropeP = ctx.sbuf("ropeP", [128, 512], BF16)
    brow_bf = ropeP[0:1, 0:512]
    sqq = ptb[0][0:64, 0:512]

    vtm_r = vtm[:].rearrange("p (t w) -> p t w", w=129)
    ctx.dve("memset", vtm_r[:, :, 64:65], 1.0)
    ctx.dve("memset", kA[64:65, :], 1.0)
    pv_done = [None, None]

    psO1, psO2 = P["psO1"], P["psO2"]
    psO_rd = [[]]
    wbuf_rd = {"w": None}
    kq_last_dv = [None]
    kq_last_gd = [None]
    v_last_ac = [None]
    wst_tok = [None, None]

    LL = (LD, ctx.sems["L1"])

    def load_w(dst, slices, name="w"):
        lastdv = None
        for i, ap in enumerate(slices):
            slot = i % 2
            wl = []
            if wst_tok[slot] is not None:
                wl.append(wst_tok[slot])
            tl = ctx.dma("sync", wstage[slot][:], ap, wait=wl or None, lane=slot)
            wl2 = [(LL[slot], tl)]
            if wbuf_rd[name] is not None:
                wl2.append(wbuf_rd[name])
            lastdv = ctx.dve("tensor_copy", dst[:, i * 128:(i + 1) * 128],
                             wstage[slot][:], wait=wl2)
            wst_tok[slot] = (DV, lastdv)
        return lastdv

    halves = q_halves(z)

    for h in range(L0H):
        ctx.stage = "Akq%d" % h
        def proj_rope(wbf, wtok, dstA, dstB, kind):
            lastdv = None
            ry_tok = [None]
            for tc in range(8):
                pb, wl = pool.acquire()
                cols = slice(tc * 512, (tc + 1) * 512)
                for cc in range(NCH):
                    w0 = (wl + [(DV, wtok)]) if cc == 0 else None
                    tmm = ctx.mm(pb["ap"][:], wbf[:, cc * 128:(cc + 1) * 128],
                                 lnfm[:, fm_col(cc, tc * 512): fm_col(cc, tc * 512) + 512],
                                 cc == 0, cc == NCH - 1, wait=w0)
                wev = [(PE, tmm)]
                if ry_tok[0] is not None:
                    wev.append(ry_tok[0])
                if kq_last_gd[0] is not None:
                    wev.append((GD, kq_last_gd[0]))
                tev = ctx.act(ropeP[:], pb["ap"][:], AF.Copy, wait=wev)
                pb["r"] = (AC, tev)
                pb2, wl2 = pool.acquire()
                tpm = ctx.mm(pb2["ap"][0:64, 0:512], c.permU[:, 0:64], ropeP[:],
                             True, True, wait=wl2 + [(AC, tev)])
                wev2 = [(PE, tpm)]
                if ry_tok[0] is not None:
                    wev2.append(ry_tok[0])
                rsw = rs_s[0:64, 0:512]
                rq = rs_s[0:64, 512:1024]
                tsw = ctx.act(rsw, pb2["ap"][0:64, 0:512], AF.Copy, wait=wev2)
                pb2["r"] = (AC, tsw)
                ctx.dve("tensor_tensor", rq, rsw,
                        cosT[0:64, cols], ALU.mult, wait=[(AC, tsw)])
                ctx.dve("tensor_tensor", scr512[0:64, :], ropeP[0:64, :],
                        sinT[0:64, cols], ALU.mult)
                ctx.dve("tensor_tensor", dstB[:, cols], rq,
                        scr512[0:64, :], ALU.add)
                ctx.dve("tensor_tensor", rq, ropeP[0:64, :],
                        cosT[0:64, cols], ALU.mult)
                ctx.dve("tensor_tensor", rsw, rsw,
                        sinT[0:64, cols], ALU.mult)
                lastdv = ctx.dve("tensor_tensor", dstA[0:64, cols], rq,
                                 rsw, ALU.subtract)
                ry_tok[0] = (DV, lastdv)
            return lastdv

        t_wk = load_w(wbuf, [P["w_ak"][cc * 128:(cc + 1) * 128, h, :] for cc in range(NCH)])
        tka = proj_rope(wbuf, t_wk, kA, kB, "k")
        wbuf_rd["w"] = (PE, PE.val)
        t_wq = load_w(wbuf, [P["w_aq"][cc * 128:(cc + 1) * 128, h, :] for cc in range(NCH)])
        tqa = proj_rope(wbuf, t_wq, qA, qB, "q")
        wbuf_rd["w"] = (PE, PE.val)
        ctx.stage = "Av%d" % h
        t_wv = load_w(wbuf, [P["w_av"][cc * 128:(cc + 1) * 128, h, :] for cc in range(NCH)])

        lastv = None
        for tc in range(8):
            pb, wl = pool.acquire()
            for cc in range(NCH):
                w0 = (wl + [(DV, t_wv)]) if cc == 0 else None
                tmm = ctx.mm(pb["ap"][:], wv_bf[:, cc * 128:(cc + 1) * 128],
                             lnfm[:, fm_col(cc, tc * 512): fm_col(cc, tc * 512) + 512],
                             cc == 0, cc == NCH - 1, wait=w0)
            tvs = ctx.act(vstage[:], pb["ap"][:], AF.Copy,
                          wait=[(PE, tmm), (DV, tqa)])
            pb["r"] = (AC, tvs)
            for j in range(4):
                g = tc * 4 + j
                pt, wlt = pool.acquire()
                tt = ctx.trans(pt["ap"][:, 0:128], vstage[:, j * 128:(j + 1) * 128],
                               c.identF[:], wait=wlt + [(AC, tvs)])
                ctx.act(vtm[:, g * 129: g * 129 + 64], pt["ap"][:, 0:64], AF.Copy,
                        wait=[(PE, tt)], inc=False)
                lastv = ctx.act(vtm[:, g * 129 + 65: g * 129 + 129], pt["ap"][:, 64:128],
                                AF.Copy)
                pt["r"] = (AC, lastv)
        wbuf_rd["w"] = (PE, PE.val)
        t_wo = load_w(wbuf, [P["w_ao"][h, :, cc2 * 128:(cc2 + 1) * 128] for cc2 in range(NCH)])
        v_last_ac[0] = lastv

        ctx.stage = "Ab%d" % h
        for tc in range(8):
            cols = slice(tc * 512, (tc + 1) * 512)
            wsq = [(DV, tka)]
            if tc == 0 and pv_done[0] is not None:
                wsq.append(pv_done[0])
            tsq = ctx.act(sqq[:], kA[0:64, cols], AF.Square, wait=wsq)
            pb, wl = pool.acquire()
            tm1 = ctx.mm(pb["ap"][0:1, :], c.ones_col[0:64, :], sqq[:], True, False,
                         wait=wl + [(AC, tsq)])
            tsq2 = ctx.act(sqq[:], kB[:, cols], AF.Square, wait=[(PE, tm1)])
            tmm = ctx.mm(pb["ap"][0:1, :], c.ones_col[0:64, :], sqq[:], False, True,
                         wait=[(AC, tsq2)])
            tev = ctx.dve("reduce_max", km[0:1, 1:2] if tc > 0 else km[0:1, 0:1],
                          pb["ap"][0:1, :], axis=AX.X, wait=[(PE, tmm)])
            if tc > 0:
                tev = ctx.dve("tensor_tensor", km[0:1, 0:1], km[0:1, 0:1],
                              km[0:1, 1:2], ALU.max)
            pb["r"] = (DV, tev)
        ctx.dve("tensor_scalar", km[0:1, 1:2], km[0:1, 0:1], 1e-38, None, ALU.add)
        tkf = ctx.fence_dv()
        tl1 = ctx.act(km[0:1, 2:3], km[0:1, 1:2], AF.Ln, wait=[(DV, tkf)])
        tk2 = ctx.act(km[0:1, 3:4], km[0:1, 2:3], AF.Exp, scale=0.5)
        tB = None
        tBg = None
        for tc in range(8):
            cols = slice(tc * 512, (tc + 1) * 512)
            tsq = ctx.act(sqq[:], qA[0:64, cols], AF.Square, wait=[(DV, tqa)])
            pb, wl = pool.acquire()
            tm1 = ctx.mm(pb["ap"][0:1, :], c.ones_col[0:64, :], sqq[:], True, False,
                         wait=wl + [(AC, tsq)])
            tsq2 = ctx.act(sqq[:], qB[:, cols], AF.Square, wait=[(PE, tm1)])
            tmm = ctx.mm(pb["ap"][0:1, :], c.ones_col[0:64, :], sqq[:], False, True,
                         wait=[(AC, tsq2)])
            td = ctx.dve("tensor_scalar", rs_s[0:1, 0:512], pb["ap"][0:1, :],
                         1e-38, None, ALU.add, wait=[(PE, tmm)])
            tdf = ctx.fence_dv()
            ta = ctx.act(rs_s[0:1, 512:1024], rs_s[0:1, 0:512], AF.Ln,
                         wait=[(DV, tdf)])
            ta = ctx.act(rs_s[0:1, 0:512], rs_s[0:1, 512:1024], AF.Exp, scale=0.5)
            ta = ctx.fence_ac()
            wB = [(AC, ta)]
            if tBg is not None:
                wB.append((GD, tBg))
            tB = ctx.dve("tensor_scalar", brow_bf[:], rs_s[0:1, 0:512],
                         km[0:1, 3:4], -1.0, ALU.mult, ALU.mult, wait=wB)
            tB = ctx.fence_dv()
            tBg = ctx.dma("gpsimd", qA[64:65, cols], brow_bf[:], wait=[(DV, tB)])
            pb["r"] = (DV, td)
        kq_last_dv[0] = tB
        kq_last_gd[0] = tBg

        if P.get("dbg_h0") is not None and h == 0:
            d = P["dbg_h0"]
            ctx.dma("gpsimd", d["kA"], kA[:], wait=[(DV, kq_last_dv[0])])
            ctx.dma("gpsimd", d["kB"], kB[:])
            ctx.dma("gpsimd", d["qA"], qA[:])
            ctx.dma("gpsimd", d["qB"], qB[:])
            ctx.dma("gpsimd", d["vtm"], vtm[:], wait=[(AC, v_last_ac[0])])
            ctx.barrier()

        ctx.stage = "As%d" % h
        for (tiles, is_own) in halves:
            if not fused and not is_own:
                continue
            t0g = tiles[0]
            lastkt = tiles[-1]
            chunk_lastkt = [min(lastkt, t0g + (qc * 512 + 511) // 128) for qc in range(2)]
            for kt in range(0, lastkt + 1):
                base = max(0, (kt - t0g) * 128)
                slot = kt % 2
                pt = ptb[slot]
                wz = []
                if pv_done[slot] is not None:
                    wz.append(pv_done[slot])
                tz = None
                if base > 0:
                    z0 = max(0, base - 256)
                    tz = ctx.dve("memset", pt[:, z0:base], 0.0, wait=wz or None)
                exp_toks = []
                for qc in range(2):
                    if kt > chunk_lastkt[qc]:
                        continue
                    lo = max(base, qc * 512)
                    hi = (qc + 1) * 512
                    if lo >= hi:
                        continue
                    pb, wl = pool.acquire()
                    w0 = wl + [(DV, kq_last_dv[0]), (GD, kq_last_gd[0])]
                    ctx.mm(pb["ap"][:, 0:hi - lo], kA[:, kt * 128:(kt + 1) * 128],
                           qA[:, t0g * 128 + lo: t0g * 128 + hi], True, False, wait=w0)
                    tmm = ctx.mm(pb["ap"][:, 0:hi - lo], kB[:, kt * 128:(kt + 1) * 128],
                                 qB[:, t0g * 128 + lo: t0g * 128 + hi], False, True)
                    wexp = [(PE, tmm)]
                    if kt in tiles and lo == base:
                        tmsk = ctx.dve("tensor_tensor", pb["ap"][:, 0:128],
                                       pb["ap"][:, 0:128], c.mask_neg[:], ALU.add,
                                       wait=[(PE, tmm)])
                        wexp.append((DV, tmsk))
                    if tz is not None:
                        wexp.append((DV, tz))
                    elif wz:
                        wexp += wz
                    texp = ctx.act(pt[:, lo:hi], pb["ap"][:, 0:hi - lo], AF.Exp,
                                   scale=1.0 / L0D, wait=wexp)
                    pb["r"] = (AC, texp)
                    exp_toks.append(texp)
                wpv = [(AC, tk) for tk in exp_toks]
                if kt == 0:
                    wpv = wpv + psO_rd[0] + [(AC, v_last_ac[0])]
                lastpv = None
                for qc in range(2):
                    if kt > chunk_lastkt[qc]:
                        continue
                    cols = slice(qc * 512, (qc + 1) * 512)
                    st_, sp_ = (kt == 0), (kt == chunk_lastkt[qc])
                    ctx.mm(psO1[0:65, cols], vtm[:, kt * 129: kt * 129 + 65],
                           pt[:, cols], st_, sp_, wait=wpv if qc == 0 else None)
                    lastpv = ctx.mm(psO2[64:128, cols], vtm[:, kt * 129 + 65: kt * 129 + 129],
                                    pt[:, cols], st_, sp_)
                pv_done[slot] = (PE, lastpv)
            t_rs = ctx.dve("tensor_copy", rs_s[64:65, 0:1024], psO1[64:65, :],
                           wait=[(PE, PE.val)])
            to1 = ctx.act(obf[0:64, :], psO1[0:64, :], AF.Copy, wait=[(PE, PE.val)])
            to2 = ctx.act(obf[64:128, :], psO2[64:128, :], AF.Copy)
            psO_rd[0] = [(AC, to2), (DV, t_rs)]
            if P.get("dbg_h0") is not None and h == 0:
                d = P["dbg_h0"]
                hf = tiles[0] // 8
                ctx.dma("gpsimd", d["obf"][hf * 128:(hf + 1) * 128, :], obf[:],
                        wait=[(AC, to2)])
                ctx.dma("gpsimd", d["rsum"][hf:hf + 1, :], rs_s[64:65, :],
                        wait=[(DV, t_rs)])
            prev_ta2 = None
            for ti in range(8):
                pb, wl = pool.acquire()
                tmmr = ctx.mm(pb["ap"][:, 0:1], rs_s[64:65, ti * 128:(ti + 1) * 128],
                              c.ones11f[64:65, :], True, True, wait=wl + [(DV, t_rs)])
                wtd = [(PE, tmmr)]
                if prev_ta2 is not None:
                    wtd.append((AC, prev_ta2))
                td1 = ctx.dve("tensor_scalar", rntmp[:, 1:2], pb["ap"][:, 0:1],
                              1e-38, None, ALU.add, wait=wtd)
                td1f = ctx.fence_dv()
                ta1 = ctx.act(rntmp[:, 0:1], rntmp[:, 1:2], AF.Ln, wait=[(DV, td1f)])
                ta2 = ctx.act(rncols[:, ti:ti + 1], rntmp[:, 0:1], AF.Exp, scale=-1.0)
                ta2 = ctx.fence_ac()
                prev_ta2 = ta2
                pb["r"] = (AC, ta2)
            for ti in range(8):
                g = tiles[ti]
                is_own_tile = g in towns
                if not fused and not is_own_tile:
                    continue
                evac_toks = []
                if (not is_own_tile):
                    slot = 0
                    if h == 0:
                        wlx = [xacc_tok[slot]] if xacc_tok[slot] is not None else []
                        tx = ctx.dma("sync", xacc[slot][:],
                                     P["x"][g * 128:(g + 1) * 128, :], wait=wlx or None)
                for qc in range(2):
                    pb, wl = pool.acquire()
                    tmm = ctx.mm(pb["ap"][:], obf[:, ti * 128:(ti + 1) * 128],
                                 wo_bf[:, qc * 512:(qc + 1) * 512], True, True,
                                 wait=wl + [(AC, AC.val), (DV, t_wo)])
                    if is_own_tile:
                        oi = towns.index(g)
                        tev = ctx.dve("scalar_tensor_tensor",
                                      r[:, oi * DM + qc * 512: oi * DM + (qc + 1) * 512],
                                      pb["ap"][:], rncols[:, ti:ti + 1],
                                      r[:, oi * DM + qc * 512: oi * DM + (qc + 1) * 512],
                                      op0=ALU.mult, op1=ALU.add, wait=[(PE, tmm)])
                    else:
                        slot = 0
                        cols = slice(qc * 512, (qc + 1) * 512)
                        if h == 0:
                            tev = ctx.dve("scalar_tensor_tensor", xacc[slot][:, cols],
                                          pb["ap"][:], rncols[:, ti:ti + 1],
                                          xacc[slot][:, cols],
                                          op0=ALU.mult, op1=ALU.add,
                                          wait=[(PE, tmm), (LD, tx)])
                        else:
                            wlx2 = [(PE, tmm)]
                            if qc == 0 and xacc_tok[slot] is not None:
                                wlx2.append(xacc_tok[slot])
                            tev = ctx.dve("tensor_scalar", xacc[slot][:, cols],
                                          pb["ap"][:], rncols[:, ti:ti + 1], None,
                                          ALU.mult, wait=wlx2)
                    pb["r"] = (DV, tev)
                    evac_toks.append(tev)
                if not is_own_tile:
                    slot = 0
                    tfx = ctx.fence_dv()
                    td = ctx.dma("gpsimd", P["xmid"][g * 128:(g + 1) * 128, :],
                                 xacc[slot][:],
                                 wait=[(DV, tfx)],
                                 accum_op=(ALU.add if h > 0 else ALU.bypass))
                    xacc_tok[slot] = (GD, td)
        wbuf_rd["w"] = (PE, PE.val)
    ctx.barrier()


def emit_l0_ffn(ctx, c, P, fused):
    LD, GD, PE, DV, AC = (ctx.sems[k] for k in ("LD", "GD", "PE", "DV", "AC"))
    z = P["z"]
    towns = own_tiles(z)
    r, lnfm, pool = P["r"], P["lnfm"], P["pool"]

    kstall = ctx.sbuf("kstall", [128, NCH * DM], F32)
    qw_bf = ctx.sbuf("qw_bf", [128, NCH * DM], BF16)
    a_bf = ctx.sbuf("a_bf", [128, NCH * 16], BF16)
    b_bf = ctx.sbuf("b_bf", [16, DM], BF16)
    bias_bf = ctx.sbuf("bias_bf", [1, DM], BF16)
    qs = ctx.sbuf("qs", [128, 8], F32)
    qt = ctx.sbuf("qt", [128, DM], F32)
    qi = ctx.sbuf("qi", [128, DM], I32)
    fmtile = ctx.sbuf("fmtile", [128, DM], BF16)
    uT = ctx.sbuf("uT", [16, 128], BF16)
    xm = ctx.sbuf("xm", [128, DM], F32)
    tmp = {"sq": ctx.sbuf("l3_sq", [128, DM], F32),
           "st": ctx.sbuf("l3_st", [128, 16], F32),
           "lnf": ctx.sbuf("l3_f", [128, DM], F32),
           "psT": [pool.t[2], pool.t[3]],
           "psT_tok": [None, None], "act_prev": None, "lnf_rd": None}

    tstg = []
    for cc in range(NCH):
        tstg.append(ctx.dma("sync", kstall[:, cc * DM:(cc + 1) * DM],
                            P["ffn_kernel"][cc * 128:(cc + 1) * 128, :]))
        tA = ctx.dma("sync", tmp["sq"][:, cc * 16:(cc + 1) * 16],
                     P["ffn_a"][cc * 128:(cc + 1) * 128, :])
        ctx.dve("tensor_copy", a_bf[:, cc * 16:(cc + 1) * 16],
                tmp["sq"][:, cc * 16:(cc + 1) * 16], wait=[(LD, tA)])
    tb = ctx.dma("sync", tmp["sq"][0:16, 0:DM], P["ffn_b"])
    ctx.dve("tensor_copy", b_bf[:], tmp["sq"][0:16, 0:DM], wait=[(LD, tb)])
    tbias = ctx.dma("sync", tmp["sq"][0:1, 0:DM], P["ffn_bias"][None, :])
    tbias2 = ctx.dve("tensor_copy", bias_bf[:], tmp["sq"][0:1, 0:DM], wait=[(LD, tbias)])

    for cc in range(NCH):
        t1 = ctx.dve("tensor_reduce", qs[:, 1:2] if cc else qs[:, 0:1],
                     kstall[:, cc * DM:(cc + 1) * DM], axis=AX.X, op=ALU.max,
                     apply_absolute_value=True, wait=[(LD, tstg[cc])])
        if cc:
            ctx.dve("tensor_tensor", qs[:, 0:1], qs[:, 0:1], qs[:, 1:2], ALU.max)
    ctx.fence_dv()
    pb, wl = pool.acquire()
    tt = ctx.trans(pb["ap"][0:1, 0:128], qs[:, 0:1], c.identF[:],
                   wait=wl + [(DV, DV.val)])
    tgm = ctx.dve("reduce_max", qs[0:1, 2:3], pb["ap"][0:1, 0:128], axis=AX.X,
                  wait=[(PE, tt)])
    pb["r"] = (DV, tgm)
    tsc = ctx.dve("tensor_scalar", qs[0:1, 3:4], qs[0:1, 2:3], 1.0 / 7.0, None, ALU.mult)
    tla = ctx.act(qs[0:1, 6:7], qs[0:1, 3:4], AF.Ln, wait=[(DV, ctx.fence_dv())])
    tra = ctx.act(qs[0:1, 5:6], qs[0:1, 6:7], AF.Exp, scale=-1.0)
    tsc = ctx.fence_ac()
    pb, wl = pool.acquire()
    tbc = ctx.mm(pb["ap"][:, 0:1], c.ones_rowF[:, 0:128], qs[0:1, 3:4], True, False,
                 wait=wl + [(AC, tsc)])
    tbc = ctx.mm(pb["ap"][:, 1:2], c.ones_rowF[:, 0:128], qs[0:1, 5:6], False, True)
    tscol = ctx.act(qs[:, 4:5], pb["ap"][:, 0:1], AF.Copy, wait=[(PE, tbc)])
    tscol = ctx.act(qs[:, 5:6], pb["ap"][:, 1:2], AF.Copy)
    tscol = ctx.fence_ac()
    pb["r"] = (AC, tscol)
    for cc in range(NCH):
        ctx.dve("tensor_scalar", qt[:], kstall[:, cc * DM:(cc + 1) * DM],
                qs[:, 5:6], None, ALU.mult, wait=[(AC, tscol)])
        ctx.dve("tensor_copy", qi[:], qt[:])
        ctx.dve("tensor_copy", qt[:], qi[:])
        ctx.dve("tensor_scalar", qt[:], qt[:], 7.0, -8.0, ALU.min, ALU.max)
        ctx.dve("tensor_scalar", qw_bf[:, cc * DM:(cc + 1) * DM], qt[:],
                qs[:, 4:5], None, ALU.mult)

    xm_tok = None
    for g in range(NTILE):
        is_own = g in towns
        if not fused and not is_own:
            continue
        if is_own:
            oi = towns.index(g)
            src = r[:, oi * DM:(oi + 1) * DM]
            w0 = []
        else:
            wlx = xm_tok if xm_tok is not None else []
            txm = ctx.dma("sync", xm[:], P["xmid"][g * 128:(g + 1) * 128, :],
                          wait=(wlx or None))
            src = xm[:]
            w0 = [(LD, txm)]
        fmw = [fmtile[:, cc * 128:(cc + 1) * 128] for cc in range(NCH)]
        te, ntok, tv = emit_ln_tile2(ctx, c, src, fmw, tmp, wait=w0)
        pb, wl = pool.acquire()
        for cc in range(NCH):
            wu = (wl + [(AC, te)]) if cc == 0 else None
            tu = ctx.mm(pb["ap"][0:16, 0:128], a_bf[:, cc * 16:(cc + 1) * 16],
                        fmtile[:, cc * 128:(cc + 1) * 128], cc == 0, cc == NCH - 1,
                        wait=wu)
        tue = ctx.act(uT[:], pb["ap"][0:16, 0:128], AF.Copy, wait=[(PE, tu)])
        pb["r"] = (AC, tue)
        add_toks = []
        for qc in range(2):
            pb, wl = pool.acquire()
            for cc in range(NCH):
                wq_ = (wl + [(DV, DV.val), (AC, tue)]) if cc == 0 else None
                ctx.mm(pb["ap"][:], fmtile[:, cc * 128:(cc + 1) * 128],
                       qw_bf[:, cc * DM + qc * 512: cc * DM + (qc + 1) * 512],
                       cc == 0, False, wait=wq_)
            ctx.mm(pb["ap"][:], uT[:], b_bf[:, qc * 512:(qc + 1) * 512], False, False)
            tmm = ctx.mm(pb["ap"][:], c.ones_row[0:1, 0:128],
                         bias_bf[:, qc * 512:(qc + 1) * 512], False, True,
                         wait=[(DV, tbias2)] if qc == 0 and g == (towns[0] if not fused else 0) else None)
            if is_own:
                oi = towns.index(g)
                ta = ctx.dve("tensor_tensor", r[:, oi * DM + qc * 512: oi * DM + (qc + 1) * 512],
                             r[:, oi * DM + qc * 512: oi * DM + (qc + 1) * 512],
                             pb["ap"][:], ALU.add, wait=[(PE, tmm)])
            else:
                ta = ctx.dve("tensor_tensor", xm[:, qc * 512:(qc + 1) * 512],
                             xm[:, qc * 512:(qc + 1) * 512],
                             pb["ap"][:], ALU.add, wait=[(PE, tmm)])
            pb["r"] = (DV, ta)
            add_toks.append(ta)
        fmw3 = [lnfm[:, fm_col(cc, g * 128): fm_col(cc, g * 128) + 128]
                for cc in range(NCH)]
        src3 = r[:, towns.index(g) * DM: (towns.index(g) + 1) * DM] if is_own else xm[:]
        te3, ntok3, tv3 = emit_ln_tile2(ctx, c, src3, fmw3, tmp,
                                        wait=[(DV, tk) for tk in add_toks])
        if not is_own:
            xm_tok = [ntok3]
    ctx.barrier()


def declare_l0_inputs(nc):
    d = {}
    d["x"] = nc.dram_tensor("x", [T, DM], F32, kind="ExternalInput").ap()
    d["w_aq"] = nc.dram_tensor("w_aq", [DM, L0H, L0D], F32, kind="ExternalInput").ap()
    d["w_ak"] = nc.dram_tensor("w_ak", [DM, L0H, L0D], F32, kind="ExternalInput").ap()
    d["w_av"] = nc.dram_tensor("w_av", [DM, L0H, L0D], F32, kind="ExternalInput").ap()
    d["w_ao"] = nc.dram_tensor("w_ao", [L0H, L0D, DM], F32, kind="ExternalInput").ap()
    d["ffn_kernel"] = nc.dram_tensor("ffn_kernel", [DM, DM], F32, kind="ExternalInput").ap()
    d["ffn_bias"] = nc.dram_tensor("ffn_bias", [DM], F32, kind="ExternalInput").ap()
    d["ffn_a"] = nc.dram_tensor("ffn_a", [DM, 16], F32, kind="ExternalInput").ap()
    d["ffn_b"] = nc.dram_tensor("ffn_b", [16, DM], F32, kind="ExternalInput").ap()
    return d


def build_part1(z, fused=False, stages=("s1", "s2", "s3"), debug=()):
    nc = bass.Bass("TRN2", target_bir_lowering=False)
    P = declare_l0_inputs(nc)
    P["z"] = z
    out_r = nc.dram_tensor("out_r", [16 * 128, DM], F32, kind="ExternalOutput").ap()
    xmid_kind = "ExternalOutput" if "xmid" in debug else "Internal"
    P["xmid"] = nc.dram_tensor("xmid", [T, DM], F32, kind=xmid_kind).ap()
    dbg_lnfm = None
    if "lnfm" in debug:
        dbg_lnfm = nc.dram_tensor("lnfm_out", [128, NCH * FMW], BF16,
                                  kind="ExternalOutput").ap()
    if "h0" in debug:
        P["dbg_lnfm_early"] = nc.dram_tensor("d_lnfm_early", [128, NCH * FMW], BF16,
                                             kind="ExternalOutput").ap()
        P["dbg_rope"] = {
            "ropeP": nc.dram_tensor("d_ropeP", [128, 512], BF16, kind="ExternalOutput").ap(),
            "ropeQ": nc.dram_tensor("d_ropeQ", [128, 512], BF16, kind="ExternalOutput").ap(),
            "ropeY": nc.dram_tensor("d_ropeY", [64, 512], BF16, kind="ExternalOutput").ap(),
            "cosT": nc.dram_tensor("d_cosT", [128, T], BF16, kind="ExternalOutput").ap(),
            "sinT": nc.dram_tensor("d_sinT", [128, T], BF16, kind="ExternalOutput").ap(),
        }
        P["dbg_h0"] = {
            "kA": nc.dram_tensor("d_kA", [65, T], BF16, kind="ExternalOutput").ap(),
            "kB": nc.dram_tensor("d_kB", [64, T], BF16, kind="ExternalOutput").ap(),
            "qA": nc.dram_tensor("d_qA", [65, T], BF16, kind="ExternalOutput").ap(),
            "qB": nc.dram_tensor("d_qB", [64, T], BF16, kind="ExternalOutput").ap(),
            "vtm": nc.dram_tensor("d_vtm", [128, NTILE * 129], BF16, kind="ExternalOutput").ap(),
            "obf": nc.dram_tensor("d_obf", [512, 1024], BF16, kind="ExternalOutput").ap(),
            "rsum": nc.dram_tensor("d_rsum", [4, 1024], F32, kind="ExternalOutput").ap(),
        }
    with contextlib.ExitStack() as stack:
        ctx = Ctx(nc, stack)
        pool = PsumPool([ctx.psum("ps%d" % i, [128, 512], F32) for i in range(4)])
        P["pool"] = pool
        P["psO1"] = ctx.psum("psO1", [65, 1024], F32)
        P["psO2"] = ctx.psum("psO2", [128, 1024], F32)
        c = build_common(ctx)
        P["r"] = ctx.sbuf("r", [128, 16 * DM], F32)
        P["lnfm"] = ctx.sbuf("lnfm", [128, NCH * FMW], BF16)
        ctx.warm = ([P["r"][:, i * DM:(i + 1) * DM] for i in range(3)],
                    P["x"][0:128, :])
        with ctx.scope():
            emit_part1(ctx, c, P, fused, stages)
        if "s3" in stages:
            with ctx.scope():
                emit_l0_ffn(ctx, c, P, fused)
        towns = own_tiles(z)
        for i in range(16):
            ctx.dma("gpsimd", out_r[i * 128:(i + 1) * 128, :],
                    P["r"][:, i * DM:(i + 1) * DM])
        if dbg_lnfm is not None:
            ctx.dma("gpsimd", dbg_lnfm, P["lnfm"][:])
        GD = ctx.sems["GD"]
        ctx.emit("gpsimd", "wait_ge", GD.h, GD.val)
        ctx.finalize()
    return nc


def declare_l1_inputs(nc):
    d = {}
    d["qkv_w"] = nc.dram_tensor("qkv_w", [DM, 3 * DM], F32, kind="ExternalInput").ap()
    d["qkv_b"] = nc.dram_tensor("qkv_b", [3 * DM], F32, kind="ExternalInput").ap()
    d["w1_write"] = nc.dram_tensor("w1_write", [DM, 64], F32, kind="ExternalInput").ap()
    d["w2_write"] = nc.dram_tensor("w2_write", [DM, 64], F32, kind="ExternalInput").ap()
    d["w1_read"] = nc.dram_tensor("w1_read", [DM, 64], F32, kind="ExternalInput").ap()
    d["w2_read"] = nc.dram_tensor("w2_read", [DM, 64], F32, kind="ExternalInput").ap()
    d["kv_geo_w"] = nc.dram_tensor("kv_geo_w", [DM, 2 * DM], F32, kind="ExternalInput").ap()
    d["out_w"] = nc.dram_tensor("out_w", [DM, DM], F32, kind="ExternalInput").ap()
    d["out_b"] = nc.dram_tensor("out_b", [DM], F32, kind="ExternalInput").ap()
    d["kv_gate"] = nc.dram_tensor("kv_gate", [1, 1], F32, kind="ExternalInput").ap()
    d["mem_scale"] = nc.dram_tensor("mem_scale", [1, L1H], F32, kind="ExternalInput").ap()
    d["decay_logits"] = nc.dram_tensor("decay_logits", [1, L1H], F32, kind="ExternalInput").ap()
    d["f_w1"] = nc.dram_tensor("f_w1", [4096, DM], F32, kind="ExternalInput").ap()
    d["f_w1s"] = nc.dram_tensor("f_w1s", [1, 4096], F32, kind="ExternalInput").ap()
    d["f_w2"] = nc.dram_tensor("f_w2", [4096, DM], F32, kind="ExternalInput").ap()
    d["f_w2s"] = nc.dram_tensor("f_w2s", [1, 4096], F32, kind="ExternalInput").ap()
    d["f_h1"] = nc.dram_tensor("f_h1", [4096, 4096], F32, kind="ExternalInput").ap()
    d["f_h1s"] = nc.dram_tensor("f_h1s", [1, 4096], F32, kind="ExternalInput").ap()
    d["f_h2"] = nc.dram_tensor("f_h2", [4096, 4096], F32, kind="ExternalInput").ap()
    d["f_h2s"] = nc.dram_tensor("f_h2s", [1, 4096], F32, kind="ExternalInput").ap()
    d["f_w3"] = nc.dram_tensor("f_w3", [DM, 4096], F32, kind="ExternalInput").ap()
    d["f_w3s"] = nc.dram_tensor("f_w3s", [1, DM], F32, kind="ExternalInput").ap()
    return d


PAIRS = [(di, ei) for di in range(6) for ei in range(6) if di <= ei]
SIGNS = [1.0, -1.0, 1.0, 1.0, -1.0, 1.0]


def emit_l1_prep(ctx, c, P):
    LD, GD, SD, PE, DV, AC, PO = (ctx.sems[k] for k in
                                  ("LD", "GD", "SD", "PE", "DV", "AC", "PO"))
    lnfm, pool = P["lnfm"], P["pool"]

    wbf = ctx.sbuf("l1p_w", [128, NCH * 64], BF16)
    wst = ctx.sbuf("l1p_wst", [128, 64], F32)
    pstage = ctx.sbuf("l1p_ps", [64, 512], F32)
    ptm = {}
    for nm in ("w1", "w2", "r1", "r2"):
        ptm[nm] = ctx.sbuf("l1p_" + nm, [128, NTILE * 64], BF16)
    ww = ctx.sbuf("l1p_ww", [128, NTILE * 336], BF16)
    rl2 = ctx.sbuf("l1p_rl2", [128, NTILE * 96], BF16)
    u = ctx.sbuf("l1p_u", [128, 96], F32)
    u2 = ctx.sbuf("l1p_u2", [128, 96], F32)
    n2 = ctx.sbuf("l1p_n2", [128, 16], F32)
    t16 = ctx.sbuf("l1p_t16", [128, 16], F32)
    rr = ctx.sbuf("l1p_rr", [128, 336], F32)
    mss = ctx.sbuf("l1p_mss", [128, 336], F32)
    mem = P["gg"]
    Ah = ctx.sbuf("l1p_Ah", [128, L1H * 128], BF16)
    drows = ctx.sbuf("l1p_drows", [128, L1H], BF16)
    dcols = ctx.sbuf("l1p_dcols", [1, L1H * 128], BF16)
    d128r = ctx.sbuf("l1p_d128r", [1, 336], F32)
    lrow = ctx.sbuf("l1p_lrow", [1, 64], F32)
    lndb = ctx.sbuf("l1p_lndb", [128, 16], F32)
    mup = ctx.sbuf("l1p_mup", [128, 128], F32)
    irow = ctx.sbuf("l1p_irow", [1, 128], F32)
    pcol = ctx.sbuf("l1p_pcol", [128, 1], F32)
    scr = ctx.sbuf("l1p_scr", [128, 257], F32)
    Prow = ctx.sbuf("l1p_P", [1, 336], F32)
    Prowb = ctx.sbuf("l1p_Pb", [1, 336], BF16)
    qrow = ctx.sbuf("l1p_q", [1, 336], F32)
    msb = ctx.sbuf("l1p_msb", [128, 16], F32)
    kgb = ctx.sbuf("l1p_kgb", [128, 1], F32)

    t0 = ctx.dma("sync", lrow[0:1, 0:16], P["decay_logits"])
    ts0 = ctx.act(lrow[0:1, 16:32], lrow[0:1, 0:16], AF.Sigmoid, wait=[(LD, t0)])
    tl0 = ctx.act(lrow[0:1, 32:48], lrow[0:1, 16:32], AF.Ln)
    td0 = ctx.act(lrow[0:1, 48:64], lrow[0:1, 32:48], AF.Exp, scale=128.0)
    td0 = ctx.fence_ac()
    pb, wl = pool.acquire()
    tb0 = ctx.mm(pb["ap"][:, 0:16], c.ones_rowF[:, 0:128], lrow[0:1, 32:48],
                 True, True, wait=wl + [(AC, td0)])
    tlb = ctx.act(lndb[:], pb["ap"][:, 0:16], AF.Copy, wait=[(PE, tb0)])
    tlb = ctx.fence_ac()
    pb["r"] = (AC, tlb)
    ctx.dve("tensor_scalar", mup[:], c.iota_jp[:], 0.0, None, ALU.is_gt)
    tpo = ctx.pool("iota", irow[:], [[1, 128]], channel_multiplier=0,
                   allow_small_or_imprecise_dtypes=True)
    tpc = ctx.pool("iota", pcol[:], [[1, 1]], channel_multiplier=1,
                   allow_small_or_imprecise_dtypes=True)
    for cc in range(21):
        ctx.dve("tensor_copy", d128r[0:1, cc * 16:(cc + 1) * 16], lrow[0:1, 48:64],
                wait=[(AC, tlb)] if cc == 0 else None)
    tps = ctx.dve("tensor_scalar", pcol[:], pcol[:], 128.0, None, ALU.subtract,
                  wait=[(PO, tpc)])
    tprev = tlb
    for h in range(L1H):
        tA1 = ctx.dve("tensor_scalar", scr[:, 0:128], c.iota_jp[:],
                      lndb[:, h:h + 1], None, ALU.mult, wait=[(AC, tprev)])
        tA1f = ctx.fence_dv()
        tA2 = ctx.act(scr[:, 128:256], scr[:, 0:128], AF.Exp, wait=[(DV, tA1f)])
        tA2f = ctx.fence_ac()
        tA3 = ctx.dve("tensor_tensor", Ah[:, h * 128:(h + 1) * 128], scr[:, 128:256],
                      mup[:], ALU.mult, wait=[(AC, tA2f)])
        tD1 = ctx.dve("tensor_scalar", scr[0:1, 0:128], irow[:],
                      lrow[0:1, 32 + h:33 + h], None, ALU.mult, wait=[(PO, tpo)])
        tD1f = ctx.fence_dv()
        tD2 = ctx.act(dcols[0:1, h * 128:(h + 1) * 128], scr[0:1, 0:128], AF.Exp,
                      wait=[(DV, tD1f)])
        tD3 = ctx.dve("tensor_scalar", scr[:, 256:257], pcol[:],
                      lndb[:, h:h + 1], None, ALU.mult, wait=[(DV, tps)])
        tD3f = ctx.fence_dv()
        tD4 = ctx.act(drows[:, h:h + 1], scr[:, 256:257], AF.Exp, scale=-1.0,
                      wait=[(DV, tD3f)])
        tprev = ctx.fence_ac()

    t1 = ctx.dma("sync", lrow[0:1, 0:16], P["mem_scale"])
    t2 = ctx.dma("sync", kgb[0:1, 0:1], P["kv_gate"])
    pb, wl = pool.acquire()
    tb1 = ctx.mm(pb["ap"][:, 0:16], c.ones_rowF[:, 0:128], lrow[0:1, 0:16],
                 True, False, wait=wl + [(LD, t2), (AC, tprev)])
    tb1 = ctx.mm(pb["ap"][:, 16:17], c.ones_rowF[:, 0:128], kgb[0:1, 0:1],
                 False, True)
    tmsb = ctx.act(msb[:], pb["ap"][:, 0:16], AF.Copy, wait=[(PE, tb1)])
    tkgb = ctx.act(kgb[:], pb["ap"][:, 16:17], AF.Copy)
    tkgf = ctx.fence_ac()
    pb["r"] = (AC, tkgf)

    wsrc = {"w1": "w1_write", "w2": "w2_write", "r1": "w1_read", "r2": "w2_read"}
    wbf_rd = None
    for nm in ("w1", "w2", "r1", "r2"):
        lastc = None
        for cc in range(NCH):
            tl = ctx.dma("sync", wst[:], P[wsrc[nm]][cc * 128:(cc + 1) * 128, :],
                         wait=([wbf_rd] if (wbf_rd and cc == 0) else None))
            wv = wbf[:, cc * 64:(cc + 1) * 64].rearrange("p (pd h) -> p h pd", h=16)
            sv = wst[:].rearrange("p (h pd) -> p h pd", pd=PD)
            lastc = ctx.dve("tensor_copy", wv, sv, wait=[(LD, tl)])
        shift = -1 if nm == "w1" else 0
        for tc in range(8):
            pb, wl = pool.acquire()
            for cc in range(NCH):
                col0 = fm_col(cc, tc * 512) + shift
                w0 = (wl + [(DV, lastc)]) if cc == 0 else None
                tmm = ctx.mm(pb["ap"][0:64, :], wbf[:, cc * 64:(cc + 1) * 64],
                             lnfm[:, col0: col0 + 512], cc == 0, cc == NCH - 1,
                             wait=w0)
            tev = ctx.act(pstage[:], pb["ap"][0:64, :], AF.Copy, wait=[(PE, tmm)])
            tevf = ctx.fence_ac()
            pb["r"] = (AC, tevf)
            for j in range(4):
                g = tc * 4 + j
                pt, wlt = pool.acquire()
                tt = ctx.trans(pt["ap"][0:128, 0:64],
                               pstage[:, j * 128:(j + 1) * 128],
                               c.identF[0:64, 0:64], wait=wlt + [(AC, tevf)])
                te2 = ctx.act(ptm[nm][:, g * 64:(g + 1) * 64], pt["ap"][0:128, 0:64],
                              AF.Copy, wait=[(PE, tt)])
                pt["r"] = (AC, te2)
        wbf_rd = (PE, PE.val)

    tfa = ctx.fence_ac()
    for g in range(NTILE):
        for which in range(2):
            p1 = ptm["w1"] if which == 0 else ptm["r1"]
            p2 = ptm["w2"] if which == 0 else ptm["r2"]
            for cc in range(6):
                pi, pj = (0, 0, 0, 1, 1, 2)[cc], (1, 2, 3, 2, 3, 3)[cc]
                w0 = [(AC, tfa)] if cc == 0 else None
                ctx.dve("tensor_tensor", u[:, cc * 16:(cc + 1) * 16],
                        p1[:, g * 64 + pi * 16: g * 64 + (pi + 1) * 16],
                        p2[:, g * 64 + pj * 16: g * 64 + (pj + 1) * 16], ALU.mult,
                        wait=w0)
                ctx.dve("tensor_tensor", u2[:, cc * 16:(cc + 1) * 16],
                        p1[:, g * 64 + pj * 16: g * 64 + (pj + 1) * 16],
                        p2[:, g * 64 + pi * 16: g * 64 + (pi + 1) * 16], ALU.mult)
                ctx.dve("tensor_tensor", u[:, cc * 16:(cc + 1) * 16],
                        u[:, cc * 16:(cc + 1) * 16],
                        u2[:, cc * 16:(cc + 1) * 16], ALU.subtract)
                ctx.dve("tensor_tensor", u2[:, cc * 16:(cc + 1) * 16],
                        u[:, cc * 16:(cc + 1) * 16],
                        u[:, cc * 16:(cc + 1) * 16], ALU.mult)
                if cc == 0:
                    ctx.dve("tensor_copy", n2[:], u2[:, 0:16])
                else:
                    ctx.dve("tensor_tensor", n2[:], n2[:],
                            u2[:, cc * 16:(cc + 1) * 16], ALU.add)
            ctx.dve("tensor_scalar", n2[:], n2[:], 1e-24, None, ALU.max)
            tnf = ctx.fence_dv()
            tln = ctx.act(t16[:], n2[:], AF.Ln, wait=[(DV, tnf)])
            tex = ctx.act(n2[:], t16[:], AF.Exp, scale=-0.5)
            tfa = ctx.fence_ac()
            for cc in range(6):
                if which == 0:
                    ctx.dve("tensor_tensor", u[:, cc * 16:(cc + 1) * 16],
                            u[:, cc * 16:(cc + 1) * 16], n2[:], ALU.mult,
                            wait=[(AC, tfa)] if cc == 0 else None)
                else:
                    ctx.dve("scalar_tensor_tensor",
                            rl2[:, g * 96 + cc * 16: g * 96 + (cc + 1) * 16],
                            u[:, cc * 16:(cc + 1) * 16], SIGNS[cc], n2[:],
                            op0=ALU.mult, op1=ALU.mult,
                            wait=[(AC, tfa)] if cc == 0 else None)
            if which == 0:
                ctx.dve("tensor_scalar", u2[:, 0:96], u[:, 0:96], 2.0, None, ALU.mult)
                for ci, (di, ei) in enumerate(PAIRS):
                    src1 = u2[:, di * 16:(di + 1) * 16] if di < ei else \
                        u[:, di * 16:(di + 1) * 16]
                    ctx.dve("tensor_tensor",
                            ww[:, g * 336 + ci * 16: g * 336 + (ci + 1) * 16],
                            src1, u[:, ei * 16:(ei + 1) * 16], ALU.mult)
    tprep = ctx.fence_dv()

    ctx.dve("memset", Prow[:], 0.0)
    ctx.dve("memset", Prowb[:], 0.0)
    tPb = ctx.fence_dv()
    for g in range(NTILE):
        wwg = ww[:, g * 336:(g + 1) * 336]
        pb, wl = pool.acquire()
        for h in range(L1H):
            rhs = wwg.rearrange("p (cc h) -> p cc h", h=16)[:, :, h]
            ctx.mm(pb["ap"][0:1, h * 21:(h + 1) * 21], drows[:, h:h + 1], rhs,
                   h == 0, h == L1H - 1,
                   wait=(wl + [(DV, tprep)]) if h == 0 else None)
        qv = qrow[:].rearrange("p (cc h) -> p h cc", h=16)
        tq = ctx.dve("tensor_copy", qv, pb["ap"][0:1, 0:336].rearrange(
            "p (h cc) -> p h cc", cc=21), wait=[(PE, PE.val)])
        pb["r"] = (DV, tq)
        pm, wlm = pool.acquire()
        for h in range(L1H):
            wA = (wlm + [(DV, tPb)]) if h == 0 else None
            ctx.mm(pm["ap"][:, h * 21:(h + 1) * 21],
                   Ah[:, h * 128:(h + 1) * 128],
                   wwg.rearrange("p (cc h) -> p cc h", h=16)[:, :, h],
                   True, False, wait=wA)
            ctx.mm(pm["ap"][:, h * 21:(h + 1) * 21],
                   dcols[0:1, h * 128:(h + 1) * 128],
                   Prowb[:].rearrange("p (cc h) -> p h cc", h=16)[:, h, :],
                   False, True)
        tpm = ctx.sems["PE"].val
        rlg = rl2[:, g * 96:(g + 1) * 96]
        ctx.dve("tensor_scalar", u2[:, 0:96], rlg, 2.0, None, ALU.mult,
                wait=[(DV, tq)])
        for ci, (di, ei) in enumerate(PAIRS):
            src1 = u2[:, di * 16:(di + 1) * 16] if di < ei else \
                rlg[:, di * 16:(di + 1) * 16]
            ctx.dve("tensor_tensor", rr[:, ci * 16:(ci + 1) * 16],
                    src1, rlg[:, ei * 16:(ei + 1) * 16], ALU.mult)
        for h in range(L1H):
            rrh = rr[:].rearrange("p (cc h) -> p cc h", h=16)[:, :, h]
            ctx.dve("tensor_tensor", mss[:, h * 21:(h + 1) * 21],
                    rrh, pm["ap"][:, h * 21:(h + 1) * 21], ALU.mult,
                    wait=[(PE, tpm)] if h == 0 else None)
        for h in range(L1H):
            ctx.dve("reduce_sum", mem[:, g * 16 + h: g * 16 + h + 1],
                    mss[:, h * 21:(h + 1) * 21], axis=AX.X)
        tms = ctx.sems["DV"].val
        pm["r"] = (DV, tms)
        ctx.dve("tensor_tensor", Prow[:], Prow[:], d128r[:], ALU.mult,
                wait=[(PE, tpm)])
        ctx.dve("tensor_tensor", Prow[:], Prow[:], qrow[:], ALU.add)
        ctx.dve("tensor_copy", Prowb[:], Prow[:])
        tPb = ctx.fence_dv()
    for g in range(NTILE):
        cols = slice(g * 16, (g + 1) * 16)
        ctx.dve("tensor_tensor", mem[:, cols], mem[:, cols], msb[:], ALU.mult,
                wait=[(DV, tPb)] if g == 0 else None)
        tg1f = ctx.fence_dv()
        ctx.act(mem[:, cols], mem[:, cols], AF.Sigmoid, wait=[(DV, tg1f)])
        tg2f = ctx.fence_ac()
        ctx.dve("tensor_scalar", mem[:, cols], mem[:, cols], kgb[:, 0:1],
                None, ALU.mult, wait=[(AC, tg2f)])
        tg3f = ctx.fence_dv()
        pt, wlt = pool.acquire()
        tt = ctx.trans(pt["ap"][0:16, 0:128], mem[:, cols], c.identF[:],
                       wait=wlt + [(DV, tg3f)])
        lastgg = ctx.act(P["ggfm"][:, g * 128:(g + 1) * 128], pt["ap"][0:16, 0:128],
                         AF.Copy, wait=[(PE, tt)])
        pt["r"] = (AC, lastgg)
    ctx.barrier()


def emit_l1_attn(ctx, c, P):
    LD, GD, SD, PE, DV, AC = (ctx.sems[k] for k in
                              ("LD", "GD", "SD", "PE", "DV", "AC"))
    z = P["z"]
    towns = own_tiles(z)
    r, lnfm, pool = P["r"], P["lnfm"], P["pool"]
    psO1 = P["psO1"]
    SCALE = 0.125

    kT = ctx.sbuf("l1a_k", [65, T], BF16)
    qT = ctx.sbuf("l1a_q", [65, 2048], BF16)
    vtm = ctx.sbuf("l1a_v", [128, NTILE * 65], BF16)
    wbf = ctx.sbuf("l1a_w", [128, NCH * 64 * 5], BF16)
    wstW = ctx.sbuf("l1a_wst", [128, 1024], F32)
    wout = ctx.sbuf("l1a_wout", [64, DM], BF16)
    bq = ctx.sbuf("l1a_bq", [64, 4], F32)
    brow = ctx.sbuf("l1a_brow", [1, 64], F32)
    obrow = ctx.sbuf("l1a_obrow", [1, DM], F32)
    kf = ctx.sbuf("l1a_kf", [64, 512], F32)
    kg = ctx.sbuf("l1a_kg", [64, 512], F32)
    ptb = [ctx.sbuf("l1a_ptb", [128, 1024], BF16),
           ctx.sbuf("l1a_ptb1", [128, 1024], BF16)]
    obf = ctx.sbuf("l1a_obf", [64, 1024], BF16)
    sqq = ptb[0][0:64, 0:512]
    km = ctx.sbuf("l1a_km", [1, 8], F32)
    rs_s = ctx.sbuf("l1a_rs", [65, 1024], F32)
    brow_bf = ctx.sbuf("l1a_brbf", [1, 512], BF16)
    rncols = ctx.sbuf("l1a_rn", [128, 8], F32)
    rntmp = ctx.sbuf("l1a_rnt", [128, 2], F32)
    ggrow = ctx.sbuf("l1a_ggrow", [1, T], BF16)

    vtm_r = vtm[:].rearrange("p (t w) -> p t w", w=65)
    ctx.dve("memset", vtm_r[:, :, 64:65], 1.0)
    ctx.dve("memset", kT[64:65, :], 1.0)
    tob = ctx.dma("sync", obrow[:], P["out_b"][None, :])

    halves = [hf for hf in q_halves(z) if hf[1]]
    ggrow_rd = [None]
    wbuf_rd = None
    kq_dv = [None]
    v_ac = [None]
    psO_rd = [[]]
    pv_done = [None, None]

    LL2 = (LD, ctx.sems["L1"])
    wst_rd = [None, None]
    for h in range(L1H):
        ctx.stage = "Bw%d" % h
        col_of = {0: h * 64, 1: DM + h * 64, 2: 2 * DM + h * 64}
        lastc = None
        for p5 in range(5):
            sl = p5 % 2
            if p5 < 3:
                src = P["qkv_w"][:, col_of[p5]: col_of[p5] + 64]
            else:
                base = (p5 - 3) * DM + h * 64
                src = P["kv_geo_w"][:, base: base + 64]
            src3 = src.rearrange("(c p) f -> p c f", p=128)
            dst3 = wstW[:, sl * 512:(sl + 1) * 512].rearrange("p (c f) -> p c f",
                                                              f=64)
            wl0 = [wst_rd[sl]] if wst_rd[sl] is not None else None
            tl = ctx.dma("sync", dst3, src3, wait=wl0, lane=sl)
            wl2 = [(LL2[sl], tl)]
            if wbuf_rd is not None and p5 == 0:
                wl2.append(wbuf_rd)
            lastc = ctx.dve("tensor_copy", wbf[:, p5 * 512:(p5 + 1) * 512],
                            wstW[:, sl * 512:(sl + 1) * 512], wait=wl2)
            wst_rd[sl] = (DV, lastc)
        for qc in range(2):
            sl = qc
            wl0 = [wst_rd[sl]] if wst_rd[sl] is not None else None
            tlo = ctx.dma("sync", wstW[0:64, sl * 512:(sl + 1) * 512],
                          P["out_w"][h * 64:(h + 1) * 64, qc * 512:(qc + 1) * 512],
                          wait=wl0, lane=sl)
            lastc2 = ctx.dve("tensor_copy", wout[:, qc * 512:(qc + 1) * 512],
                             wstW[0:64, sl * 512:(sl + 1) * 512],
                             wait=[(LL2[sl], tlo)])
            wst_rd[sl] = (DV, lastc2)
        wgg = [ggrow_rd[0]] if ggrow_rd[0] is not None else []
        tggr = ctx.dma("gpsimd", ggrow[:], P["ggfm"][h:h + 1, :], wait=wgg or None)
        pb, wl = pool.acquire()
        for j in range(3):
            tbl = ctx.dma("sync", brow[:],
                          P["qkv_b"][None, j * DM + h * 64: j * DM + (h + 1) * 64])
            tt = ctx.trans(pb["ap"][0:64, j:j + 1], brow[:],
                           c.ones11f[0:1, 0:1],
                           wait=(wl if j == 0 else []) + [(LD, tbl)])
        tbq = ctx.act(bq[:, 0:3], pb["ap"][0:64, 0:3], AF.Copy, wait=[(PE, tt)])
        tbqf = ctx.fence_ac()
        pb["r"] = (AC, tbqf)

        ctx.stage = "Bkv%d" % h
        lastk = None
        lastv = None
        for tc in range(8):
            cols = slice(tc * 512, (tc + 1) * 512)
            pk, wlk = pool.acquire()
            for cc in range(NCH):
                w0 = (wlk + [(DV, lastc), (AC, tbqf)]) if cc == 0 else None
                tmm = ctx.mm(pk["ap"][0:64, :],
                             wbf[:, (1 * NCH + cc) * 64:(1 * NCH + cc + 1) * 64],
                             lnfm[:, fm_col(cc, tc * 512): fm_col(cc, tc * 512) + 512],
                             cc == 0, cc == NCH - 1, wait=w0)
            tkf = ctx.dve("tensor_scalar", kf[:], pk["ap"][0:64, :], bq[0:64, 1:2],
                          None, ALU.add, wait=[(PE, tmm)])
            tkff = ctx.fence_dv()
            pk["r"] = (DV, tkff)
            pg, wlg = pool.acquire()
            for cc in range(NCH):
                w0 = wlg if cc == 0 else None
                tmm = ctx.mm(pg["ap"][0:64, :],
                             wbf[:, (3 * NCH + cc) * 64:(3 * NCH + cc + 1) * 64],
                             lnfm[:, fm_col(cc, tc * 512): fm_col(cc, tc * 512) + 512],
                             cc == 0, cc == NCH - 1, wait=w0)
            tkg = ctx.act(kg[:], pg["ap"][0:64, :], AF.Copy, wait=[(PE, tmm)])
            tkgf2 = ctx.fence_ac()
            pg["r"] = (AC, tkgf2)
            pgg, wlgg = pool.acquire()
            tggb = ctx.mm(pgg["ap"][0:64, :], c.ones_row[0:1, 0:64],
                          ggrow[0:1, cols], True, True,
                          wait=wlgg + [(AC, tkgf2), (GD, tggr)])
            tke1 = ctx.dve("tensor_tensor", kg[:], kg[:], pgg["ap"][0:64, :],
                           ALU.mult, wait=[(PE, tggb), (AC, tkgf2), (DV, tkff)])
            lastk = ctx.dve("tensor_tensor", kT[0:64, cols], kf[:], kg[:], ALU.add)
            pgg["r"] = (DV, lastk)
            pv, wlv = pool.acquire()
            for cc in range(NCH):
                w0 = (wlv + [(DV, lastk)]) if cc == 0 else None
                tmm = ctx.mm(pv["ap"][0:64, :],
                             wbf[:, (2 * NCH + cc) * 64:(2 * NCH + cc + 1) * 64],
                             lnfm[:, fm_col(cc, tc * 512): fm_col(cc, tc * 512) + 512],
                             cc == 0, cc == NCH - 1, wait=w0)
            tvf = ctx.dve("tensor_scalar", kf[:], pv["ap"][0:64, :], bq[0:64, 2:3],
                          None, ALU.add, wait=[(PE, tmm)])
            pv["r"] = (DV, tvf)
            pvg, wlvg = pool.acquire()
            for cc in range(NCH):
                w0 = (wlvg + [(DV, tvf)]) if cc == 0 else None
                tmm = ctx.mm(pvg["ap"][0:64, :],
                             wbf[:, (4 * NCH + cc) * 64:(4 * NCH + cc + 1) * 64],
                             lnfm[:, fm_col(cc, tc * 512): fm_col(cc, tc * 512) + 512],
                             cc == 0, cc == NCH - 1, wait=w0)
            tvg = ctx.act(kg[:], pvg["ap"][0:64, :], AF.Copy, wait=[(PE, tmm)])
            tvgf = ctx.fence_ac()
            pvg["r"] = (AC, tvgf)
            pgg2, wlgg2 = pool.acquire()
            tggb2 = ctx.mm(pgg2["ap"][0:64, :], c.ones_row[0:1, 0:64],
                           ggrow[0:1, cols], True, True,
                           wait=wlgg2 + [(AC, tvgf), (GD, tggr)])
            ctx.dve("tensor_tensor", kg[:], kg[:], pgg2["ap"][0:64, :],
                    ALU.mult, wait=[(PE, tggb2), (AC, tvgf)])
            tven = ctx.dve("tensor_tensor", kf[:], kf[:], kg[:], ALU.add)
            tvenf = ctx.fence_dv()
            pgg2["r"] = (DV, tven)
            for j in range(4):
                g = tc * 4 + j
                pt, wlt = pool.acquire()
                tt = ctx.trans(pt["ap"][0:128, 0:64], kf[:, j * 128:(j + 1) * 128],
                               c.identF[0:64, 0:64], wait=wlt + [(DV, tvenf)])
                lastv = ctx.act(vtm[:, g * 65: g * 65 + 64], pt["ap"][0:128, 0:64],
                                AF.Copy, wait=[(PE, tt)])
                pt["r"] = (AC, lastv)
        v_ac[0] = ctx.fence_ac()
        ctx.stage = "Bq%d" % h
        ranges = [(0, 1024), (3072, 4096)] if z == 0 else [(1024, 3072)]
        lastq = None
        qofs = 0
        for (lo, hi) in ranges:
            for t0 in range(lo, hi, 512):
                pq, wlq = pool.acquire()
                for cc in range(NCH):
                    w0 = wlq if cc == 0 else None
                    tmm = ctx.mm(pq["ap"][0:64, :],
                                 wbf[:, (0 * NCH + cc) * 64:(0 * NCH + cc + 1) * 64],
                                 lnfm[:, fm_col(cc, t0): fm_col(cc, t0) + 512],
                                 cc == 0, cc == NCH - 1, wait=w0)
                lastq = ctx.dve("tensor_scalar", qT[0:64, qofs:qofs + 512],
                                pq["ap"][0:64, :], bq[0:64, 0:1],
                                None, ALU.add, wait=[(PE, tmm)])
                pq["r"] = (DV, lastq)
                qofs += 512
        tqf = ctx.fence_dv()
        ctx.stage = "Bb%d" % h
        for tc in range(8):
            cols = slice(tc * 512, (tc + 1) * 512)
            tsq = ctx.act(sqq[:], kT[0:64, cols], AF.Square, wait=[(DV, lastk)])
            pb, wl = pool.acquire()
            tmm = ctx.mm(pb["ap"][0:1, :], c.ones_col[0:64, :], sqq[:], True, True,
                         wait=wl + [(AC, tsq)])
            tev = ctx.dve("reduce_max", km[0:1, 1:2] if tc > 0 else km[0:1, 0:1],
                          pb["ap"][0:1, :], axis=AX.X, wait=[(PE, tmm)])
            if tc > 0:
                tev = ctx.dve("tensor_tensor", km[0:1, 0:1], km[0:1, 0:1],
                              km[0:1, 1:2], ALU.max)
            pb["r"] = (DV, tev)
        ctx.dve("tensor_scalar", km[0:1, 1:2], km[0:1, 0:1], 1e-38, None, ALU.add)
        tkmf = ctx.fence_dv()
        tl1 = ctx.act(km[0:1, 2:3], km[0:1, 1:2], AF.Ln, wait=[(DV, tkmf)])
        tk2 = ctx.act(km[0:1, 3:4], km[0:1, 2:3], AF.Exp, scale=0.5)
        tBg = None
        for tc in range(4):
            cols = slice(tc * 512, (tc + 1) * 512)
            tsq = ctx.act(sqq[:], qT[0:64, cols], AF.Square, wait=[(DV, tqf)])
            pb, wl = pool.acquire()
            tmm = ctx.mm(pb["ap"][0:1, :], c.ones_col[0:64, :], sqq[:], True, True,
                         wait=wl + [(AC, tsq)])
            td = ctx.dve("tensor_scalar", rs_s[0:1, 0:512], pb["ap"][0:1, :],
                         1e-38, None, ALU.add, wait=[(PE, tmm)])
            tdf = ctx.fence_dv()
            ta = ctx.act(rs_s[0:1, 512:1024], rs_s[0:1, 0:512], AF.Ln,
                         wait=[(DV, tdf)])
            ta = ctx.act(rs_s[0:1, 0:512], rs_s[0:1, 512:1024], AF.Exp, scale=0.5)
            ta = ctx.fence_ac()
            wB = [(AC, ta)]
            if tBg is not None:
                wB.append((GD, tBg))
            tB = ctx.dve("tensor_scalar", brow_bf[:], rs_s[0:1, 0:512],
                         km[0:1, 3:4], -1.0, ALU.mult, ALU.mult, wait=wB)
            tB = ctx.fence_dv()
            tBg = ctx.dma("gpsimd", qT[64:65, cols], brow_bf[:], wait=[(DV, tB)])
            pb["r"] = (DV, tB)
        kq_dv[0] = ctx.fence_dv()

        ctx.stage = "Bs%d" % h
        qbase = {}
        qofs = 0
        for (lo, hi) in ranges:
            qbase[lo // 1024] = qofs
            if hi - lo == 2048:
                qbase[lo // 1024 + 1] = qofs + 1024
            qofs += hi - lo
        for (tiles, _own) in halves:
            t0g = tiles[0]
            lastkt = tiles[-1]
            qb = qbase[t0g // 8]
            chunk_lastkt = [min(lastkt, t0g + (qc * 512 + 511) // 128)
                            for qc in range(2)]
            for kt in range(0, lastkt + 1):
                base = max(0, (kt - t0g) * 128)
                slot = kt % 2
                pt = ptb[slot]
                wz = [pv_done[slot]] if pv_done[slot] is not None else []
                tz = None
                if base > 0:
                    z0c = max(0, base - 256)
                    tz = ctx.dve("memset", pt[:, z0c:base], 0.0, wait=wz or None)
                exp_toks = []
                for qc in range(2):
                    if kt > chunk_lastkt[qc]:
                        continue
                    lo2 = max(base, qc * 512)
                    hi2 = (qc + 1) * 512
                    if lo2 >= hi2:
                        continue
                    pb, wl = pool.acquire()
                    w0 = wl + [(DV, kq_dv[0]), (GD, tBg)]
                    tmm = ctx.mm(pb["ap"][:, 0:hi2 - lo2], kT[:, kt * 128:(kt + 1) * 128],
                                 qT[:, qb + lo2: qb + hi2], True, True, wait=w0)
                    wexp = [(PE, tmm)]
                    if kt in tiles and lo2 == base:
                        tmsk = ctx.dve("tensor_tensor", pb["ap"][:, 0:128],
                                       pb["ap"][:, 0:128], c.mask_neg[:], ALU.add,
                                       wait=[(PE, tmm)])
                        wexp.append((DV, tmsk))
                    if tz is not None:
                        wexp.append((DV, tz))
                    elif wz:
                        wexp += wz
                    texp = ctx.act(pt[:, lo2:hi2], pb["ap"][:, 0:hi2 - lo2], AF.Exp,
                                   scale=SCALE, wait=wexp)
                    pb["r"] = (AC, texp)
                    exp_toks.append(texp)
                wpv = [(AC, tk) for tk in exp_toks]
                if kt == 0:
                    wpv = wpv + psO_rd[0] + [(AC, v_ac[0])]
                lastpv = None
                for qc in range(2):
                    if kt > chunk_lastkt[qc]:
                        continue
                    cols = slice(qc * 512, (qc + 1) * 512)
                    st_, sp_ = (kt == 0), (kt == chunk_lastkt[qc])
                    lastpv = ctx.mm(psO1[0:65, cols], vtm[:, kt * 65: kt * 65 + 65],
                                    pt[:, cols], st_, sp_,
                                    wait=wpv if qc == 0 else None)
                pv_done[slot] = (PE, lastpv)
            t_rs = ctx.dve("tensor_copy", rs_s[64:65, 0:1024], psO1[64:65, :],
                           wait=[(PE, PE.val)])
            to1 = ctx.act(obf[:], psO1[0:64, :], AF.Copy, wait=[(PE, PE.val)])
            to1 = ctx.fence_ac()
            psO_rd[0] = [(AC, to1), (DV, t_rs)]
            prev_ta2 = None
            for ti in range(8):
                pb, wl = pool.acquire()
                tmmr = ctx.mm(pb["ap"][:, 0:1], rs_s[64:65, ti * 128:(ti + 1) * 128],
                              c.ones11f[64:65, :], True, True, wait=wl + [(DV, t_rs)])
                wtd = [(PE, tmmr)]
                if prev_ta2 is not None:
                    wtd.append((AC, prev_ta2))
                td1 = ctx.dve("tensor_scalar", rntmp[:, 1:2], pb["ap"][:, 0:1],
                              1e-38, None, ALU.add, wait=wtd)
                td1f = ctx.fence_dv()
                ta1 = ctx.act(rntmp[:, 0:1], rntmp[:, 1:2], AF.Ln, wait=[(DV, td1f)])
                ta2 = ctx.act(rncols[:, ti:ti + 1], rntmp[:, 0:1], AF.Exp, scale=-1.0)
                ta2 = ctx.fence_ac()
                prev_ta2 = ta2
                pb["r"] = (AC, ta2)
            for ti in range(8):
                g = tiles[ti]
                oi = towns.index(g)
                for qc in range(2):
                    pb, wl = pool.acquire()
                    tmm = ctx.mm(pb["ap"][:], obf[:, ti * 128:(ti + 1) * 128],
                                 wout[:, qc * 512:(qc + 1) * 512], True, True,
                                 wait=wl + [(AC, AC.val), (DV, lastc2)])
                    tev = ctx.dve("scalar_tensor_tensor",
                                  r[:, oi * DM + qc * 512: oi * DM + (qc + 1) * 512],
                                  pb["ap"][:], rncols[:, ti:ti + 1],
                                  r[:, oi * DM + qc * 512: oi * DM + (qc + 1) * 512],
                                  op0=ALU.mult, op1=ALU.add, wait=[(PE, tmm)])
                    pb["r"] = (DV, tev)
        wbuf_rd = (PE, PE.val)
        ggrow_rd[0] = (PE, PE.val)
    tdvall = ctx.fence_dv()
    for oi in range(16):
        for qc in range(2):
            pb, wl = pool.acquire()
            tmm = ctx.mm(pb["ap"][:], c.ones_rowF[:, 0:128],
                         obrow[0:1, qc * 512:(qc + 1) * 512], True, True,
                         wait=wl + [(LD, tob), (DV, tdvall)])
            tadd = ctx.dve("tensor_tensor",
                           r[:, oi * DM + qc * 512: oi * DM + (qc + 1) * 512],
                           r[:, oi * DM + qc * 512: oi * DM + (qc + 1) * 512],
                           pb["ap"][:], ALU.add, wait=[(PE, tmm)])
            pb["r"] = (DV, tadd)
    ctx.barrier()


def emit_wprep(ctx, c, P):
    LD, PE, AC = (ctx.sems[k] for k in ("LD", "PE", "AC"))
    LN = [ctx.sems[k] for k in ("LD", "L1", "L2", "L3")]
    GN = [ctx.sems[k] for k in ("GD", "G1")]
    pool = P["pool"]
    st_in = [ctx.sbuf("wp_in%d" % i, [128, 2048], F32) for i in range(2)]
    st_out = [ctx.sbuf("wp_out%d" % i, [128, 2048], BF16) for i in range(2)]
    rd_in = [[None] * 4, [None] * 4]
    wr_out = [None, None]
    rd_out = [[None, None], [None, None]]
    sts = []
    for (wname, dname, K, NO) in (("f_w1", "w1t", DM, 4096), ("f_w2", "w2t", DM, 4096),
                                  ("f_h1", "h1t", 4096, 4096), ("f_h2", "h2t", 4096, 4096),
                                  ("f_w3", "w3t", 4096, DM)):
        for so in range(NO // 512):
            for si in range(K // 512):
                sts.append((P[wname], P[dname], K, NO, so, si))
    tins = {}
    for sti, (w, scr, K, NO, so, si) in enumerate(sts):
        bi = sti % 2
        for a in range(4):
            wl = [rd_in[bi][a]] if rd_in[bi][a] is not None else None
            tins[(sti, a)] = ctx.dma("sync", st_in[bi][:, a * 512:(a + 1) * 512],
                                     w[so * 512 + a * 128: so * 512 + (a + 1) * 128,
                                       si * 512:(si + 1) * 512],
                                     wait=wl, lane=a)
        first_ev = True
        for a in range(4):
            for bb in range(4):
                pt, wlt = pool.acquire()
                wmt = wlt + [(LN[a], tins[(sti, a)])]
                tt = ctx.trans(pt["ap"][0:128, 0:128],
                               st_in[bi][:, a * 512 + bb * 128: a * 512 + (bb + 1) * 128],
                               c.identF[:], wait=wmt)
                if bb == 3:
                    rd_in[bi][a] = (PE, tt)
                wev = [(PE, tt)]
                if first_ev:
                    for g2 in range(2):
                        if rd_out[bi][g2] is not None:
                            wev.append(rd_out[bi][g2])
                    first_ev = False
                lastev = ctx.act(st_out[bi][:, bb * 512 + a * 128: bb * 512 + (a + 1) * 128],
                                 pt["ap"][0:128, 0:128], AF.Copy,
                                 wait=wev)
                pt["r"] = (AC, lastev)
        wr_out[bi] = lastev
        for oq in range(4):
            ov = st_out[bi][:].rearrange("p (b w) -> p b w", w=512)[:, :, oq * 128:(oq + 1) * 128]
            dst = scr[so * 4 + oq, si * 512:(si + 1) * 512, :].rearrange(
                "(b p) o -> p b o", p=128)
            g2 = oq % 2
            tw = ctx.dma("gpsimd", dst, ov,
                         wait=[(AC, lastev)] if oq < 2 else None, lane=g2)
            rd_out[bi][g2] = (GN[g2], tw)
    ctx.barrier()


def emit_l1_ffn(ctx, c, P):
    LD, SD, GD, PE, DV, AC = (ctx.sems[k] for k in
                              ("LD", "SD", "GD", "PE", "DV", "AC"))
    z = P["z"]
    towns = own_tiles(z)
    pool = P["pool"]
    rdram = P["rdram"]
    out = P["out"]

    pool.n_rot = 4
    ln4blk = ctx.sbuf("f_ln4", [128, NCH * 512], BF16)
    g_fm = ctx.sbuf("f_g", [128, 32 * 512], BF16)
    h_fm = ctx.sbuf("f_h", [128, 32 * 512], BF16)
    slab = [ctx.sbuf("f_sl%d" % i, [128, 32 * 128], BF16) for i in range(4)]
    scol = ctx.sbuf("f_scol", [128, 32 * 5], F32)
    srow = ctx.sbuf("f_srow", [1, 4096], F32)
    ev1 = ctx.sbuf("f_ev1", [128, 512], F32)
    outst = ctx.sbuf("f_outst", [128, 4 * 1024], F32)
    rst = ctx.sbuf("f_rst", [128, 1024], F32)

    tprev = None
    for si, nm in enumerate(("f_w1s", "f_w2s", "f_h1s", "f_h2s", "f_w3s")):
        n = 4096 if si < 4 else DM
        tl = ctx.dma("sync", srow[0:1, 0:n], P[nm])
        for ccb in range(n // 128):
            pt, wlt = pool.acquire()
            tt = ctx.trans(pt["ap"][0:128, 0:1],
                           srow[0:1, ccb * 128:(ccb + 1) * 128],
                           c.ones11f[0:1, 0:1], wait=wlt + [(LD, tl)])
            tprev = ctx.act(scol[:, si * 32 + ccb: si * 32 + ccb + 1],
                            pt["ap"][0:128, 0:1], AF.Copy, wait=[(PE, tt)])
            pt["r"] = (AC, tprev)
    tscf = ctx.fence_ac()

    slab_rd = {}
    LSEM = {0: LD, 1: ctx.sems["L1"], 2: SD, 3: ctx.sems["S1"]}

    def slab_guard(*keys):
        vs = [slab_rd[k][1] for k in keys if k in slab_rd]
        return [(PE, max(vs))] if vs else []
    nblk = 4
    ln4_tok = None
    for blk in range(nblk):
        ctx.stage = "Fb%d" % blk
        tb0 = blk * 512
        tl4 = None
        for cc in range(NCH):
            wl4 = [ln4_tok] if (ln4_tok is not None and cc == 0) else None
            tl4 = ctx.dma("gpsimd", ln4blk[:, cc * 512:(cc + 1) * 512],
                          P["ln4_dram"][cc, :, tb0:tb0 + 512], wait=wl4)
        for og in range(16):
            pp = og % 2
            co = pp * 1024
            toks = []
            for j, wnm in ((0, "w1t"), (1, "w1t"), (2, "w2t"), (3, "w2t")):
                oc = og * 2 + (j % 2)
                wls = slab_guard((j, pp), ("f", j))
                toks.append(ctx.dma(
                    "sync" if j < 2 else "scalar",
                    slab[j][:, co:co + 1024].rearrange("p (cc o) -> p cc o", o=128),
                    P[wnm][oc].rearrange("(cc p) o -> p cc o", p=128),
                    wait=wls or None, lane=j % 2))
            pss = []
            for j in range(4):
                ps, wlp = pool.acquire()
                for cc in range(NCH):
                    w0 = (wlp + [(LSEM[j], toks[j]), (AC, tscf), (GD, tl4)]) \
                        if cc == 0 else None
                    tmm = ctx.mm(ps["ap"][:], slab[j][:, co + cc * 128: co + (cc + 1) * 128],
                                 ln4blk[:, cc * 512:(cc + 1) * 512],
                                 cc == 0, cc == NCH - 1, wait=w0)
                slab_rd[(j, pp)] = (PE, tmm)
                pss.append((ps, tmm))
            prev_t3 = None
            for jj in range(2):
                oc = og * 2 + jj
                psa, ta = pss[jj]
                psb, tb = pss[2 + jj]
                w1w = [(PE, tb)]
                if prev_t3 is not None:
                    w1w.append((DV, prev_t3))
                t1 = ctx.act(ev1[:], psa["ap"][:], AF.Silu,
                             scale=scol[:, 0 * 32 + oc:0 * 32 + oc + 1],
                             wait=w1w)
                t1f = ctx.fence_ac()
                psa["r"] = (AC, t1f)
                t2 = ctx.dve("tensor_scalar", g_fm[:, oc * 512:(oc + 1) * 512],
                             psb["ap"][:], scol[:, 1 * 32 + oc:1 * 32 + oc + 1],
                             None, ALU.mult, wait=[(AC, t1f)])
                t3 = ctx.dve("tensor_tensor", g_fm[:, oc * 512:(oc + 1) * 512],
                             g_fm[:, oc * 512:(oc + 1) * 512], ev1[:], ALU.mult)
                prev_t3 = t3
                psb["r"] = (DV, t3)
        tgf = ctx.fence_dv()
        if P.get("dbg_ffn") is not None and blk == 0:
            ctx.dma("gpsimd", P["dbg_ffn"]["g"], g_fm[:], wait=[(DV, tgf)])
            ctx.barrier()
        for (wnm, sidx, src, dst) in (("h1t", 2, g_fm, h_fm), ("h2t", 3, h_fm, g_fm)):
            for og in range(16):
                pp = og % 2
                toks = []
                for j in range(2):
                    b = j + 2 * pp
                    oc = og * 2 + j
                    wls = slab_guard(("f", b), (b, 0), (b, 1))
                    toks.append(ctx.dma(
                        "sync" if j == 0 else "scalar",
                        slab[b][:].rearrange("p (cc o) -> p cc o", o=128),
                        P[wnm][oc].rearrange("(cc p) o -> p cc o", p=128),
                        wait=wls or None, lane=pp))
                for j in range(2):
                    b = j + 2 * pp
                    oc = og * 2 + j
                    ps, wlp = pool.acquire()
                    sem = LSEM[pp] if j == 0 else LSEM[2 + pp]
                    for cc in range(32):
                        w0 = (wlp + [(sem, toks[j]), (DV, tgf)]) if cc == 0 else None
                        tmm = ctx.mm(ps["ap"][:], slab[b][:, cc * 128:(cc + 1) * 128],
                                     src[:, cc * 512:(cc + 1) * 512],
                                     cc == 0, cc == 31, wait=w0)
                    slab_rd[("f", b)] = (PE, tmm)
                    tg = ctx.act(dst[:, oc * 512:(oc + 1) * 512], ps["ap"][:],
                                 AF.Gelu, scale=scol[:, sidx * 32 + oc:sidx * 32 + oc + 1],
                                 wait=[(PE, tmm)])
                    tgf2 = ctx.fence_ac()
                    ps["r"] = (AC, tgf2)
            tgf = ctx.fence_ac()
            if P.get("dbg_ffn") is not None and blk == 0:
                ctx.dma("gpsimd", P["dbg_ffn"]["h" if wnm == "h1t" else "h2"],
                        dst[:], wait=[(AC, tgf)])
                ctx.barrier()
        for og in range(4):
            pp = og % 2
            toks = []
            for j in range(2):
                b = j + 2 * pp
                oc = og * 2 + j
                wls = slab_guard(("f", b), (b, 0), (b, 1))
                toks.append(ctx.dma(
                    "sync" if j == 0 else "scalar",
                    slab[b][:].rearrange("p (cc o) -> p cc o", o=128),
                    P["w3t"][oc].rearrange("(cc p) o -> p cc o", p=128),
                    wait=wls or None, lane=pp))
            for j in range(2):
                b = j + 2 * pp
                oc = og * 2 + j
                ps, wlp = pool.acquire()
                sem = LSEM[pp] if j == 0 else LSEM[2 + pp]
                for cc in range(32):
                    w0 = (wlp + [(sem, toks[j]), (AC, tgf)]) if cc == 0 else None
                    tmm = ctx.mm(ps["ap"][:], slab[b][:, cc * 128:(cc + 1) * 128],
                                 g_fm[:, cc * 512:(cc + 1) * 512],
                                 cc == 0, cc == 31, wait=w0)
                slab_rd[("f", b)] = (PE, tmm)
                tw3 = ctx.act(ev1[:], ps["ap"][:], AF.Copy,
                              scale=scol[:, 4 * 32 + oc:4 * 32 + oc + 1],
                              wait=[(PE, tmm)])
                tw3f = ctx.fence_ac()
                ps["r"] = (AC, tw3f)
                for j2 in range(4):
                    pt, wlt = pool.acquire()
                    tt = ctx.trans(pt["ap"][0:128, 0:128],
                                   ev1[:, j2 * 128:(j2 + 1) * 128],
                                   c.identF[:], wait=wlt + [(AC, tw3f)])
                    tev = ctx.act(outst[:, j2 * 1024 + oc * 128: j2 * 1024 + (oc + 1) * 128],
                                  pt["ap"][0:128, 0:128], AF.Copy, wait=[(PE, tt)])
                    pt["r"] = (AC, tev)
        tof = ctx.fence_ac()
        if P.get("dbg_ffn") is not None and blk == 0:
            ctx.dma("gpsimd", P["dbg_ffn"]["o"], outst[:], wait=[(AC, tof)])
            ctx.barrier()
        tlast_out = None
        tadf = None
        for j2 in range(4):
            trl = ctx.dma("sync", rst[:], rdram[(tb0 + j2 * 128):(tb0 + (j2 + 1) * 128), :],
                          wait=[(DV, tadf)] if tadf is not None else None)
            tadd = ctx.dve("tensor_tensor", outst[:, j2 * 1024:(j2 + 1) * 1024],
                           outst[:, j2 * 1024:(j2 + 1) * 1024], rst[:], ALU.add,
                           wait=[(LD, trl), (AC, tof)])
            tadf = ctx.fence_dv()
            tlast_out = ctx.dma("gpsimd", out[(tb0 + j2 * 128):(tb0 + (j2 + 1) * 128), :],
                                outst[:, j2 * 1024:(j2 + 1) * 1024], wait=[(DV, tadf)])
        ctx.emit("tensor", "wait_ge", GD.h, tlast_out)
        ln4_tok = (PE, ctx.sems["PE"].val)
        tgf = ctx.fence_dv()
    ctx.barrier()


def emit_ln4(ctx, c, P):
    LD, GD, PE, DV, AC = (ctx.sems[k] for k in ("LD", "GD", "PE", "DV", "AC"))
    r, pool = P["r"], P["pool"]
    with ctx.scope():
        fmt = ctx.sbuf("l4_fmt", [128, DM], BF16)
        tmp = {"sq": ctx.sbuf("l4_sq", [128, DM], F32),
               "st": ctx.sbuf("l4_st", [128, 16], F32),
               "lnf": ctx.sbuf("l4_f", [128, DM], F32),
               "psT": [pool.t[2], pool.t[3]],
               "psT_tok": [None, None], "act_prev": None, "lnf_rd": None}
        fm_tok = None
        for oi in range(16):
            fmw = [fmt[:, cc * 128:(cc + 1) * 128] for cc in range(NCH)]
            if fm_tok is not None:
                tmp["lnf_rd"] = tmp["lnf_rd"]
            te, ntok, tv = emit_ln_tile2(ctx, c, r[:, oi * DM:(oi + 1) * DM],
                                         fmw, tmp,
                                         wait=[fm_tok] if fm_tok else None)
            tfa = ctx.fence_ac()
            for cc in range(NCH):
                tW = ctx.dma("gpsimd",
                             P["ln4_dram"][cc, :, oi * 128:(oi + 1) * 128],
                             fmt[:, cc * 128:(cc + 1) * 128],
                             wait=[(AC, tfa)] if cc == 0 else None)
            fm_tok = (GD, tW)
            ctx.dma("sync", P["rdram"][oi * 128:(oi + 1) * 128, :],
                    r[:, oi * DM:(oi + 1) * DM])
    ctx.barrier()


def emit_part2_body(ctx, c, P):
    emit_l1_prep(ctx, c, P)
    emit_l1_attn(ctx, c, P)


def build_part2(z, debug=()):
    nc = bass.Bass("TRN2", target_bir_lowering=False)
    P = declare_l1_inputs(nc)
    P["z"] = z
    P["x"] = nc.dram_tensor("x2", [T, DM], F32, kind="ExternalInput").ap()
    P["out"] = nc.dram_tensor("outp", [16 * 128, DM], F32, kind="ExternalOutput").ap()
    P["rdram"] = nc.dram_tensor("rdram", [2048, DM], F32, kind="Internal").ap()
    P["w1t"] = nc.dram_tensor("w1t", [32, DM, 128], BF16, kind="Internal").ap()
    P["w2t"] = nc.dram_tensor("w2t", [32, DM, 128], BF16, kind="Internal").ap()
    P["h1t"] = nc.dram_tensor("h1t", [32, 4096, 128], BF16, kind="Internal").ap()
    P["h2t"] = nc.dram_tensor("h2t", [32, 4096, 128], BF16, kind="Internal").ap()
    P["w3t"] = nc.dram_tensor("w3t", [8, 4096, 128], BF16, kind="Internal").ap()
    with contextlib.ExitStack() as stack:
        ctx = Ctx(nc, stack)
        pool = PsumPool([ctx.psum("ps%d" % i, [128, 512], F32) for i in range(4)])
        P["pool"] = pool
        P["psO1"] = ctx.psum("psO1", [65, 1024], F32)
        P["psO2"] = ctx.psum("psO2", [128, 1024], F32)
        c = build_common(ctx)
        P["ln4_dram"] = nc.dram_tensor("ln4d", [NCH, 128, 2048], BF16,
                                       kind="Internal").ap()
        if "ffn" in debug:
            P["dbg_ffn"] = {
                "g": nc.dram_tensor("d_g", [128, 32 * 512], BF16, kind="ExternalOutput").ap(),
                "h": nc.dram_tensor("d_h", [128, 32 * 512], BF16, kind="ExternalOutput").ap(),
                "h2": nc.dram_tensor("d_h2", [128, 32 * 512], BF16, kind="ExternalOutput").ap(),
                "o": nc.dram_tensor("d_o", [128, 4096], F32, kind="ExternalOutput").ap(),
            }
        with ctx.scope():
            P["gg"] = ctx.sbuf("gg", [128, NTILE * 16], F32)
            P["ggfm"] = ctx.sbuf("ggfm", [16, T], BF16)
            P["r"] = ctx.sbuf("r", [128, 16 * DM], F32)
            P["lnfm"] = ctx.sbuf("lnfm", [128, NCH * FMW], BF16)
            ctx.warm = ([P["r"][:, i * DM:(i + 1) * DM] for i in range(3)],
                        P["x"][0:128, :])
            towns = own_tiles(z)
            with ctx.scope():
                xt = [ctx.sbuf("xt0", [128, DM], F32), ctx.sbuf("xt1", [128, DM], F32)]
                xt_tok = [None, None]
                tmp = {"sq": ctx.sbuf("ln_sq", [128, DM], F32),
                       "st": ctx.sbuf("ln_st", [128, 16], F32),
                       "lnf": ctx.sbuf("ln_f", [128, DM], F32),
                       "psT": [pool.t[2], pool.t[3]],
                       "psT_tok": [None, None], "act_prev": None, "lnf_rd": None}
                lnfm_r = P["lnfm"][:].rearrange("p (c w) -> p c w", w=FMW)
                ctx.dve("memset", lnfm_r[:, :, 0:1], 0.0)
                LD = ctx.sems["LD"]
                tdum = ctx.dma("sync", xt[0][:], P["x"][0:128, :])
                xt_tok[0] = [(LD, tdum)]
                for g in range(NTILE):
                    slot = g % 2
                    wl = xt_tok[slot] if xt_tok[slot] is not None else []
                    tl = ctx.dma("sync", xt[slot][:], P["x"][g * 128:(g + 1) * 128, :],
                                 wait=wl or None)
                    if g in towns:
                        oi = towns.index(g)
                        ctx.dma("sync", P["r"][:, oi * DM:(oi + 1) * DM],
                                P["x"][g * 128:(g + 1) * 128, :])
                    fmw = [P["lnfm"][:, fm_col(cc, g * 128): fm_col(cc, g * 128) + 128]
                           for cc in range(NCH)]
                    te, ntok, tv = emit_ln_tile2(ctx, c, xt[slot][:], fmw, tmp,
                                                 wait=[(LD, tl)])
                    xt_tok[slot] = [ntok]
            ctx.barrier()
            with ctx.scope():
                emit_l1_prep(ctx, c, P)
            with ctx.scope():
                emit_l1_attn(ctx, c, P)
            emit_ln4(ctx, c, P)
        with ctx.scope():
            emit_wprep(ctx, c, P)
        with ctx.scope():
            emit_l1_ffn(ctx, c, P)
        GD = ctx.sems["GD"]
        ctx.emit("gpsimd", "wait_ge", GD.h, GD.val)
        ctx.finalize()
    return nc


def _shard_inputs_p1(inp):
    base = {
        "w_aq": inp["L0_attn_w_aq"], "w_ak": inp["L0_attn_w_ak"],
        "w_av": inp["L0_attn_w_av"], "w_ao": inp["L0_attn_w_ao"],
        "ffn_kernel": inp["L0_ffn_kernel"], "ffn_bias": inp["L0_ffn_bias"],
        "ffn_a": inp["L0_ffn_a"], "ffn_b": inp["L0_ffn_b"],
    }
    maps = []
    for core in range(8):
        m = dict(base)
        m["x"] = np.ascontiguousarray(inp["L0_attn_x"][core // 2])
        maps.append(m)
    return maps


def _shard_inputs_p2(inp, x2):
    base = {
        "qkv_w": inp["L1_attn_qkv_w"], "qkv_b": inp["L1_attn_qkv_b"],
        "w1_write": inp["L1_attn_w1_write"], "w2_write": inp["L1_attn_w2_write"],
        "w1_read": inp["L1_attn_w1_read"], "w2_read": inp["L1_attn_w2_read"],
        "kv_geo_w": inp["L1_attn_kv_geo_w"], "out_w": inp["L1_attn_out_w"],
        "out_b": inp["L1_attn_out_b"],
        "kv_gate": np.ascontiguousarray(inp["L1_attn_kv_gate"]).reshape(1, 1),
        "mem_scale": np.ascontiguousarray(inp["L1_attn_mem_scale"]).reshape(1, -1),
        "decay_logits": np.ascontiguousarray(inp["L1_attn_decay_logits"]).reshape(1, -1),
        "f_w1": inp["L1_ffn_w1_w"], "f_w1s": inp["L1_ffn_w1_s"].reshape(1, -1),
        "f_w2": inp["L1_ffn_w2_w"], "f_w2s": inp["L1_ffn_w2_s"].reshape(1, -1),
        "f_h1": inp["L1_ffn_h1_w"], "f_h1s": inp["L1_ffn_h1_s"].reshape(1, -1),
        "f_h2": inp["L1_ffn_h2_w"], "f_h2s": inp["L1_ffn_h2_s"].reshape(1, -1),
        "f_w3": inp["L1_ffn_w3_w"], "f_w3s": inp["L1_ffn_w3_s"].reshape(1, -1),
    }
    maps = []
    for core in range(8):
        m = dict(base)
        m["x2"] = np.ascontiguousarray(x2[core // 2])
        maps.append(m)
    return maps


_NC_CACHE = {}


def kernel(**inputs):
    inp = {k: np.ascontiguousarray(np.asarray(v)) for k, v in inputs.items()}
    x2 = np.zeros((NB, T, DM), np.float32)
    outs = np.zeros((NB, T, DM), np.float32)
    p1maps = _shard_inputs_p1(inp)
    for z in (0, 1):
        if ("p1", z) not in _NC_CACHE:
            _NC_CACHE[("p1", z)] = build_part1(z, fused=True,
                                               stages=("s1", "s2", "s3"))
        res = run_bass_kernel_spmd(_NC_CACHE[("p1", z)], p1maps,
                                   core_ids=list(range(8)))
        towns = own_tiles(z)
        for b in range(NB):
            rr = res.results[2 * b + z]["out_r"]
            for i, g in enumerate(towns):
                x2[b, g * 128:(g + 1) * 128] = rr[i * 128:(i + 1) * 128]
    p2maps = _shard_inputs_p2(inp, x2)
    for z in (0, 1):
        if ("p2", z) not in _NC_CACHE:
            _NC_CACHE[("p2", z)] = build_part2(z)
        res = run_bass_kernel_spmd(_NC_CACHE[("p2", z)], p2maps,
                                   core_ids=list(range(8)))
        towns = own_tiles(z)
        for b in range(NB):
            rr = res.results[2 * b + z]["outp"]
            for i, g in enumerate(towns):
                outs[b, g * 128:(g + 1) * 128] = rr[i * 128:(i + 1) * 128]
    return outs


def build_fused(z):
    nc = bass.Bass("TRN2", target_bir_lowering=False)
    P = declare_l0_inputs(nc)
    P.update(declare_l1_inputs(nc))
    P["z"] = z
    P["out"] = nc.dram_tensor("outp", [16 * 128, DM], F32, kind="ExternalOutput").ap()
    P["xmid"] = nc.dram_tensor("xmid", [T, DM], F32, kind="Internal").ap()
    P["rdram"] = nc.dram_tensor("rdram", [2048, DM], F32, kind="Internal").ap()
    P["w1t"] = nc.dram_tensor("w1t", [32, DM, 128], BF16, kind="Internal").ap()
    P["w2t"] = nc.dram_tensor("w2t", [32, DM, 128], BF16, kind="Internal").ap()
    P["h1t"] = nc.dram_tensor("h1t", [32, 4096, 128], BF16, kind="Internal").ap()
    P["h2t"] = nc.dram_tensor("h2t", [32, 4096, 128], BF16, kind="Internal").ap()
    P["w3t"] = nc.dram_tensor("w3t", [8, 4096, 128], BF16, kind="Internal").ap()
    P["ln4_dram"] = nc.dram_tensor("ln4d", [NCH, 128, 2048], BF16,
                                   kind="Internal").ap()
    with contextlib.ExitStack() as stack:
        ctx = Ctx(nc, stack)
        pool = PsumPool([ctx.psum("ps%d" % i, [128, 512], F32) for i in range(4)])
        P["pool"] = pool
        P["psO1"] = ctx.psum("psO1", [65, 1024], F32)
        P["psO2"] = ctx.psum("psO2", [128, 1024], F32)
        c = build_common(ctx)
        with ctx.scope():
            P["r"] = ctx.sbuf("r", [128, 16 * DM], F32)
            P["lnfm"] = ctx.sbuf("lnfm", [128, NCH * FMW], BF16)
            ctx.warm = ([P["r"][:, i * DM:(i + 1) * DM] for i in range(3)],
                        P["x"][0:128, :])
            with ctx.scope():
                emit_part1(ctx, c, P, fused=True)
            with ctx.scope():
                emit_l0_ffn(ctx, c, P, fused=True)
            P["gg"] = ctx.sbuf("gg", [128, NTILE * 16], F32)
            P["ggfm"] = ctx.sbuf("ggfm", [16, T], BF16)
            with ctx.scope():
                emit_l1_prep(ctx, c, P)
            with ctx.scope():
                emit_l1_attn(ctx, c, P)
            emit_ln4(ctx, c, P)
        with ctx.scope():
            emit_wprep(ctx, c, P)
        with ctx.scope():
            emit_l1_ffn(ctx, c, P)
        GD = ctx.sems["GD"]
        ctx.emit("gpsimd", "wait_ge", GD.h, GD.val)
        ctx.finalize()
    return nc


def _shard_inputs_fused(inp):
    m1 = _shard_inputs_p1(inp)
    m2 = _shard_inputs_p2(inp, np.zeros((NB, 1, 1), np.float32))
    maps = []
    for core in range(8):
        m = dict(m1[core])
        m.update({k: v for k, v in m2[core].items() if k != "x2"})
        maps.append(m)
    return maps


def kernel(**inputs):
    inp = {k: np.ascontiguousarray(np.asarray(v)) for k, v in inputs.items()}
    outs = np.zeros((NB, T, DM), np.float32)
    maps = _shard_inputs_fused(inp)
    for z in (0, 1):
        if ("f", z) not in _NC_CACHE:
            _NC_CACHE[("f", z)] = build_fused(z)
        res = run_bass_kernel_spmd(_NC_CACHE[("f", z)], maps,
                                   core_ids=list(range(8)))
        towns = own_tiles(z)
        for b in range(NB):
            rr = res.results[2 * b + z]["outp"]
            for i, g in enumerate(towns):
                outs[b, g * 128:(g + 1) * 128] = rr[i * 128:(i + 1) * 128]
    return outs

def build_onelaunch():
    nc = bass.Bass("TRN2", target_bir_lowering=False)
    PB = declare_l0_inputs(nc)
    PB.update(declare_l1_inputs(nc))
    zfl = nc.dram_tensor("zflag", [1, 1], I32, kind="ExternalInput").ap()
    PB["out"] = nc.dram_tensor("outp", [16 * 128, DM], F32, kind="ExternalOutput").ap()
    PB["xmid"] = nc.dram_tensor("xmid", [T, DM], F32, kind="Internal").ap()
    PB["rdram"] = nc.dram_tensor("rdram", [2048, DM], F32, kind="Internal").ap()
    PB["w1t"] = nc.dram_tensor("w1t", [32, DM, 128], BF16, kind="Internal").ap()
    PB["w2t"] = nc.dram_tensor("w2t", [32, DM, 128], BF16, kind="Internal").ap()
    PB["h1t"] = nc.dram_tensor("h1t", [32, 4096, 128], BF16, kind="Internal").ap()
    PB["h2t"] = nc.dram_tensor("h2t", [32, 4096, 128], BF16, kind="Internal").ap()
    PB["w3t"] = nc.dram_tensor("w3t", [8, 4096, 128], BF16, kind="Internal").ap()
    PB["ln4_dram"] = nc.dram_tensor("ln4d", [NCH, 128, 2048], BF16,
                                    kind="Internal").ap()
    with contextlib.ExitStack() as stack:
        ctx = Ctx(nc, stack)
        zfs = ctx.sbuf("zfs", [1, 1], I32)
        qs = []
        warm = None
        for z in (0, 1):
            ctx.sfx = "_z%d" % z
            ctx.q = {e: [] for e in ENGINES}
            for s in ctx.sems.values():
                s.val = 0
            ctx.ps_w = {}
            ctx.ps_r = {}
            P = dict(PB)
            P["z"] = z
            with ctx.scope():
                pool = PsumPool([ctx.psum("ps%d" % i, [128, 512], F32)
                                 for i in range(4)])
                P["pool"] = pool
                P["psO1"] = ctx.psum("psO1", [65, 1024], F32)
                P["psO2"] = ctx.psum("psO2", [128, 1024], F32)
                c = build_common(ctx)
                with ctx.scope():
                    P["r"] = ctx.sbuf("r", [128, 16 * DM], F32)
                    P["lnfm"] = ctx.sbuf("lnfm", [128, NCH * FMW], BF16)
                    if z == 0:
                        warm = ([P["r"][:, i * DM:(i + 1) * DM] for i in range(3)],
                                P["x"][0:128, :])
                    ctx.stage = "l0attn"
                    with ctx.scope():
                        emit_part1(ctx, c, P, fused=True)
                    ctx.stage = "l0ffn"
                    with ctx.scope():
                        emit_l0_ffn(ctx, c, P, fused=True)
                    P["gg"] = ctx.sbuf("gg", [128, NTILE * 16], F32)
                    P["ggfm"] = ctx.sbuf("ggfm", [16, T], BF16)
                    ctx.stage = "l1prep"
                    with ctx.scope():
                        emit_l1_prep(ctx, c, P)
                    ctx.stage = "l1attn"
                    with ctx.scope():
                        emit_l1_attn(ctx, c, P)
                    ctx.stage = "ln4"
                    emit_ln4(ctx, c, P)
                ctx.stage = "wprep"
                with ctx.scope():
                    emit_wprep(ctx, c, P)
                ctx.stage = "l1ffn"
                with ctx.scope():
                    emit_l1_ffn(ctx, c, P)
                GD = ctx.sems["GD"]
                ctx.emit("gpsimd", "wait_ge", GD.h, GD.val)
            qs.append(ctx.q)
        ctx.warm = warm
        ctx.finalize_switch(qs, zfs, zfl)
    return nc


def kernel(**inputs):
    inp = {k: np.ascontiguousarray(np.asarray(v)) for k, v in inputs.items()}
    maps = _shard_inputs_fused(inp)
    for ci in range(8):
        maps[ci]["zflag"] = np.array([[ci % 2]], np.int32)
    if "one" not in _NC_CACHE:
        _NC_CACHE["one"] = build_onelaunch()
    res = run_bass_kernel_spmd(_NC_CACHE["one"], maps, core_ids=list(range(8)))
    outs = np.zeros((NB, T, DM), np.float32)
    for ci in range(8):
        b, z = ci // 2, ci % 2
        rr = res.results[ci]["outp"]
        for i, g in enumerate(own_tiles(z)):
            outs[b, g * 128:(g + 1) * 128] = rr[i * 128:(i + 1) * 128]
    return outs
